# version 05
# speedup vs baseline: 1.6431x; 1.0017x over previous
import math
import numpy as np
import ml_dtypes
import concourse.bass as bass
import concourse.mybir as mybir

"""Mini dependency scheduler for raw-Bass straight-line programs.

Engines: 'T' tensor, 'V' vector, 'S' scalar, 'G' gpsimd, 'Y' sync.
Each op declares its dependency tokens; finalize() assigns semaphore
increments only to ops somebody waits on, dedupes waits with per-engine
high-water marks, and emits the five engine streams inside one nc.Block().

Rules encoded:
  - cross-engine dep  -> wait_ge(sem[producer.eng], producer.inc_id)
  - same-engine dep   -> producer has inc? wait own sem : drain()
  - DMA ops go on named CHANNELS, one semaphore each (+16 per transfer).
    A wait for channel-DMA d uses value 16 * (number of channel DMAs created
    before the consumer). SOUNDNESS CONTRACT (caller's responsibility): any
    channel DMA created AFTER a consumer must be dep-gated on that consumer
    or a later op — use parity channels for ping-pong streams so waits are
    exact. The race detector in the simulator verifies this at runtime.
  - SWDGE (G) and HWDGE (Y/S) DMAs never share a channel (enforced by the
    one-engine-per-channel assert).
"""
import concourse.bass as bass

ENGINES = ("T", "V", "S", "G", "Y")
DMA_QUEUES = ("G", "Y", "S", "V")


class _Op:
    __slots__ = ("eng", "emit", "deps", "is_dma", "seq", "idx", "inc_id",
                 "ch", "ch_idx", "tag")

    def __init__(self, eng, emit, deps, is_dma, seq, idx, tag):
        self.eng = eng
        self.emit = emit
        self.deps = deps
        self.is_dma = is_dma
        self.seq = seq
        self.idx = idx
        self.inc_id = None
        self.ch = None
        self.ch_idx = None
        self.tag = tag


class Builder:
    def __init__(self, nc: bass.Bass):
        self.nc = nc
        self.ops = {e: [] for e in ENGINES}
        self._all = []
        self.ch_count = {}
        self.ch_eng = {}
        self.ch_seqs = {}

    def op(self, eng, emit, deps=(), tag=""):
        o = _Op(eng, emit, [d for d in deps if d is not None], False,
                len(self._all), len(self.ops[eng]), tag)
        self.ops[eng].append(o)
        self._all.append(o)
        return o

    def dma(self, eng, emit, deps=(), ch=None, look=0, seg=None, tag=""):
        assert eng in DMA_QUEUES
        ch = ch or ("q" + eng)
        if ch in self.ch_eng:
            assert self.ch_eng[ch] == eng, f"channel {ch} on two engines"
        else:
            self.ch_eng[ch] = eng
            self.ch_count[ch] = 0
            self.ch_seqs[ch] = []
            self.ch_look = getattr(self, "ch_look", {})
            self.ch_look[ch] = 0
            self.ch_segs = getattr(self, "ch_segs", {})
            self.ch_segs[ch] = []
        self.ch_look[ch] = max(self.ch_look[ch], look)
        self.ch_segs[ch].append(seg)
        o = _Op(eng, emit, [d for d in deps if d is not None], True,
                len(self._all), len(self.ops[eng]), tag)
        self.ch_count[ch] += 1
        o.ch = ch
        o.ch_idx = self.ch_count[ch]
        self.ch_seqs[ch].append(o.seq)
        self.ops[eng].append(o)
        self._all.append(o)
        return o

    def _wait_val(self, d, consumer):
        import bisect
        n = bisect.bisect_left(self.ch_seqs[d.ch], consumer.seq)
        assert n >= d.ch_idx, (d.tag, consumer.tag)
        look = self.ch_look.get(d.ch, 0)
        if look and n > 0:
            segs = self.ch_segs[d.ch]
            s = segs[n - 1]
            cap = n
            while cap < len(segs) and segs[cap] == s:
                cap += 1
            n = min(n + look, cap)
        else:
            n = min(n + look, self.ch_count[d.ch])
        return 16 * n

    def check_deadlock(self):
        done = set()
        ptr = {e: 0 for e in ENGINES}
        ch_done = {c: 0 for c in self.ch_count}
        progress = True
        while progress:
            progress = False
            for e in ENGINES:
                while ptr[e] < len(self.ops[e]):
                    o = self.ops[e][ptr[e]]
                    deps_ok = True
                    for d in o.deps:
                        if d.is_dma:
                            need = self._wait_val(d, o) // 16
                            if ch_done[d.ch] < need:
                                deps_ok = False
                        elif id(d) not in done:
                            deps_ok = False
                        if not deps_ok:
                            break
                    if not deps_ok:
                        break
                    done.add(id(o))
                    if o.is_dma:
                        ch_done[o.ch] = max(ch_done[o.ch], o.ch_idx)
                    ptr[e] += 1
                    progress = True
        stuck = {e: ptr[e] for e in ENGINES if ptr[e] < len(self.ops[e])}
        if stuck:
            msgs = []
            for e, i in stuck.items():
                o = self.ops[e][i]
                blocked = [f"{d.eng}:{d.tag}" for d in o.deps if id(d) not in done]
                msgs.append(f"engine {e} stuck at #{i} {o.tag!r} waiting {blocked}")
            raise RuntimeError("DEADLOCK in builder schedule:\n" + "\n".join(msgs))

    def finalize(self):
        self.check_deadlock()
        nc = self.nc

        need_inc = set()
        for o in self._all:
            for d in o.deps:
                if not d.is_dma:
                    need_inc.add(id(d))
        inc_counter = {e: 0 for e in ENGINES}
        for e in ENGINES:
            for o in self.ops[e]:
                if id(o) in need_inc:
                    inc_counter[e] += 1
                    o.inc_id = inc_counter[e]

        sems, ctxs = {}, []
        for e in ENGINES:
            if inc_counter[e] > 0:
                cm = nc.semaphore(f"sem_{e}")
                sems[e] = cm.__enter__()
                ctxs.append(cm)
        csems = {}
        for c in self.ch_count:
            cm = nc.semaphore(f"dma_{c}")
            csems[c] = cm.__enter__()
            ctxs.append(cm)

        blk_cm = nc.Block()
        block = blk_cm.__enter__()

        def run_engine(e, engine):
            hw_eng = {x: 0 for x in ENGINES}
            hw_ch = {c: 0 for c in self.ch_count}
            flushed = 0
            for o in self.ops[e]:
                for d in o.deps:
                    if d.is_dma:
                        tgt = self._wait_val(d, o)
                        if hw_ch[d.ch] < tgt:
                            engine.wait_ge(csems[d.ch], tgt)
                            hw_ch[d.ch] = tgt
                    elif d.eng == e:
                        if d.idx >= flushed:
                            if d.inc_id is not None:
                                if hw_eng[e] < d.inc_id:
                                    engine.wait_ge(sems[e], d.inc_id)
                                    hw_eng[e] = d.inc_id
                                flushed = max(flushed, d.idx + 1)
                            else:
                                engine.drain()
                                flushed = o.idx
                    else:
                        if d.inc_id is None:
                            raise RuntimeError(f"dep of {o.tag!r} on {d.tag!r} has no inc (bug)")
                        if hw_eng[d.eng] < d.inc_id:
                            engine.wait_ge(sems[d.eng], d.inc_id)
                            hw_eng[d.eng] = d.inc_id
                with self.nc.allow_low_precision("deliberate bf16 operand casts"):
                    ins = o.emit(engine)
                if o.is_dma:
                    ins.then_inc(csems[o.ch], 16)
                elif o.inc_id is not None:
                    ins.then_inc(sems[e], 1)

        if self.ops["G"]:
            @block.gpsimd
            def _(g):
                run_engine("G", g)
        if self.ops["T"]:
            @block.tensor
            def _(t):
                run_engine("T", t)
        if self.ops["S"]:
            @block.scalar
            def _(s):
                run_engine("S", s)
        if self.ops["V"]:
            @block.vector
            def _(v):
                run_engine("V", v)

        @block.sync
        def _(sy):
            run_engine("Y", sy)
            for c, n in self.ch_count.items():
                if n > 0:
                    sy.wait_ge(csems[c], 16 * n)

        blk_cm.__exit__(None, None, None)
        for cm in ctxs:
            cm.__exit__(None, None, None)
        return inc_counter, dict(self.ch_count)


"""Megakernel builder for mk_prb_d1 (GLA + 6-layer parallel-block transformer).

One raw-Bass program per core; core c owns batch element c (T=2048 tokens).
Feature-major residual stream y [1024, T] f32; bf16 matmul operands; weights
streamed from HBM with f32->bf16 cast DMAs (gpsimd).

Build: build_program(n_layers=0..6, do_gla=bool) -> (nc, meta)
"""
import math
import numpy as np
import ml_dtypes
import concourse.bass as bass
import concourse.mybir as mybir

F32, BF16 = mybir.dt.float32, mybir.dt.bfloat16
AF = mybir.ActivationFunctionType
ALU = mybir.AluOpType
AX = mybir.AxisListType
bf = ml_dtypes.bfloat16

T = 2048
D = 1024
NCH = 8
NT4 = 4
NTT = 16
NW = 8
HEADS = 16
DHEAD = 64
INNER = 1024
DI2 = 5460
DI = 2730
NA = 22
GH = 4
GDK = 128
GDV = 256
GC = 64
GNC = 32


def _round32(x):
    return (x + 31) & ~31


def host_tables():
    n = T
    half = np.arange(0, DHEAD, 2, dtype=np.float64)
    inv_freq = 1.0 / 10000.0 ** (half / DHEAD)
    t = np.arange(n, dtype=np.float64)
    freqs = t[:, None] * inv_freq[None, :]
    freqs = np.concatenate([freqs, freqs], -1)
    base_scale = (half + 0.4 * DHEAD) / (1.4 * DHEAD)
    power = (t - n // 2) / (256 // 2)
    scale = base_scale[None, :] ** power[:, None]
    scale = np.concatenate([scale, scale], -1)
    cos, sin = np.cos(freqs), np.sin(freqs)
    Cq = (cos * scale).T
    Sq = (sin * scale).T
    Ck = (cos / scale).T
    Sk = (sin / scale).T
    out = np.stack([np.concatenate([a, a], 0) for a in (Cq, Sq, Ck, Sk)])
    return out.astype(np.float32)


def host_consts():
    c = {}
    c["ident_f"] = np.eye(128, dtype=np.float32)
    c["ident_b"] = np.eye(128, dtype=bf)
    pr = np.zeros((128, 128), dtype=np.float32)
    for blk in (0, 64):
        for r in range(32):
            pr[blk + r + 32, blk + r] = -1.0
            pr[blk + r, blk + r + 32] = 1.0
    c["permrot"] = pr.astype(bf)
    j = np.arange(128)[:, None]
    i = np.arange(128)[None, :]
    c["mask_lo"] = (j >= i).astype(bf)
    c["mask_hi"] = (i >= j).astype(bf)
    c["ones_col"] = np.ones((128, 1), dtype=bf)
    c["ones_rows"] = np.ones((128, 128), dtype=bf)
    blkA = np.zeros((128, 2), dtype=bf)
    blkA[0:64, 0] = 1
    blkA[64:128, 1] = 1
    c["blkA"] = blkA
    blk2 = np.zeros((2, 128), dtype=bf)
    blk2[0, 0:64] = 1
    blk2[1, 64:128] = 1
    c["blk2"] = blk2
    m = np.ones((128, T), dtype=np.float32)
    m[:, ::GC] = 0.0
    c["scanmask"] = m
    c["tables"] = host_tables()
    return c


def pack_params(n_layers, do_gla, inp):
    cols = []
    for l in range(n_layers):
        cols.append(np.asarray(inp["L0_ffn_ln_g"][l]).reshape(8, 128).T)
        cols.append(np.asarray(inp["L0_ffn_ln_b"][l]).reshape(8, 128).T)
        cols.append(np.asarray(inp["L0_ffn_b2"][l]).reshape(8, 128).T)
        b1 = np.asarray(inp["L0_ffn_b1"][l])
        a = np.zeros((NA * 128,), np.float32); a[:DI] = b1[:DI]
        g = np.zeros((NA * 128,), np.float32); g[:DI] = b1[DI:]
        cols.append(a.reshape(NA, 128).T)
        cols.append(g.reshape(NA, 128).T)
        cols.append(np.tile(np.asarray(inp["L0_ffn_q_scale"][l]), 2)[:, None])
        cols.append(np.tile(np.asarray(inp["L0_ffn_k_scale"][l]), 2)[:, None])
    if do_gla:
        cols.append(np.asarray(inp["L0_attn_gnw"]).reshape(2, 128).T)
        cols.append(np.asarray(inp["L0_attn_bgk2"]).reshape(4, 128).T)
    if not cols:
        return None
    return np.ascontiguousarray(np.concatenate(cols, 1).astype(np.float32))


class MK:
    def __init__(self, nc, b, n_layers, do_gla):
        self.nc = nc
        self.b = b
        self.n_layers = n_layers
        self.do_gla = do_gla
        self.sb = {}
        self.ps = []
        self.dram = {}
        self.const = {}
        self.tok = {}

    def alloc(self):
        nc = self.nc
        arena = nc.alloc_sbuf_tensor("ARENA", [128, 211456], mybir.dt.uint8)
        base = nc.lookup_mloc(arena).addr
        self.base = base
        off = 0

        def at(name, shape, dt, o):
            h = nc.alloc_sbuf_tensor_at(name, shape, dt, offset=base + o)
            self.sb[name] = h
            return h

        self.y = [at(f"y{d}", [128, T], F32, off + d * T * 4) for d in range(NCH)]
        off += NCH * T * 4
        self.ybf_off = off
        stride = _round32((T + 1) * 2)
        self.ybf = [at(f"ybf{d}", [128, T + 1], BF16, off + d * stride) for d in range(NCH)]
        off += NCH * stride
        self.aqk_off = off
        self.aqk = [at(f"aqk{d}", [128, T], BF16, off + d * T * 2) for d in range(NCH)]
        off += NCH * T * 2
        self.av_off = off
        self.vaug = [at(f"vaug{t}", [128, 520], BF16, off + t * 1056) for t in range(NTT)]
        self.gv = [at(f"gv{d}", [128, T], BF16, off + d * T * 2) for d in range(NCH)]
        self.u = [at(f"u{i}", [128, 512], BF16, off + i * 1024) for i in range(NA)]
        self.xstg = [at(f"xstg{i}", [128, D], F32, off + i * 4096) for i in range(2)]
        self.sqstg = [at(f"sqstg{i}", [128, D], F32, off + 8192 + i * 4096) for i in range(2)]
        self.xnstg = [at(f"xnstg{i}", [128, D], F32, off + 16384 + i * 4096) for i in range(2)]
        self.ostg = [at(f"ostg{i}", [128, D], F32, off + 24576 + i * 4096) for i in range(2)]
        self.glstg = [at(f"glstg{i}", [128, T], F32, off + i * 8192) for i in range(2)]
        off += NTT * 2080
        self.u2 = [at(f"u2_{i}", [128, 512], BF16, self.ybf_off + i * 1024) for i in range(NA)]
        self.tab_off = off
        self.tabC = at("tabC", [128, T], BF16, off)
        self.tabS = at("tabS", [128, T], BF16, off + T * 2)
        self.scanmask = at("scanmask_sb", [128, T], F32, off)
        self.wo_big = at("wo_big", [128, 4096], BF16, off)
        self.wo_row = [self.wo_big[:, k * 1024:(k + 1) * 1024] for k in range(4)]
        self.wo_t = [[self.wo_big[:, (k * 8 + m) * 128:(k * 8 + m) * 128 + 128]
                      for m in range(8)] for k in range(4)]
        off += 2 * T * 2
        self.wst_off = off
        self.wstbig = at("wstbig", [128, 8192], BF16, off)
        self.wst = [[self.wstbig[:, (hb * 8 + k) * 512:(hb * 8 + k) * 512 + 512]
                     for k in range(8)] for hb in range(2)]
        self.t16 = at("t16", [16, T], BF16, off + 8192)
        self.w16 = at("w16", [128, 128], BF16, off + 8192 + 4096)
        self.wgk2 = at("wgk2", [16, 512], BF16, off + 8192 + 4096 + 256)
        off += 16 * 1024
        self.ao_off = off
        self.ow = [[at(f"ow{pp}_{j}", [128, 256], BF16, off + (pp * 4 + j) * 512)
                    for j in range(4)] for pp in range(2)]
        self.gG = at("gG", [128, T], F32, off)
        self.gost = [[at(f"gost{pp}_{j}", [128, 256], BF16, off + (pp * 8 + j) * 512)
                      for j in range(NCH)] for pp in range(2)]
        self.w2ring = at("w2ring", [128, 4096], BF16, off)
        off += 8192
        self.scr_off = off
        s = off
        self.qsq = [at(f"qsq{j}", [128, 512], BF16, s + j * 1024) for j in range(2)]
        self.sdq = at("sdq", [2, 512], F32, s + 2048)
        self.rqb = at("rqb", [2, 512], BF16, s + 4096)
        self.qn = [at(f"qn{j}", [128, 512], BF16, s + 5120 + j * 1024) for j in range(2)]
        self.t1 = at("t1", [128, 512], BF16, s + 7168)
        self.expS = [at(f"expS{i}", [128, 384], BF16, s + i * 768) for i in range(4)]
        self.lrow = [at("lrow0", [128, 256], BF16, s + 5120),
                     at("lrow1", [128, 256], BF16, s + 6656)]
        self.rbs = [at("rbs0", [64, 256], F32, s + 5632),
                    at("rbs1", [64, 256], F32, s + 7168)]
        self.sqb = at("sqb", [128, 512], BF16, self.ao_off + 4096)
        self.sqb2 = at("sqb2", [128, 512], BF16, self.ao_off + 5120)
        self.stt = at("sttbuf", [128, 512], F32, self.ao_off + 6144)
        self.mr = at("mr", [1, 1024], BF16, s + 3072)
        self.gg = at("gg", [128, 512], F32, s + 0)
        self.gg2 = at("gg2", [128, 512], F32, s + 2048)
        self.gsp = at("gsp", [128, T], F32, s + 0)
        self.e512a = at("e512a", [128, 512], F32, s + 0)
        self.e512b = at("e512b", [128, 512], F32, s + 2048)
        self.kgs = at("kgs", [128, 64], BF16, s + 0)
        self.kgt = at("kgt", [64, 128], BF16, s + 256)
        self.vt = at("vt", [64, 256], BF16, s + 512)
        self.at_sb = at("at_sb", [64, 64], BF16, s + 1024)
        self.Sst = [at(f"Sst{h}", [128, GDV], F32, s + 2048 + h * GDV * 4) for h in range(GH)]
        self.Sbf = [at(f"Sbf{h}", [128, GDV], BF16, s + 6144 + h * GDV * 2) for h in range(GH)]
        self.gsil = at("gsil", [128, 512], F32, s + 0)
        self.gnt = at("gnt", [128, 512], F32, s + 2048)
        self.outA = at("outA", [128, 512], F32, s + 4096)
        self.outB = at("outB", [128, 512], F32, s + 6144)
        self.osq = at("osq", [128, 512], BF16, self.tab_off + 4096)
        self.rqg = at("rqg", [1, 512], BF16, self.tab_off + 5120)
        off += 8192
        self.par_off = off
        p = off
        self.cpool = at("cpool", [128, 8], F32, p); p += 32
        self.stat = at("stat", [128, 8], F32, p); p += 32
        self.onesb = at("onesb", [128, 1], BF16, p); p += 32
        self.onesr = at("onesr", [128, 128], BF16, p); p += 256
        self.blkA = at("blkA_sb", [128, 2], BF16, p); p += 32
        self.blk2 = at("blk2_sb", [2, 128], BF16, p); p += 256
        self.permrot = at("permrot_sb", [128, 128], BF16, p); p += 256
        self.mask_lo = at("mask_lo_sb", [128, 128], BF16, p); p += 256
        self.mask_hi = at("mask_hi_sb", [128, 128], BF16, p); p += 256
        self.ident_f = at("ident_f_sb", [128, 128], F32, p); p += 512
        self.ident_b = at("ident_b_sb", [128, 128], BF16, p); p += 256
        self.glast = at("glast", [128, 128], F32, p); p += 512
        L = max(self.n_layers, 0)
        self.npar = 70 * L + (6 if self.do_gla else 0)
        if self.npar:
            self.par = at("par", [128, self.npar], F32, p)
            p += _round32(self.npar * 4)
            if self.do_gla:
                gb = 70 * L
                self.gnwv = self.par[:, gb:gb + 2]
                self.bgk2s = self.par[:, gb + 2:gb + 6]
                self.bgk2n = at("bgk2n", [128, 4], F32, p)
                p += 32
        self.par_end = p
        total = p
        assert total <= 211456, total
        self.total = total

        self.psum = [self.nc.alloc_psum_tensor(f"PS{i}", [128, 512], F32) for i in range(8)]

    def pc_lng(self, l):
        return self.par[:, 70 * l + 0:70 * l + 8]

    def pc_lnb(self, l):
        return self.par[:, 70 * l + 8:70 * l + 16]

    def pc_b2(self, l):
        return self.par[:, 70 * l + 16:70 * l + 24]

    def pc_b1a(self, l):
        return self.par[:, 70 * l + 24:70 * l + 46]

    def pc_b1g(self, l):
        return self.par[:, 70 * l + 46:70 * l + 68]

    def pc_qs(self, l):
        return self.par[:, 70 * l + 68:70 * l + 69]

    def pc_ks(self, l):
        return self.par[:, 70 * l + 69:70 * l + 70]

    def scr_bar(self):
        mark = self.state.get("scr_mark", 0)
        bar = [o for o in self.b._all[mark:] if not o.is_dma]
        bar += self.state.pop("scr_dma_extra", [])
        self.state["scr_mark"] = len(self.b._all)
        return bar

    def load_consts(self, c):
        nc, b = self.nc, self.b
        self.inl = {}
        names = ["ident_f", "ident_b", "permrot", "mask_lo", "mask_hi", "ones_col",
                 "ones_rows", "blkA", "blk2"] + (["scanmask"] if self.do_gla else [])
        for name in names:
            self.inl[name] = nc.inline_tensor(np.asarray(c[name]), name="c_" + name)
        self.inl["tables"] = nc.inline_tensor(c["tables"], name="c_tables")
        t = []
        t.append(b.dma("G", lambda g: g.dma_start(self.ident_f[:], self.inl["ident_f"][:, :]), ch="cst", tag="c_idf"))
        t.append(b.dma("G", lambda g: g.dma_start(self.ident_b[:], self.inl["ident_b"][:, :]), ch="cst", tag="c_idb"))
        t.append(b.dma("G", lambda g: g.dma_start(self.permrot[:], self.inl["permrot"][:, :]), ch="cst", tag="c_pr"))
        t.append(b.dma("G", lambda g: g.dma_start(self.mask_lo[:], self.inl["mask_lo"][:, :]), ch="cst", tag="c_ml"))
        t.append(b.dma("G", lambda g: g.dma_start(self.mask_hi[:], self.inl["mask_hi"][:, :]), ch="cst", tag="c_mh"))
        t.append(b.dma("G", lambda g: g.dma_start(self.onesb[:], self.inl["ones_col"][:, :]), ch="cst", tag="c_oc"))
        t.append(b.dma("G", lambda g: g.dma_start(self.onesr[:], self.inl["ones_rows"][:, :]), ch="cst", tag="c_or"))
        t.append(b.dma("G", lambda g: g.dma_start(self.blkA[:], self.inl["blkA"][:, :]), ch="cst", tag="c_ba"))
        t.append(b.dma("G", lambda g: g.dma_start(self.blk2[:], self.inl["blk2"][:, :]), ch="cst", tag="c_b2"))
        if self.do_gla:
            t.append(b.dma("G", lambda g: g.dma_start(self.scanmask[:], self.inl["scanmask"][:, :]), ch="cst", tag="c_sm"))
        self.tok["consts"] = t
        cp = self.cpool
        vals = [1.0, 1e-5, 1e-24, math.log(GDK ** -0.5), 1e-5, 0.0, 0.0, 0.0]
        cps = []
        for i, v in enumerate(vals):
            cps.append(b.op("V", (lambda vv, i=i, v=v: vv.memset(cp[:, i:i + 1], v)), tag=f"cp{i}"))
        self.tok["cpool"] = cps
        if self.npar:
            tp = b.dma("G", lambda g: g.dma_start(self.par[:], self.dram["params"][:, :]),
                       ch="cst", tag="c_par")
            self.tok["par"] = [tp]
            if self.do_gla:
                self.tok["par"].append(
                    b.op("V", (lambda v: v.tensor_scalar(self.bgk2n[:], self.bgk2s, -1.0, None, ALU.mult)),
                         deps=[tp], tag="negb"))

    def swap_tables(self, which, deps):
        b = self.b
        t0 = b.dma("G", (lambda g, w=which: g.dma_start(self.tabC[:], self.inl["tables"][2 * w, :, :])),
                   deps=deps, ch="tab", tag=f"tabC{which}")
        t1 = b.dma("G", (lambda g, w=which: g.dma_start(self.tabS[:], self.inl["tables"][2 * w + 1, :, :])),
                   deps=deps, ch="tab", tag=f"tabS{which}")
        return [t0, t1]

    def emit_p0(self):
        import os
        b = self.b
        x = self.dram["x"]
        lvl = int(os.environ.get("MK_P0_LVL", "9"))
        ev_last = {}
        xn_done, sq_done, s2_done = {}, {}, {}
        self.tok["p0_ybf"] = []
        for tt in range(NTT):
            st = tt % 2
            so = 8 * st
            so = 4 * st
            dep = []
            if tt - 2 in xn_done:
                dep = [xn_done[tt - 2], sq_done[tt - 2]]
            ld = b.dma("Y", (lambda sy, tt=tt, st=st:
                             sy.dma_start(self.xstg[st][:], x[128 * tt:128 * (tt + 1), :])),
                       deps=dep, ch=f"xi{tt % 2}", tag=f"p0x{tt}")
            if lvl < 2:
                xn_done[tt] = b.op("V", (lambda v, st=st, tt=tt: v.tensor_copy(
                    self.y[tt % NCH][:, 0:D] if False else self.xnstg[st][:], self.xstg[st][:])),
                    deps=[ld] + ([xn_done[tt - 2]] if tt - 2 in xn_done else []), tag=f"p0cp{tt}")
                sq_done[tt] = xn_done[tt]
                s2_done[tt] = xn_done[tt]
                if lvl < 1:
                    self.tok["p0_ybf"].append(xn_done[tt])
                    continue
                t_xn = xn_done[tt]
            if lvl >= 2:
                t_sq = b.op("S", (lambda s, st=st: s.activation(self.sqstg[st][:], self.xstg[st][:], AF.Square)),
                            deps=[ld] + ([s2_done[tt - 2]] if tt - 2 in s2_done else []), tag=f"p0sq{tt}")
                sq_done[tt] = t_sq
                prev = [xn_done[tt - 2]] if tt - 2 in xn_done else []
                t_mu = b.op("V", (lambda v, st=st, so=so: v.reduce_sum(self.stat[:, so:so + 1],
                                                                       self.xstg[st][:], AX.X)),
                            deps=[ld] + prev, tag=f"p0mu{tt}")
                t_s2 = b.op("V", (lambda v, st=st, so=so: v.reduce_sum(self.stat[:, so + 1:so + 2],
                                                                       self.sqstg[st][:], AX.X)),
                            deps=[t_sq] + prev, tag=f"p0s2{tt}")
                s2_done[tt] = t_s2
                t_st = b.op("V", (lambda v, so=so: v.tensor_scalar(self.stat[:, so:so + 1],
                                                                   self.stat[:, so:so + 1], 1.0 / D, None, ALU.mult)),
                            deps=[t_mu], tag=f"p0mun{tt}")
                t_v2 = b.op("V", (lambda v, so=so: v.tensor_scalar(self.stat[:, so + 1:so + 2],
                                                                   self.stat[:, so + 1:so + 2], 1.0 / D, None, ALU.mult)),
                            deps=[t_s2], tag=f"p0e2{tt}")
                t_v1 = b.op("V", (lambda v, so=so: v.tensor_mul(self.stat[:, so + 2:so + 3],
                                                                self.stat[:, so:so + 1], self.stat[:, so:so + 1])),
                            deps=[t_st], tag=f"p0msq{tt}")
                t_var = b.op("V", (lambda v, so=so: v.tensor_sub(self.stat[:, so + 1:so + 2],
                                                                 self.stat[:, so + 1:so + 2],
                                                                 self.stat[:, so + 2:so + 3])),
                             deps=[t_v1, t_v2], tag=f"p0var{tt}")
                t_sd = b.op("S", (lambda s, so=so: s.activation(self.stat[:, so + 2:so + 3],
                                                                self.stat[:, so + 1:so + 2], AF.Sqrt,
                                                                bias=self.cpool[:, 1:2])),
                            deps=[t_var] + self.tok["cpool"], tag=f"p0sd{tt}")
                t_rs = b.op("V", (lambda v, so=so: v.reciprocal(self.stat[:, so + 3:so + 4],
                                                                self.stat[:, so + 2:so + 3])),
                            deps=[t_sd], tag=f"p0rs{tt}")
                t_xn = b.op("V", (lambda v, st=st, so=so: v.tensor_scalar(self.xnstg[st][:], self.xstg[st][:],
                                                                          self.stat[:, so:so + 1],
                                                                          self.stat[:, so + 3:so + 4],
                                                                          ALU.subtract, ALU.mult)),
                            deps=[t_rs] + ([xn_done[tt - 2]] if tt - 2 in xn_done else []), tag=f"p0xn{tt}")
                xn_done[tt] = t_xn
            if lvl < 3:
                self.tok["p0_ybf"].append(xn_done[tt])
                continue
            bk = (tt % 2) * 2
            for half in range(2):
                bank = bk + half
                def tr(t, st=st, half=half, bank=bank):
                    r = None
                    for q in range(4):
                        d = half * 4 + q
                        r = t.transpose(self.psum[bank][0:128, 128 * q:128 * (q + 1)],
                                        self.xnstg[st][:, 128 * d:128 * (d + 1)],
                                        self.ident_f[:, :])
                    return r
                deps = [t_xn, self.tok["consts"][0]]
                key = (tt - 2, half)
                if key in ev_last:
                    deps += ev_last[key]
                t_tr = b.op("T", tr, deps=deps, tag=f"p0tr{tt}_{half}")
                evs = []
                eng = "V" if half == 0 else "S"
                for q in range(4):
                    d = half * 4 + q
                    if eng == "V":
                        e1 = b.op("V", (lambda v, d=d, bank=bank, q=q, tt=tt:
                                        v.tensor_copy(self.y[d][:, 128 * tt:128 * (tt + 1)],
                                                      self.psum[bank][0:128, 128 * q:128 * (q + 1)])),
                                  deps=[t_tr], tag=f"p0ey{tt}_{d}")
                        e2 = b.op("V", (lambda v, d=d, bank=bank, q=q, tt=tt:
                                        v.tensor_copy(self.ybf[d][:, 1 + 128 * tt:1 + 128 * (tt + 1)],
                                                      self.psum[bank][0:128, 128 * q:128 * (q + 1)])),
                                  deps=[t_tr], tag=f"p0eb{tt}_{d}")
                    else:
                        e1 = b.op("S", (lambda s, d=d, bank=bank, q=q, tt=tt:
                                        s.copy(self.y[d][:, 128 * tt:128 * (tt + 1)],
                                               self.psum[bank][0:128, 128 * q:128 * (q + 1)])),
                                  deps=[t_tr], tag=f"p0ey{tt}_{d}")
                        e2 = b.op("S", (lambda s, d=d, bank=bank, q=q, tt=tt:
                                        s.copy(self.ybf[d][:, 1 + 128 * tt:1 + 128 * (tt + 1)],
                                               self.psum[bank][0:128, 128 * q:128 * (q + 1)])),
                                  deps=[t_tr], tag=f"p0eb{tt}_{d}")
                    evs += [e1, e2]
                ev_last[(tt, half)] = evs
                self.tok["p0_ybf"] += evs
        for d in range(NCH):
            self.tok["p0_ybf"].append(
                b.op("V", (lambda v, d=d: v.memset(self.ybf[d][:, 0:1], 0.0)), tag=f"padz{d}"))

    def emit_p3(self, y_ready, gla_ready):
        b = self.b
        x = self.dram["x"]
        out = self.dram["out"]
        add_tok = []
        if self.do_gla:
            gld = {}
            for d in range(NCH):
                dep = [add_tok[d - 2]] if d - 2 >= 0 else []
                ld = b.dma("Y", (lambda sy, d=d: sy.dma_start(self.glstg[d % 2][:],
                                                              self.dram["gla_out"][128 * d:128 * (d + 1), :])),
                           deps=gla_ready + dep + (y_ready if d < 2 else []),
                           ch=f"gi{d % 2}", tag=f"p3g{d}")
                a = b.op("V", (lambda v, d=d: v.tensor_add(self.y[d][:], self.y[d][:], self.glstg[d % 2][:])),
                         deps=[ld] + y_ready, tag=f"p3a{d}")
                add_tok.append(a)
        else:
            add_tok = y_ready
        ev_prev = {}
        st_tok = {}
        ld2 = {}
        for tt in range(NTT):
            st = tt % 2
            dep = list(ev_prev.get((tt - 2, 0), [])) + list(ev_prev.get((tt - 2, 1), []))
            ld2[tt] = b.dma("Y", (lambda sy, tt=tt, st=st:
                                  sy.dma_start(self.xstg[st][:], x[128 * tt:128 * (tt + 1), :])),
                            deps=dep + (list(add_tok) if tt < 2 else []), ch=f"xi{tt % 2}", tag=f"p3x{tt}")
            bk = (tt % 2) * 2
            evs_all = []
            for half in range(2):
                bank = bk + half
                def tr(t, tt=tt, half=half, bank=bank):
                    r = None
                    for q in range(4):
                        d = half * 4 + q
                        r = t.transpose(self.psum[bank][0:128, 128 * q:128 * (q + 1)],
                                        self.y[d][:, 128 * tt:128 * (tt + 1)],
                                        self.ident_f[:, :])
                    return r
                deps = list(add_tok) + [self.tok["consts"][0]]
                if (tt - 2, half) in ev_prev:
                    deps += ev_prev[(tt - 2, half)]
                t_tr = b.op("T", tr, deps=deps, tag=f"p3tr{tt}_{half}")
                evs = []
                for q in range(4):
                    d = half * 4 + q
                    edep = [t_tr, ld2[tt]]
                    if tt - 2 in st_tok:
                        edep.append(st_tok[tt - 2])
                    e = b.op("V", (lambda v, d=d, bank=bank, q=q, st=st:
                                   v.tensor_add(self.ostg[st][:, 128 * d:128 * (d + 1)],
                                                self.psum[bank][0:128, 128 * q:128 * (q + 1)],
                                                self.xstg[st][:, 128 * d:128 * (d + 1)])),
                             deps=edep, tag=f"p3e{tt}_{d}")
                    evs.append(e)
                ev_prev[(tt, half)] = evs
                evs_all += evs
            st_tok[tt] = b.dma("Y", (lambda sy, tt=tt, st=st:
                                     sy.dma_start(out[128 * tt:128 * (tt + 1), :], self.ostg[st][:])),
                               deps=evs_all, ch=f"ou{tt % 2}", tag=f"p3o{tt}")


def build_program(n_layers=0, do_gla=False):
    nc = bass.Bass("TRN2", target_bir_lowering=False)
    b = Builder(nc)
    mk = MK(nc, b, n_layers, do_gla)
    mk.dram["x"] = nc.dram_tensor("x", [T, D], F32, kind="ExternalInput")
    mk.dram["out"] = nc.dram_tensor("out", [T, D], F32, kind="ExternalOutput")
    if do_gla:
        mk.dram["gla_o"] = nc.dram_tensor("gla_o", [D, T], BF16, kind="Internal")
        mk.dram["gla_out"] = nc.dram_tensor("gla_out", [D, T], F32, kind="Internal")
        for nm, shp in [("gWq", [D, 512]), ("gWk", [D, 512]), ("gWv", [D, D]), ("gWg", [D, D]),
                        ("gWgk1", [D, 16]), ("gWgk2", [16, 512]), ("gWo", [D, D])]:
            mk.dram[nm] = nc.dram_tensor(nm, shp, F32, kind="ExternalInput")
    if n_layers > 0:
        for nm, shp in [("Wqkv", [n_layers, D, 3072]), ("Wo", [n_layers, D, D]),
                        ("W1", [n_layers, D, DI2]), ("W2", [n_layers, DI, D])]:
            mk.dram[nm] = nc.dram_tensor(nm, shp, F32, kind="ExternalInput")
    mk.alloc()
    if mk.npar:
        mk.dram["params"] = nc.dram_tensor("params", [128, mk.npar], F32, kind="ExternalInput")
    mk.load_consts(host_consts())
    mk.state = {
        "ps_free": {}, "tab_readers": [], "ybf_readers": [], "u2_writers": [],
        "aqk_free": [], "vaug_free": [], "wst_free": {0: [], 1: []},
        "wq_bi": 0, "attn_step": 0, "w1_slot": 0, "w2_slot": 0,
        "w1_free": {}, "w2_free": {}, "u_free": {},
        "expS_free": {0: [], 1: [], 2: [], 3: []},
        "ao_free": {0: [], 1: []},
        "scr_qsq": {0: [], 1: []}, "scr_sdq": [], "scr_rqb": [],
        "scr_qn": {0: [], 1: []}, "scr_t1": [],
        "scr_lrow": {0: [], 1: []}, "scr_rbs": {0: [], 1: []},
        "scr_sqb": {0: [], 1: []}, "scr_stt": [], "scr_mr": [], "scr_gg": {0: [], 1: []},
        "scr_gsp": [], "scr_gG": [], "scr_glast": [], "scr_e": {0: [], 1: []},
        "scr_sst": [], "scr_kgs": [], "scr_kgt": [], "scr_vt": [], "scr_at": [],
        "scr_mark": 0,
    }
    mk.emit_p0()
    y_ready = list(mk.tok["p0_ybf"])
    mk.state["ybf_ready"] = y_ready
    import os
    if os.environ.get("MK_P0_ONLY"):
        out2 = nc.dram_tensor("out_fm", [D, T], F32, kind="ExternalOutput")
        for d in range(NCH):
            b.dma("Y", (lambda sy, d=d: sy.dma_start(out2[128 * d:128 * (d + 1), :], mk.y[d][:])),
                  deps=y_ready, ch="ou0", tag=f"dmpy{d}")
        b.finalize()
        return nc, mk
    gla_ready = []
    if do_gla:
        gla_ready = emit_gla(mk)
        mk.state["ao_free"] = {0: list(gla_ready), 1: list(gla_ready)}
        mk.state["ao_hi_init"] = list(gla_ready)
    for l in range(n_layers):
        y_ready = emit_layer(mk, l, y_ready)
    mk.emit_p3(y_ready, gla_ready)
    b.finalize()
    return nc, mk


"""Inner-transformer layer emitter (attn sublayer + geglu ffn sublayer).

PSUM discipline: mk.state["ps_free"][bank] holds the tokens of the LAST
readers of that bank; any op group writing the bank deps on them and the
group's evac tokens replace them.  One evac ENGINE per bank at a time.
"""
import numpy as np

QK_SCALE = 8.0


def psf(mk, bank):
    return list(mk.state["ps_free"].get(bank, []))


def pset(mk, bank, toks):
    mk.state["ps_free"][bank] = list(toks)


def emit_layer(mk, l, y_ready):
    b = mk.b
    st = mk.state

    if l > 0:
        _ybf_refresh(mk, l, "a", list(y_ready))
    y_writers = []
    ywin = {}
    emit_attn_hg(mk, l, 0, y_writers, ywin)
    fh = FfnHook(mk, l, ywin)
    emit_attn_hg(mk, l, 1, y_writers, ywin, hook=fh.hook)
    emit_ffn_w(mk, l, fh, y_writers)
    return y_writers


def _ybf_refresh(mk, l, ph, ydep):
    b = mk.b
    st = mk.state
    toks = []
    for d in range(NCH):
        dep = ydep + st["ybf_readers"] + st["u2_writers"]
        if d < 4:
            t = b.op("V", (lambda v, d=d: v.tensor_copy(mk.ybf[d][:, 1:T + 1], mk.y[d][:])),
                     deps=dep, tag=f"L{l}{ph}cast{d}")
            tz = b.op("V", (lambda v, d=d: v.memset(mk.ybf[d][:, 0:1], 0.0)),
                      deps=dep, tag=f"L{l}{ph}cz{d}")
        else:
            t = b.op("S", (lambda s, d=d: s.copy(mk.ybf[d][:, 1:T + 1], mk.y[d][:])),
                     deps=dep, tag=f"L{l}{ph}cast{d}")
            tz = b.op("V", (lambda v, d=d: v.memset(mk.ybf[d][:, 0:1], 0.0)),
                      deps=dep, tag=f"L{l}{ph}cz{d}")
        toks += [t, tz]
    st["ybf_readers"] = []
    st["u2_writers"] = []
    st["ybf_ready"] = toks


def _shift_ap(mk, k, c0, n):
    off = 1 + c0 if k < 4 else c0
    return mk.ybf[k][:, off:off + n]


def emit_attn_hg(mk, l, hg, y_writers, ywin=None, hook=None):
    b = mk.b
    st = mk.state
    if ywin is None:
        ywin = {}
    Wqkv = mk.dram["Wqkv"]
    Wo = mk.dram["Wo"]

    tabq = mk.swap_tables(0, st["tab_readers"])
    st["tab_readers"] = []

    secs = [hg * 512, 1024 + hg * 512, 2048 + hg * 512]
    ybf_dep = st["ybf_ready"]

    def fill_block(si, col0):
        bi = st["wq_bi"]
        pp = bi % 2
        dep = st["wst_free"][pp]
        dst = mk.wstbig[:, pp * 4096:pp * 4096 + 4096].rearrange("p (k c) -> p k c", c=512)
        srcm = Wqkv[l, 0:1024, col0:col0 + 512].rearrange("(k p) c -> p k c", p=128)
        last = b.dma("G", (lambda g, dst=dst, srcm=srcm: g.dma_start(dst, srcm)),
                     deps=dep, ch=f"wq{pp}", tag=f"L{l}h{hg}wf{si}")
        st["wq_bi"] += 1
        return pp, last

    pp0, f0 = fill_block(0, secs[0])
    pp1, f1 = fill_block(1, secs[1])
    ppv = fnv = None

    qk_evac = {}
    for si in range(2):
        pp = (pp0, pp1)[si]
        ftok = (f0, f1)[si]
        for m in range(4):
            banks = [(m % 2) * 4 + n for n in range(4)]

            def mm(t, pp=pp, m=m, banks=banks):
                r = None
                for k in range(8):
                    t.ldweights(mk.wst[pp][k][:, 128 * m:128 * (m + 1)])
                    for n in range(4):
                        r = t.matmul(mk.psum[banks[n]][0:128, 0:512],
                                     mk.wst[pp][k][:, 128 * m:128 * (m + 1)],
                                     _shift_ap(mk, k, 512 * n, 512),
                                     start=(k == 0), stop=(k == 7))
                return r

            deps = [ftok] + ybf_dep + st["aqk_free"]
            for bk in banks:
                deps += psf(mk, bk)
            tk = b.op("T", mm, deps=deps, tag=f"L{l}h{hg}qk{si}_{m}")
            st["ybf_readers"].append(tk)
            st["wst_free"][pp] = [tk]
            dst = mk.aqk[4 * si + m]
            for n in range(4):
                e = b.op("S", (lambda s, dst=dst, bk=banks[n], n=n:
                               s.copy(dst[:, 512 * n:512 * (n + 1)], mk.psum[bk][0:128, 0:512])),
                         deps=[tk], tag=f"L{l}h{hg}qke{si}_{m}_{n}")
                pset(mk, banks[n], [e])
                qk_evac[(si, m, n)] = e
        if si == 0:
            ppv, fnv = fill_block(2, secs[2])
    st["aqk_free"] = []

    ones_tok = []
    for tt2 in range(NTT):
        ones_tok.append(b.op("V", (lambda v, tt2=tt2: v.memset(
            mk.vaug[tt2][:, 0:520].rearrange("p (h d) -> p h d", d=65)[:, :, 64:65], 1.0)),
            deps=st["vaug_free"], tag=f"L{l}h{hg}vo{tt2}"))
    st["vaug_free"] = []
    v_evac = {}
    for tt2 in range(NTT):
        bank = tt2 % 4

        def mmv(t, ppv=ppv, tt2=tt2, bank=bank):
            r = None
            for k in range(8):
                t.ldweights(_shift_ap(mk, k, 128 * tt2, 128))
                r = t.matmul(mk.psum[bank][0:128, 0:512],
                             _shift_ap(mk, k, 128 * tt2, 128),
                             mk.wst[ppv][k][:, 0:512],
                             start=(k == 0), stop=(k == 7))
            return r

        tk = b.op("T", mmv, deps=[fnv] + ybf_dep + psf(mk, bank), tag=f"L{l}h{hg}v{tt2}")
        st["ybf_readers"].append(tk)
        st["wst_free"][ppv] = [tk]
        e = b.op("S", (lambda s, tt2=tt2, bank=bank:
                       s.copy(mk.vaug[tt2][:, 0:520].rearrange("p (h d) -> p h d", d=65)[:, :, 0:64],
                              mk.psum[bank][0:128, 0:512].rearrange("p (h d) -> p h d", d=64))),
                 deps=[tk, ones_tok[tt2]], tag=f"L{l}h{hg}ve{tt2}")
        pset(mk, bank, [e])
        v_evac[tt2] = e

    rot_done = {}

    def rotary(si, tab_toks):
        tab_readers = []
        scv = mk.pc_qs(l) if si == 0 else mk.pc_ks(l)
        for i in range(16):
            m, n = i % 4, i // 4
            src = mk.aqk[4 * si + m]
            sl = slice(512 * n, 512 * (n + 1))
            bY, bZ = i % 2, 2 + i % 2
            pr = i % 2
            qsq = mk.qsq[pr]
            qn = mk.qn[pr]
            t_sq = b.op("S", (lambda s, src=src, sl=sl, qsq=qsq:
                              s.activation(qsq[:, 0:512], src[:, sl], AF.Square)),
                        deps=[qk_evac[(si, m, n)]] + st["scr_qsq"][pr], tag=f"L{l}h{hg}r{si}sq{i}")

            def mm1(t, qsq=qsq):
                t.ldweights(mk.blkA[:, 0:2])
                return t.matmul(mk.psum[6][0:2, 0:512], mk.blkA[:, 0:2], qsq[:, 0:512],
                                start=True, stop=True)
            t_m1 = b.op("T", mm1, deps=[t_sq] + psf(mk, 6) + mk.tok["consts"], tag=f"L{l}h{hg}r{si}m1_{i}")
            t_sd = b.op("S", (lambda s: s.activation(mk.sdq[0:2, 0:512], mk.psum[6][0:2, 0:512], AF.Ln,
                                                     bias=mk.cpool[0:2, 2:3])),
                        deps=[t_m1] + st["scr_sdq"] + mk.tok["cpool"], tag=f"L{l}h{hg}r{si}sd{i}")
            pset(mk, 6, [t_sd])
            t_rq = b.op("S", (lambda s: s.activation(mk.rqb[0:2, 0:512], mk.sdq[0:2, 0:512],
                                                     AF.Exp, scale=-0.5)),
                        deps=[t_sd] + st["scr_rqb"], tag=f"L{l}h{hg}r{si}rq{i}")
            st["scr_sdq"] = [t_rq]

            def mm2(t, bY=bY):
                t.ldweights(mk.blk2[0:2, 0:128])
                return t.matmul(mk.psum[bY][0:128, 0:512], mk.blk2[0:2, 0:128], mk.rqb[0:2, 0:512],
                                start=True, stop=True)
            t_m2 = b.op("T", mm2, deps=[t_rq] + psf(mk, bY), tag=f"L{l}h{hg}r{si}m2_{i}")
            st["scr_rqb"] = [t_m2]
            t_qn = b.op("V", (lambda v, src=src, sl=sl, scv=scv, qn=qn, bY=bY:
                              v.scalar_tensor_tensor(qn[:, 0:512], src[:, sl], scv,
                                                     mk.psum[bY][0:128, 0:512],
                                                     ALU.mult, ALU.mult)),
                        deps=[t_m2] + mk.tok["par"][:1] + st["scr_qn"][pr], tag=f"L{l}h{hg}r{si}qn{i}")
            pset(mk, bY, [t_qn])

            def mm3(t, bZ=bZ, qn=qn):
                t.ldweights(mk.permrot[:, :])
                return t.matmul(mk.psum[bZ][0:128, 0:512], mk.permrot[:, :], qn[:, 0:512],
                                start=True, stop=True)
            t_m3 = b.op("T", mm3, deps=[t_qn] + psf(mk, bZ), tag=f"L{l}h{hg}r{si}m3_{i}")
            t_t1 = b.op("V", (lambda v, sl=sl, qn=qn: v.tensor_mul(mk.t1[:, 0:512], qn[:, 0:512], mk.tabC[:, sl])),
                        deps=[t_qn] + tab_toks + st["scr_t1"], tag=f"L{l}h{hg}r{si}t1_{i}")
            st["scr_qn"][pr] = [t_t1, t_m3]
            t_t2 = b.op("V", (lambda v, sl=sl, bZ=bZ, qsq=qsq:
                              v.tensor_mul(qsq[:, 0:512], mk.psum[bZ][0:128, 0:512],
                                           mk.tabS[:, sl])),
                        deps=[t_m3] + st["scr_qsq"][pr], tag=f"L{l}h{hg}r{si}t2_{i}")
            pset(mk, bZ, [t_t2])
            t_fin = b.op("V", (lambda v, src=src, sl=sl, qsq=qsq: v.tensor_add(src[:, sl], mk.t1[:, 0:512],
                                                                               qsq[:, 0:512])),
                         deps=[t_t2, t_t1], tag=f"L{l}h{hg}r{si}f{i}")
            st["scr_t1"] = [t_fin]
            st["scr_qsq"][pr] = [t_fin]
            tab_readers += [t_t1, t_t2]
            rot_done[(si, m, n)] = t_fin
            if i == 15:
                st["rot_guard"] = [t_fin, t_rq, t_m3]
        return tab_readers

    bar = mk.scr_bar()
    for key in ("scr_sdq", "scr_rqb", "scr_t1"):
        st[key] = list(bar)
    for key in ("scr_qsq", "scr_qn"):
        st[key] = {0: list(bar), 1: list(bar)}
    tabq_readers = rotary(0, tabq)
    tabk = mk.swap_tables(1, tabq_readers)
    tabk_readers = rotary(1, tabk)

    wo_tok = []
    for kk in range(4):
        dst = mk.tabC.tensor if False else None
        wo_tok.append(b.dma("G", (lambda g, kk=kk:
                                  g.dma_start(_wo_row(mk, kk),
                                              Wo[l, hg * 512 + 128 * kk:hg * 512 + 128 * (kk + 1), 0:1024])),
                            deps=tabk_readers if kk == 0 else [],
                            ch="wo", tag=f"L{l}h{hg}wo{kk}"))

    bar = mk.scr_bar()
    for i in range(4):
        st["expS_free"][i] = st["expS_free"][i] + bar
    for pkey in ("scr_lrow", "scr_rbs"):
        for pp_ in range(2):
            st[pkey][pp_] = st[pkey][pp_] + bar
    step = st["attn_step"]
    ow_free = {0: st["ao_free"].get(0, []), 1: st["ao_free"].get(1, [])}
    for w in range(NW):
        pp = w % 2
        ow_writers = []
        for hl in range(8):
            qt = mk.aqk[hl // 2]
            kt = mk.aqk[4 + hl // 2]
            base = (hl % 2) * 64
            p = step % 2
            bA, bB = p * 2, p * 2 + 1
            bPV = 4 + p
            eA = mk.expS[p * 2]
            eB = mk.expS[p * 2 + 1]
            q0 = 256 * w

            def sim(t, w=w, base=base, qt=qt, kt=kt, bA=bA, bB=bB, q0=q0):
                r = None
                for c in range(4):
                    if w == 0 and c < 2:
                        continue
                    kt0 = 256 * (w - 1) + 128 * c
                    bank = mk.psum[bA] if c < 2 else mk.psum[bB]
                    first = (c == 0) or (c == 2)
                    col = 0 if c in (0, 2) else (128 if c == 1 else 256)
                    ncols = 256 if c in (1, 2) else 128
                    rhs0 = q0 if c != 3 else q0 + 128
                    t.ldweights(kt[base:base + 64, kt0:kt0 + 128], tile_position=(base, 0))
                    r = t.matmul(bank[0:128, col:col + ncols],
                                 kt[base:base + 64, kt0:kt0 + 128],
                                 qt[base:base + 64, rhs0:rhs0 + ncols],
                                 start=first, stop=(c == 1 or c == 3),
                                 tile_position=(base, 0))
                return r

            deps = [rot_done[(0, hl // 2, w // 2)], rot_done[(1, hl // 2, w // 2)]]
            if w > 0:
                deps.append(rot_done[(1, hl // 2, (w - 1) // 2)])
            deps += psf(mk, bA) + psf(mk, bB)
            t_sim = b.op("T", sim, deps=deps, tag=f"L{l}h{hg}w{w}s{hl}")
            t_eA = None
            if w > 0:
                t_eA = b.op("S", (lambda s, bA=bA, eA=eA: s.activation(eA[:, 0:384], mk.psum[bA][0:128, 0:384],
                                                                       AF.Exp, scale=QK_SCALE)),
                            deps=[t_sim] + st["expS_free"][p * 2], tag=f"L{l}h{hg}w{w}eA{hl}")
                pset(mk, bA, [t_eA])
            t_eB = b.op("S", (lambda s, bB=bB, eB=eB: s.activation(eB[:, 0:384], mk.psum[bB][0:128, 0:384],
                                                                   AF.Exp, scale=QK_SCALE)),
                        deps=[t_sim] + st["expS_free"][p * 2 + 1], tag=f"L{l}h{hg}w{w}eB{hl}")
            pset(mk, bB, [t_eB])
            mks = []
            if w > 0:
                mks.append(b.op("V", (lambda v, eA=eA: v.tensor_mul(eA[:, 0:128], eA[:, 0:128],
                                                                    mk.mask_lo[:, :])),
                                deps=[t_eA] + mk.tok["consts"], tag=f"L{l}h{hg}w{w}mA0{hl}"))
                mks.append(b.op("V", (lambda v, eA=eA: v.tensor_mul(eA[:, 256:384], eA[:, 256:384],
                                                                    mk.mask_lo[:, :])),
                                deps=[t_eA], tag=f"L{l}h{hg}w{w}mA1{hl}"))
            mks.append(b.op("V", (lambda v, eB=eB: v.tensor_mul(eB[:, 0:128], eB[:, 0:128],
                                                                mk.mask_hi[:, :])),
                            deps=[t_eB] + mk.tok["consts"], tag=f"L{l}h{hg}w{w}mB0{hl}"))
            mks.append(b.op("V", (lambda v, eB=eB: v.tensor_mul(eB[:, 256:384], eB[:, 256:384],
                                                                mk.mask_hi[:, :])),
                            deps=[t_eB], tag=f"L{l}h{hg}w{w}mB1{hl}"))

            def pvmm(t, w=w, hl=hl, eA=eA, eB=eB, bPV=bPV):
                srcs_all = []
                for th in range(2):
                    if th == 0:
                        s_ = ([(2 * (w - 1), eA, 0), (2 * (w - 1) + 1, eA, 128)] if w > 0 else [])
                        s_ += [(2 * w, eB, 0)]
                    else:
                        s_ = ([(2 * (w - 1) + 1, eA, 256)] if w > 0 else [])
                        s_ += [(2 * w, eB, 128), (2 * w + 1, eB, 256)]
                    srcs_all.append(s_)
                r = None
                first = True
                for th in range(2):
                    for (vt, ebuf, col) in srcs_all[th]:
                        t.ldweights(mk.vaug[vt][:, 65 * hl:65 * hl + 65])
                        r = t.matmul(mk.psum[bPV][0:65, 128 * th:128 * (th + 1)],
                                     mk.vaug[vt][:, 65 * hl:65 * hl + 65],
                                     ebuf[:, col:col + 128],
                                     start=first, stop=(th == 1 and (vt, ebuf, col) == srcs_all[1][-1]))
                        first = False
                return r

            pv_deps = ([t_eB] + ([t_eA] if t_eA else []) + mks + psf(mk, bPV) +
                       [v_evac[j] for j in range(max(0, 2 * w - 2), 2 * w + 2)])
            t_pv = b.op("T", pvmm, deps=pv_deps, tag=f"L{l}h{hg}w{w}pv{hl}")
            st["vaug_free"].append(t_pv)
            st["expS_free"][p * 2] = [t_pv]
            st["expS_free"][p * 2 + 1] = [t_pv]
            lrow = mk.lrow[p]
            rbs = mk.rbs[p]
            bbc = 6 + p
            t_rc1 = b.op("S", (lambda s, bPV=bPV: s.activation(mk.psum[bPV][64:65, 0:256],
                                                               mk.psum[bPV][64:65, 0:256], AF.Ln)),
                         deps=[t_pv], tag=f"L{l}h{hg}w{w}rcl{hl}")
            t_rc = b.op("S", (lambda s, bPV=bPV, lrow=lrow: s.activation(lrow[64:65, 0:256],
                                                                         mk.psum[bPV][64:65, 0:256],
                                                                         AF.Exp, scale=-1.0)),
                        deps=[t_rc1] + st["scr_lrow"][p], tag=f"L{l}h{hg}w{w}rc{hl}")

            def bcmm(t, bbc=bbc, lrow=lrow):
                t.ldweights(mk.onesr[64:65, 0:64], tile_position=(64, 0))
                return t.matmul(mk.psum[bbc][0:64, 0:256], mk.onesr[64:65, 0:64],
                                lrow[64:65, 0:256], start=True, stop=True, tile_position=(64, 0))
            t_bc = b.op("T", bcmm, deps=[t_rc] + psf(mk, bbc), tag=f"L{l}h{hg}w{w}bc{hl}")
            st["scr_lrow"][p] = [t_bc]
            t_rb = b.op("S", (lambda s, bbc=bbc, rbs=rbs: s.copy(rbs[0:64, 0:256], mk.psum[bbc][0:64, 0:256])),
                        deps=[t_bc] + st["scr_rbs"][p], tag=f"L{l}h{hg}w{w}rb{hl}")
            pset(mk, bbc, [t_rb])
            j = hl // 2
            r0 = 0 if hl % 2 == 0 else 64
            t_om = b.op("V", (lambda v, bPV=bPV, pp=pp, j=j, rbs=rbs, r0=r0:
                              v.tensor_mul(mk.ow[pp][j][r0:r0 + 64, 0:256],
                                           mk.psum[bPV][0:64, 0:256],
                                           rbs[0:64, 0:256])),
                        deps=[t_rb, t_pv] + ow_free[pp], tag=f"L{l}h{hg}w{w}om{hl}")
            ow_writers.append(t_om)
            pset(mk, bPV, [t_om])
            st["scr_rbs"][p] = [t_om]
            step += 1
        wo_evacs = []
        for m in range(8):
            bWo = (pp * 2) + (m % 2)
            r0 = 0 if m < 4 else 256

            def womm(t, m=m, bWo=bWo, r0=r0, pp=pp):
                r = None
                for kk in range(4):
                    t.ldweights(mk.wo_t[kk][m][:, :])
                    r = t.matmul(mk.psum[bWo][0:128, r0:r0 + 256],
                                 mk.wo_t[kk][m][:, :], mk.ow[pp][kk][:, 0:256],
                                 start=(kk == 0), stop=(kk == 3))
                return r

            deps = list(ow_writers) + wo_tok + psf(mk, bWo)
            t_wo = b.op("T", womm, deps=deps, tag=f"L{l}h{hg}w{w}WO{m}")
            st["tab_readers"].append(t_wo)
            e = b.op("V", (lambda v, m=m, w=w, bWo=bWo, r0=r0:
                           v.tensor_add(mk.y[m][:, 256 * w:256 * (w + 1)],
                                        mk.y[m][:, 256 * w:256 * (w + 1)],
                                        mk.psum[bWo][0:128, r0:r0 + 256])),
                     deps=[t_wo], tag=f"L{l}h{hg}w{w}WOe{m}")
            pset(mk, bWo, [e])
            wo_evacs.append(e)
            y_writers.append(e)
        ow_free[pp] = wo_evacs
        ywin[(hg, w)] = wo_evacs
        if hook is not None:
            hook(w)
    st["ao_free"] = ow_free
    st["attn_step"] = step


def _wo_row(mk, kk):
    return mk.wo_row[kk][:, :]


def _w2slot(mk, j):
    return mk.w2ring[:, j * 256:j * 256 + 256]


class FfnHook:

    def __init__(self, mk, l, ywin):
        self.mk, self.l, self.ywin = mk, l, ywin
        self.rtok = {}
        self.xn2_tok = {}
        self.stats_guard = []

    def hook(self, w):
        if w == 1:
            self.start()
            self.refresh(0)
        elif w == 3:
            self.refresh(1)
            self.stats(0)
        elif w == 5:
            self.refresh(2)
            self.stats(1)
        elif w == 7:
            self.refresh(3)
            self.stats(2)
            self.stats(3)

    def start(self):
        mk, st = self.mk, self.mk.state
        self.ybf_rd = st["ybf_readers"] + st["u2_writers"]
        st["ybf_readers"] = []
        st["u2_writers"] = []
        w2seed = sum((st["w2_free"].get(j, []) for j in range(8, 16)), [])
        w2seed += st.pop("ao_hi_init", [])
        st["scr_sqb"] = {0: list(w2seed), 1: list(w2seed)}
        st["scr_stt"] = list(w2seed)
        st["scr_mr"] = list(st.get("rot_guard", []))

    def refresh(self, n):
        mk, l, st, b = self.mk, self.l, self.mk.state, self.mk.b
        dep = list(self.ybf_rd)
        for hgx in (0, 1):
            for wx in range(max(0, 2 * n - 1), 2 * n + 2):
                dep += self.ywin.get((hgx, wx), [])
        c0 = 512 * n
        toks = []
        for k in range(NCH):
            if k < 4:
                t = b.op("V", (lambda v, k=k, c0=c0: v.tensor_copy(
                    mk.ybf[k][:, 1 + c0:513 + c0], mk.y[k][:, c0:c0 + 512])),
                    deps=dep, tag=f"L{l}fr{n}_{k}")
            else:
                t = b.op("S", (lambda s, k=k, c0=c0: s.copy(
                    mk.ybf[k][:, 1 + c0:513 + c0], mk.y[k][:, c0:c0 + 512])),
                    deps=dep, tag=f"L{l}fr{n}_{k}")
            toks.append(t)
            if n == 0:
                toks.append(b.op("V", (lambda v, k=k: v.memset(mk.ybf[k][:, 0:1], 0.0)),
                                 deps=dep, tag=f"L{l}frz{k}"))
        self.rtok[n] = toks

    def stats(self, n):
        mk, l, st, b = self.mk, self.l, self.mk.state, self.mk.b
        xn2_tok = self.xn2_tok
        ybf_dep = self.rtok[n] + self.rtok.get(n - 1, [])
        par = n % 2
        bMu = 4 if par == 0 else 6
        bSd = 5 if par == 0 else 7

        def mumm(t, n=n, bMu=bMu):
            t.ldweights(mk.onesb[:, 0:1])
            r = None
            for k in range(NCH):
                r = t.matmul(mk.psum[bMu][0:1, 0:512], mk.onesb[:, 0:1],
                             _shift_ap(mk, k, 512 * n, 512),
                             start=(k == 0), stop=(k == NCH - 1))
            return r
        t_mumm = b.op("T", mumm, deps=ybf_dep + mk.tok["consts"] + psf(mk, bMu),
                      tag=f"L{l}F{n}mum")
        st["ybf_readers"].append(t_mumm)
        t_st = None
        for k in range(NCH):
            sb = mk.sqb if k % 2 == 0 else mk.sqb2
            tq = b.op("G", (lambda g, k=k, n=n, sb=sb: g.tensor_mul(
                sb[:, 0:512], _shift_ap(mk, k, 512 * n, 512),
                _shift_ap(mk, k, 512 * n, 512))),
                deps=ybf_dep + st["scr_sqb"][k % 2], tag=f"L{l}F{n}sq{k}")

            def statmm(t, k=k, sb=sb, bSd=bSd):
                if k == 0:
                    t.ldweights(mk.onesb[:, 0:1])
                return t.matmul(mk.psum[bSd][0:1, 0:512], mk.onesb[:, 0:1],
                                sb[:, 0:512], start=(k == 0), stop=(k == NCH - 1))

            deps = [tq] + mk.tok["consts"]
            if k == 0:
                deps += psf(mk, bSd)
            if t_st is not None:
                deps.append(t_st)
            t_st = b.op("T", statmm, deps=deps, tag=f"L{l}F{n}st{k}")
            st["scr_sqb"][k % 2] = [t_st]
        st["ybf_readers"].append(t_st)
        t_mu = b.op("V", (lambda v, bMu=bMu: v.tensor_scalar(mk.mr[0:1, 0:512],
                                                             mk.psum[bMu][0:1, 0:512],
                                                             1.0 / D, None, ALU.mult)),
                    deps=[t_mumm] + st["scr_mr"], tag=f"L{l}F{n}mu")
        t_e2 = b.op("V", (lambda v, bSd=bSd: v.tensor_scalar(mk.psum[bSd][0:1, 0:512],
                                                             mk.psum[bSd][0:1, 0:512],
                                                             1.0 / D, None, ALU.mult)),
                    deps=[t_st], tag=f"L{l}F{n}e2")
        t_ms = b.op("V", (lambda v: v.tensor_mul(mk.stt[0:1, 0:512],
                                                 mk.mr[0:1, 0:512], mk.mr[0:1, 0:512])),
                    deps=[t_mu] + st["scr_stt"], tag=f"L{l}F{n}ms")
        t_var = b.op("V", (lambda v, bSd=bSd: v.scalar_tensor_tensor(
            mk.psum[bSd][0:1, 0:512], mk.stt[0:1, 0:512], -1.0,
            mk.psum[bSd][0:1, 0:512], ALU.mult, ALU.add)),
            deps=[t_ms, t_e2], tag=f"L{l}F{n}var")
        st["scr_stt"] = [t_var]
        t_sdv = b.op("S", (lambda s, bSd=bSd: s.activation(mk.psum[bSd][0:1, 0:512],
                                                           mk.psum[bSd][0:1, 0:512], AF.Ln,
                                                           bias=mk.cpool[0:1, 1:2])),
                     deps=[t_var] + mk.tok["cpool"], tag=f"L{l}F{n}sd")
        t_rs = b.op("S", (lambda s, bSd=bSd: s.activation(mk.mr[0:1, 512:1024],
                                                          mk.psum[bSd][0:1, 0:512],
                                                          AF.Exp, scale=-0.5)),
                    deps=[t_sdv], tag=f"L{l}F{n}rs")
        def bcm(t, bMu=bMu, bSd=bSd):
            t.ldweights(mk.onesr[0:1, 0:128])
            t.matmul(mk.psum[bMu][0:128, 0:512], mk.onesr[0:1, 0:128],
                     mk.mr[0:1, 0:512], start=True, stop=True)
            return t.matmul(mk.psum[bSd][0:128, 0:512], mk.onesr[0:1, 0:128],
                            mk.mr[0:1, 512:1024], start=True, stop=True)
        t_bc = b.op("T", bcm, deps=[t_mu, t_rs, t_ms, t_var, t_sdv], tag=f"L{l}F{n}bc")
        st["scr_mr"] = [t_bc]
        for k in range(NCH):
            ts1 = b.op("V", (lambda v, k=k, n=n, bMu=bMu: v.tensor_sub(
                mk.stt[:, 0:512], _shift_ap(mk, k, 512 * n, 512), mk.psum[bMu][0:128, 0:512])),
                deps=[t_bc] + st["scr_stt"] + ybf_dep, tag=f"L{l}F{n}x1_{k}")
            ts2 = b.op("V", (lambda v, k=k, bSd=bSd: v.scalar_tensor_tensor(
                mk.stt[:, 0:512], mk.stt[:, 0:512], mk.pc_lng(l)[:, k:k + 1],
                mk.psum[bSd][0:128, 0:512], ALU.mult, ALU.mult)),
                deps=[ts1] + mk.tok["par"][:1], tag=f"L{l}F{n}x2_{k}")
            ts3 = b.op("S", (lambda s, k=k, n=n: s.activation(
                mk.aqk[k][:, 512 * n:512 * (n + 1)], mk.stt[:, 0:512],
                AF.Identity, bias=mk.pc_lnb(l)[:, k:k + 1])),
                deps=[ts2] + st["aqk_free"] + mk.tok["par"][:1] +
                     self.ywin.get((1, min(2 * n + 2, 7)), []),
                tag=f"L{l}F{n}x3_{k}")
            st["scr_stt"] = [ts3]
            st["ybf_readers"].append(ts1)
            xn2_tok[(k, n)] = ts3
            last_ts1, last_ts2, last_ts3 = ts1, ts2, ts3
        pset(mk, bMu, [last_ts1])
        pset(mk, bSd, [last_ts2])
        self.stats_guard = [t_bc, last_ts3, last_ts2]


def emit_ffn_w(mk, l, fh, y_writers):
    b = mk.b
    st = mk.state
    W1 = mk.dram["W1"]
    W2 = mk.dram["W2"]
    xn2_tok = fh.xn2_tok
    st["aqk_free"] = []
    for slj in range(8, 16):
        st["w2_free"][slj] = st["w2_free"].get(slj, []) + fh.stats_guard

    bar2 = mk.scr_bar()
    st["scr_gg"] = {0: list(bar2), 1: list(bar2)}
    for sl in range(4):
        st["w1_free"][sl] = list(st["wst_free"][sl % 2])
    uvdep = list(st["vaug_free"])
    st["vaug_free"] = []
    CA = [128] * (NA - 1) + [DI - 128 * (NA - 1)]

    def w1slot(sl, k):
        base = (sl % 2) * 4096 + k * 512 + 256 * (sl // 2)
        return mk.wstbig[:, base:base + 256]

    w1_mm_tok = {}
    u_written = {}
    w1_fill_tok = {}
    sl_base = st["w1_slot"]
    st["w1_slot"] += 2 * NA

    def w1_fill(np_, i, seg=None):
        sl = (sl_base + np_ * NA + i) % 4
        ca = CA[i]
        gate0 = DI + 128 * i
        dep = st["w1_free"].get(sl, [])
        kv = mk.wstbig[:, :].rearrange("p (k c) -> p k c", c=512)
        co = 256 * (sl // 2)
        dsta = kv[:, 8 * (sl % 2):8 * (sl % 2) + 8, co:co + ca]
        dstg = kv[:, 8 * (sl % 2):8 * (sl % 2) + 8, co + 128:co + 128 + ca]
        srca = W1[l, 0:1024, 128 * i:128 * i + ca].rearrange("(k p) c -> p k c", p=128)
        srcg = W1[l, 0:1024, gate0:gate0 + ca].rearrange("(k p) c -> p k c", p=128)
        seg = seg or (l, np_)
        ch = f"w1{sl % 2}"
        b.dma("G", (lambda g, dsta=dsta, srca=srca: g.dma_start(dsta, srca)),
              deps=dep, ch=ch, look=2, seg=seg, tag=f"L{l}W1f{np_}_{i}a")
        w1_fill_tok[(np_, i)] = b.dma(
            "G", (lambda g, dstg=dstg, srcg=srcg: g.dma_start(dstg, srcg)),
            deps=[], ch=ch, look=2, seg=seg, tag=f"L{l}W1f{np_}_{i}g")

    def w1_compute(np_):
        for i in range(NA):
            if (np_, i) not in w1_fill_tok:
                w1_fill(np_, i)
            sl = (sl_base + np_ * NA + i) % 4
            ca = CA[i]
            last = w1_fill_tok[(np_, i)]
            banks = [(i % 2) * 4 + j for j in range(4)]

            def w1mm(t, sl=sl, i=i, ca=ca, np_=np_, banks=banks):
                r = None
                for k in range(8):
                    t.ldweights(w1slot(sl, k)[:, 0:ca])
                    for jn in range(2):
                        n = 2 * np_ + jn
                        r = t.matmul(mk.psum[banks[jn]][0:ca, 0:512],
                                     w1slot(sl, k)[:, 0:ca],
                                     mk.aqk[k][:, 512 * n:512 * (n + 1)],
                                     start=(k == 0), stop=(k == 7))
                    t.ldweights(w1slot(sl, k)[:, 128:128 + ca])
                    for jn in range(2):
                        n = 2 * np_ + jn
                        r = t.matmul(mk.psum[banks[2 + jn]][0:ca, 0:512],
                                     w1slot(sl, k)[:, 128:128 + ca],
                                     mk.aqk[k][:, 512 * n:512 * (n + 1)],
                                     start=(k == 0), stop=(k == 7))
                return r

            deps = [last] + [xn2_tok[(k, 2 * np_ + jn)] for k in range(8) for jn in range(2)]
            for bk in banks:
                deps += psf(mk, bk)
            t_mm = b.op("T", w1mm, deps=deps, tag=f"L{l}W1m{np_}_{i}")
            st["w1_free"][sl] = [t_mm]
            st["aqk_free"].append(t_mm)
            w1_mm_tok[(np_, i)] = t_mm
            for jn in range(2):
                n = 2 * np_ + jn
                gbuf = mk.gg if jn == 0 else mk.gg2
                tg = b.op("S", (lambda s, i=i, ca=ca, bk=banks[2 + jn], gbuf=gbuf:
                               s.activation(gbuf[0:ca, 0:512],
                                            mk.psum[bk][0:ca, 0:512], AF.Gelu,
                                            bias=mk.pc_b1g(l)[0:ca, i:i + 1])),
                          deps=[t_mm] + st["scr_gg"][jn] + mk.tok["par"][:1], tag=f"L{l}W1g{np_}_{i}_{jn}")
                udst = (mk.u if jn == 0 else mk.u2)[i]
                uw = b.op("V", (lambda v, i=i, ca=ca, bk=banks[jn], gbuf=gbuf, udst=udst:
                               v.scalar_tensor_tensor(udst[0:ca, 0:512], mk.psum[bk][0:ca, 0:512],
                                                      mk.pc_b1a(l)[0:ca, i:i + 1], gbuf[0:ca, 0:512],
                                                      ALU.add, ALU.mult)),
                          deps=[tg] + st["u_free"].get((jn, i), []) + (uvdep if np_ == 0 else []),
                          tag=f"L{l}W1u{np_}_{i}_{jn}")
                st["scr_gg"][jn] = [uw]
                pset(mk, banks[jn], [uw])
                pset(mk, banks[2 + jn], [uw])
                st["u_free"][(jn, i)] = []
                u_written[(np_, jn, i)] = uw
                if jn == 1:
                    st["u2_writers"].append(uw)

    def w2_emit(np_):
        if np_ == 0:
            ao_seed = st["ao_free"].get(0, []) + st["ao_free"].get(1, [])
            for slj in range(16):
                st["w2_free"][slj] = st["w2_free"].get(slj, []) + ao_seed
        for mg in range(4):
            banks = [(mg % 2) * 4 + j for j in range(4)]
            prev_mm = None
            first_deps = []
            for bk in banks:
                first_deps += psf(mk, bk)
            for kk in range(NA):
                slj = st["w2_slot"] % 16
                st["w2_slot"] += 1
                ck = 128 if kk < NA - 1 else DI - 128 * (NA - 1)
                ft = b.dma("G", (lambda g, kk=kk, slj=slj, ck=ck, mg=mg:
                                 g.dma_start(_w2slot(mk, slj)[0:ck, 0:256],
                                             W2[l, 128 * kk:128 * kk + ck, 256 * mg:256 * (mg + 1)])),
                           deps=st["w2_free"].get(slj, []), ch=f"w2{slj % 4}",
                           look=3, seg=(l, "w2", np_), tag=f"L{l}W2f{np_}_{mg}_{kk}")

                def w2mm(t, np_=np_, mg=mg, kk=kk, slj=slj, ck=ck, banks=banks):
                    r = None
                    for jm in range(2):
                        t.ldweights(_w2slot(mk, slj)[0:ck, 128 * jm:128 * (jm + 1)])
                        for jn in range(2):
                            n = 2 * np_ + jn
                            usrc = (mk.u if jn == 0 else mk.u2)[kk]
                            r = t.matmul(mk.psum[banks[2 * jm + jn]][0:128, 0:512],
                                         _w2slot(mk, slj)[0:ck, 128 * jm:128 * (jm + 1)],
                                         usrc[0:ck, 0:512],
                                         start=(kk == 0), stop=(kk == NA - 1))
                    return r

                deps = [ft] + [u_written[(np_, jn, kk)] for jn in range(2)]
                if prev_mm is not None:
                    deps.append(prev_mm)
                else:
                    deps += first_deps
                t_w2 = b.op("T", w2mm, deps=deps, tag=f"L{l}W2m{np_}_{mg}_{kk}")
                st["w2_free"][slj] = [t_w2]
                for jn in range(2):
                    st["u_free"][(jn, kk)] = st["u_free"].get((jn, kk), []) + [t_w2]
                prev_mm = t_w2
            for jm in range(2):
                m = 2 * mg + jm
                for jn in range(2):
                    n = 2 * np_ + jn
                    e = b.op("V", (lambda v, m=m, n=n, bk=banks[2 * jm + jn]:
                                   v.scalar_tensor_tensor(mk.y[m][:, 512 * n:512 * (n + 1)],
                                                          mk.psum[bk][0:128, 0:512],
                                                          mk.pc_b2(l)[:, m:m + 1],
                                                          mk.y[m][:, 512 * n:512 * (n + 1)],
                                                          ALU.add, ALU.add)),
                             deps=[t_w2] + mk.tok["par"][:1], tag=f"L{l}W2e{np_}_{m}_{n}")
                    pset(mk, banks[2 * jm + jn], [e])
                    y_writers.append(e)
        return t_w2

    w1_compute(0)
    for i in range(4):
        w1_fill(1, i, seg=(l, 1, "h"))
    w2_emit(0)
    w1_compute(1)
    last_w2 = w2_emit(1)
    last_w1mm = w1_mm_tok[(1, NA - 1)]
    st["wst_free"] = {0: [last_w1mm], 1: [last_w1mm]}
    st["vaug_free"] = [last_w2]
    st["ybf_readers"].append(last_w2)
    st["ao_free"] = {0: [last_w2], 1: [last_w2]}


"""GLA token-mixer branch: gla_out = (GLA(xn) normalized, gated) @ Wo.

Phases:
  P1a: q,k,v,gk projections from xn (ybf cols 1:T+1); weights single-buffered
       through wstbig block 0; t16/w16/wgk2 parked in block-1 region.
  P1b: per head: gk pre -> sp=-log_sigmoid path (Exp,Ln), masked cumsum scan
       -> G' (gG); factors qg (in-place over q), ki (in-place over k), glast.
  P1c: chunk scan: per (c, h): kg scale+transpose, v transposes, A^T+mask,
       o = o_intra + o_inter, S update; o -> gla_o scratch (HBM, bf16).
  P1d: g projection (-> aqk); o groupnorm * silu(g); @ gWo -> gla_out (HBM f32).
"""

LNSCALE_COL = 3
EPS_COL = 4


def emit_gla(mk):
    b = mk.b
    st = mk.state
    xn = lambda k, n: mk.ybf[k][:, 1 + 512 * n:1 + 512 * (n + 1)]
    ybf_dep = st["ybf_ready"]

    gw_free = {0: [], 1: []}

    def wslot(bi, k):
        if bi == 0:
            return mk.wst[0][k]
        return mk.w2ring[:, 512 * k:512 * (k + 1)]

    def fill(src_ap, tag, bi=0):
        dst = (mk.wstbig[:, 0:4096] if bi == 0
               else mk.w2ring[:, 0:4096]).rearrange("p (k c) -> p k c", c=512)
        tok = b.dma("G", (lambda g, dst=dst, src_ap=src_ap: g.dma_start(dst, src_ap)),
                    deps=gw_free[bi], ch=f"gw{bi}", tag=tag)
        return tok

    def proj4(ftok, dst_tiles, tag, bi=0, extra_dep=()):
        evs = []
        for m in range(4):
            banks = [(m % 2) * 4 + n for n in range(4)]

            def mm(t, m=m, banks=banks, bi=bi):
                r = None
                for k in range(8):
                    t.ldweights(wslot(bi, k)[:, 128 * m:128 * (m + 1)])
                    for n in range(4):
                        r = t.matmul(mk.psum[banks[n]][0:128, 0:512],
                                     wslot(bi, k)[:, 128 * m:128 * (m + 1)],
                                     xn(k, n), start=(k == 0), stop=(k == 7))
                return r

            deps = [ftok] + list(extra_dep) + ybf_dep
            for bk in banks:
                deps += psf(mk, bk)
            tk = b.op("T", mm, deps=deps, tag=f"{tag}m{m}")
            st["ybf_readers"].append(tk)
            for n in range(4):
                e = b.op("S", (lambda s, m=m, n=n, bk=banks[n]:
                               s.copy(dst_tiles[m][:, 512 * n:512 * (n + 1)],
                                      mk.psum[bk][0:128, 0:512])),
                         deps=[tk], tag=f"{tag}e{m}_{n}")
                pset(mk, banks[n], [e])
                evs.append(e)
        return [tk], evs

    fq = fill(mk.dram["gWq"][0:1024, 0:512].rearrange("(k p) c -> p k c", p=128), "gfq", 0)
    fk = fill(mk.dram["gWk"][0:1024, 0:512].rearrange("(k p) c -> p k c", p=128), "gfk", 1)
    gw_free[0], q_ev = proj4(fq, [mk.aqk[m] for m in range(4)], "gq", 0)
    gw_free[1], k_ev = proj4(fk, [mk.aqk[4 + m] for m in range(4)], "gk", 1)
    fv0 = fill(mk.dram["gWv"][0:1024, 0:512].rearrange("(k p) c -> p k c", p=128), "gfv0", 0)
    gw_free[0], v_ev0 = proj4(fv0, [mk.gv[m] for m in range(4)], "gv0", 0, extra_dep=st["vaug_free"])
    st["vaug_free"] = []
    fv1 = fill(mk.dram["gWv"][0:1024, 512:1024].rearrange("(k p) c -> p k c", p=128), "gfv1", 1)
    gw_free[1], v_ev1 = proj4(fv1, [mk.gv[4 + m] for m in range(4)], "gv1", 1)
    v_ev = v_ev0 + v_ev1
    st["scr_gG"] = list(st["scr_gG"]) + list(gw_free[1])

    wst1_users = []
    f16 = b.dma("G", (lambda g: g.dma_start(
        mk.w16[:, :].rearrange("p (k c) -> p k c", c=16),
        mk.dram["gWgk1"][0:1024, 0:16].rearrange("(k p) c -> p k c", p=128))),
        ch="gw2", tag="gf16")
    fg2 = b.dma("G", (lambda g: g.dma_start(mk.wgk2[:, :], mk.dram["gWgk2"][0:16, 0:512])),
                ch="gw2", tag="gfg2")
    t16_ev = []
    for n in range(NT4):
        bank = 4 + n % 2

        def mm16(t, n=n, bank=bank):
            r = None
            for k in range(8):
                t.ldweights(mk.w16[:, 16 * k:16 * (k + 1)])
                r = t.matmul(mk.psum[bank][0:16, 0:512], mk.w16[:, 16 * k:16 * (k + 1)],
                             xn(k, n), start=(k == 0), stop=(k == 7))
            return r

        tk = b.op("T", mm16, deps=[f16] + ybf_dep + psf(mk, bank), tag=f"gt16_{n}")
        wst1_users.append(tk)
        st["ybf_readers"].append(tk)
        e = b.op("S", (lambda s, n=n, bank=bank:
                       s.copy(mk.t16[0:16, 512 * n:512 * (n + 1)], mk.psum[bank][0:16, 0:512])),
                 deps=[tk], tag=f"gt16e{n}")
        pset(mk, bank, [e])
        t16_ev.append(e)

    par_dep = mk.tok["par"]
    glast_tok = {}
    fac_tok = {}
    scan_tok = []
    prev_head = []
    for h in range(GH):
        sp_ev = []
        for n in range(NT4):
            bank = (n % 2)

            def mmp(t, h=h, n=n, bank=bank):
                t.ldweights(mk.wgk2[0:16, 128 * h:128 * (h + 1)])
                return t.matmul(mk.psum[bank][0:128, 0:512], mk.wgk2[0:16, 128 * h:128 * (h + 1)],
                                mk.t16[0:16, 512 * n:512 * (n + 1)], start=True, stop=True)
            tk = b.op("T", mmp, deps=[fg2, t16_ev[n]] + psf(mk, bank), tag=f"gp{h}_{n}")
            wst1_users.append(tk)
            e1 = b.op("S", (lambda s, n=n, bank=bank, h=h:
                            s.activation(mk.gsp[:, 512 * n:512 * (n + 1)], mk.psum[bank][0:128, 0:512],
                                         AF.Exp, bias=mk.bgk2n[:, h:h + 1], scale=-1.0)),
                     deps=[tk] + par_dep + prev_head + st["scr_gsp"], tag=f"gpe{h}_{n}")
            pset(mk, bank, [e1])
            e2 = b.op("S", (lambda s, n=n: s.activation(mk.gsp[:, 512 * n:512 * (n + 1)],
                                                        mk.gsp[:, 512 * n:512 * (n + 1)],
                                                        AF.Ln, bias=mk.cpool[:, 0:1])),
                     deps=[e1] + mk.tok["cpool"], tag=f"gpl{h}_{n}")
            sp_ev.append(e2)
        t_scan = b.op("V", (lambda v: v.tensor_tensor_scan(mk.gG[:, :], mk.scanmask[:, :],
                                                           mk.gsp[:, :], 0.0, ALU.mult, ALU.add)),
                      deps=sp_ev + mk.tok["consts"] + st["scr_gG"], tag=f"gsc{h}")
        scan_tok.append(t_scan)
        st["scr_gsp"] = [t_scan]
        t_gl = b.op("S", (lambda s, h=h: s.activation(
            mk.glast[:, 32 * h:32 * (h + 1)],
            mk.gG[:, :].rearrange("p (c g) -> p c g", g=GC)[:, :, 63:64],
            AF.Exp, scale=-1.0 / 16.0)),
            deps=[t_scan] + st["scr_glast"], tag=f"ggl{h}")
        glast_tok[h] = t_gl
        ftoks = []
        for n in range(NT4):
            ea = mk.e512a if n % 2 == 0 else mk.e512b
            t_e1 = b.op("S", (lambda s, n=n, ea=ea: s.activation(
                ea[:, 0:512], mk.gG[:, 512 * n:512 * (n + 1)], AF.Exp,
                bias=mk.cpool[:, LNSCALE_COL:LNSCALE_COL + 1], scale=-1.0 / 16.0)),
                deps=[t_scan] + mk.tok["cpool"] + st["scr_e"][n % 2], tag=f"gf1{h}_{n}")
            t_q = b.op("V", (lambda v, h=h, n=n, ea=ea: v.tensor_mul(
                mk.aqk[h][:, 512 * n:512 * (n + 1)], mk.aqk[h][:, 512 * n:512 * (n + 1)],
                ea[:, 0:512])),
                deps=[t_e1, q_ev[4 * h + n]], tag=f"gfq{h}_{n}")
            t_e2 = b.op("S", (lambda s, n=n, ea=ea: s.activation(
                ea[:, 0:512], mk.gG[:, 512 * n:512 * (n + 1)], AF.Exp, scale=1.0 / 16.0)),
                deps=[t_q, t_e1], tag=f"gf2{h}_{n}")
            t_k = b.op("V", (lambda v, h=h, n=n, ea=ea: v.tensor_mul(
                mk.aqk[4 + h][:, 512 * n:512 * (n + 1)], mk.aqk[4 + h][:, 512 * n:512 * (n + 1)],
                ea[:, 0:512])),
                deps=[t_e2, k_ev[4 * h + n]], tag=f"gfk{h}_{n}")
            st["scr_e"][n % 2] = [t_k]
            ftoks += [t_q, t_k]
        fac_tok[h] = ftoks
        prev_head = ftoks
        st["scr_gG"] = ftoks + [t_gl]

    go_cnt = 0
    stage_done = {}
    ost_wr = {}
    sst_rd = {h: [] for h in range(GH)}
    sst_wr = {h: [] for h in range(GH)}
    zS = []
    for h in range(GH):
        zS.append(b.op("V", (lambda v, h=h: v.memset(mk.Sst[h][:, :], 0.0)),
                       deps=list(prev_head) + scan_tok, tag=f"gz{h}"))
        zS[-1] = b.op("V", (lambda v, h=h: v.memset(mk.Sbf[h][:, :], 0.0)),
                      deps=[zS[-1]] + list(prev_head), tag=f"gzb{h}")
    step = 0
    o_dma = []
    for c in range(GNC):
        for h in range(GH):
            p = step % 2
            bTR, bA, bO, bSD = p, 2 + p, 4 + p, 6 + p
            qg = mk.aqk[h][:, GC * c:GC * (c + 1)]
            ki = mk.aqk[4 + h][:, GC * c:GC * (c + 1)]
            t_kg = b.op("V", (lambda v, ki=ki, h=h, c=c: v.tensor_scalar_mul(
                mk.kgs[:, 0:GC], ki, mk.glast[:, 32 * h + c:32 * h + c + 1])),
                deps=[glast_tok[h]] + fac_tok[h] + st["scr_kgs"] +
                     (prev_head if c == 0 else []), tag=f"gkg{c}_{h}")
            def mmtr(t, h=h, c=c, bTR=bTR):
                t.ldweights(mk.kgs[:, 0:GC])
                t.matmul(mk.psum[bTR][0:64, 0:128], mk.kgs[:, 0:GC], mk.ident_b[:, :],
                         start=True, stop=False)
                t.ldweights(mk.gv[2 * h][:, GC * c:GC * (c + 1)])
                t.matmul(mk.psum[bTR][0:64, 128:256], mk.gv[2 * h][:, GC * c:GC * (c + 1)],
                         mk.ident_b[:, :], start=False, stop=False)
                t.ldweights(mk.gv[2 * h + 1][:, GC * c:GC * (c + 1)])
                return t.matmul(mk.psum[bTR][0:64, 256:384], mk.gv[2 * h + 1][:, GC * c:GC * (c + 1)],
                                mk.ident_b[:, :], start=False, stop=True)
            deps = [t_kg] + mk.tok["consts"] + psf(mk, bTR)
            if c == 0:
                deps += v_ev
            t_tr = b.op("T", mmtr, deps=deps, tag=f"gtr{c}_{h}")
            t_ek = b.op("S", (lambda s, bTR=bTR: s.copy(mk.kgt[0:64, 0:128], mk.psum[bTR][0:64, 0:128])),
                        deps=[t_tr] + st["scr_kgt"], tag=f"gek{c}_{h}")
            t_ev = b.op("S", (lambda s, bTR=bTR: s.copy(mk.vt[0:64, 0:256], mk.psum[bTR][0:64, 128:384])),
                        deps=[t_tr] + st["scr_vt"], tag=f"gev{c}_{h}")
            pset(mk, bTR, [t_ek, t_ev])
            def mmat(t, qg=qg, ki=ki, bA=bA):
                t.ldweights(ki)
                return t.matmul(mk.psum[bA][0:64, 0:64], ki, qg, start=True, stop=True)
            t_at = b.op("T", mmat, deps=fac_tok[h] + psf(mk, bA), tag=f"gat{c}_{h}")
            t_am = b.op("V", (lambda v, bA=bA: v.tensor_mul(mk.at_sb[0:64, 0:64],
                                                            mk.psum[bA][0:64, 0:64],
                                                            mk.mask_hi[0:64, 0:64])),
                        deps=[t_at] + st["scr_at"] + mk.tok["consts"], tag=f"gam{c}_{h}")
            pset(mk, bA, [t_am])
            def mmo(t, h=h, qg=qg, bO=bO):
                import os
                nointer = os.environ.get("MK_GLA_NOINTER") == "1"
                nointra = os.environ.get("MK_GLA_NOINTRA") == "1"
                if nointra:
                    t.ldweights(mk.Sbf[h][:, 0:128])
                    t.matmul(mk.psum[bO][0:128, 0:GC], mk.Sbf[h][:, 0:128], qg,
                             start=True, stop=False)
                    t.ldweights(mk.Sbf[h][:, 128:256])
                    return t.matmul(mk.psum[bO][0:128, GC:2 * GC], mk.Sbf[h][:, 128:256], qg,
                                    start=False, stop=True)
                t.ldweights(mk.vt[0:64, 0:128])
                t.matmul(mk.psum[bO][0:128, 0:GC], mk.vt[0:64, 0:128], mk.at_sb[0:64, 0:64],
                         start=True, stop=False)
                t.ldweights(mk.vt[0:64, 128:256])
                r2 = t.matmul(mk.psum[bO][0:128, GC:2 * GC], mk.vt[0:64, 128:256], mk.at_sb[0:64, 0:64],
                              start=False, stop=nointer)
                if nointer:
                    return r2
                if os.environ.get("MK_GLA_SWAP") == "1":
                    pass
                t.ldweights(mk.Sbf[h][:, 0:128])
                t.matmul(mk.psum[bO][0:128, 0:GC], mk.Sbf[h][:, 0:128], qg,
                         start=False, stop=False)
                t.ldweights(mk.Sbf[h][:, 128:256])
                return t.matmul(mk.psum[bO][0:128, GC:2 * GC], mk.Sbf[h][:, 128:256], qg,
                                start=False, stop=True)
            t_o = b.op("T", mmo, deps=[t_am, t_ev, t_ek, zS[h]] + sst_wr[h] + psf(mk, bO),
                       tag=f"go{c}_{h}")
            sst_rd[h] = [t_o]
            def mmsd(t, bSD=bSD):
                t.ldweights(mk.kgt[0:64, 0:128])
                return t.matmul(mk.psum[bSD][0:128, 0:256], mk.kgt[0:64, 0:128],
                                mk.vt[0:64, 0:256], start=True, stop=True)
            t_sd = b.op("T", mmsd, deps=[t_ek, t_ev] + psf(mk, bSD), tag=f"gsd{c}_{h}")
            st["scr_kgt"] = [t_sd]
            st["scr_vt"] = [t_sd, t_o]
            st["scr_at"] = [t_o]
            t_s1 = b.op("V", (lambda v, h=h, c=c: v.tensor_scalar_mul(
                mk.Sst[h][:, :], mk.Sst[h][:, :], mk.glast[:, 32 * h + c:32 * h + c + 1])),
                deps=[t_o, zS[h], glast_tok[h]] + sst_wr[h], tag=f"gs1{c}_{h}")
            t_s2 = b.op("V", (lambda v, h=h, bSD=bSD: v.tensor_add(
                mk.Sst[h][:, :], mk.Sst[h][:, :], mk.psum[bSD][0:128, 0:256])),
                deps=[t_s1, t_sd], tag=f"gs2{c}_{h}")
            t_s3 = b.op("V", (lambda v, h=h: v.tensor_copy(mk.Sbf[h][:, :], mk.Sst[h][:, :])),
                        deps=[t_s2] + sst_rd[h], tag=f"gs3{c}_{h}")
            sst_wr[h] = [t_s2, t_s3]
            pset(mk, bSD, [t_s2])
            pp = (c // 4) % 2
            grp = c % 4
            oev = []
            for half in range(2):
                j = 2 * h + half
                e = b.op("S", (lambda s, pp=pp, j=j, grp=grp, half=half, bO=bO:
                               s.copy(mk.gost[pp][j][:, GC * grp:GC * (grp + 1)],
                                      mk.psum[bO][0:128, GC * half:GC * (half + 1)])),
                         deps=[t_o] + stage_done.get((pp, j), []), tag=f"goe{c}_{h}_{half}")
                ost_wr.setdefault((pp, j), []).append(e)
                oev.append(e)
            pset(mk, bO, oev)
            step += 1
        if c % 4 == 3:
            pp = (c // 4) % 2
            c0 = c - 3
            for j in range(8):
                dm = b.dma("Y", (lambda sy, pp=pp, j=j, c0=c0:
                                 sy.dma_start(mk.dram["gla_o"][128 * j:128 * (j + 1),
                                                              GC * c0:GC * (c0 + 4)],
                                              mk.gost[pp][j][:, 0:256])),
                           deps=ost_wr.get((pp, j), []), ch=f"go{pp}", tag=f"godma{c0}_{j}")
                stage_done[(pp, j)] = [dm]
                ost_wr[(pp, j)] = []
                o_dma.append(dm)

    last_users = []
    for h in range(GH):
        last_users += sst_rd[h] + sst_wr[h]
    last_users += [t_at, t_o]
    fg0 = fill(mk.dram["gWg"][0:1024, 0:512].rearrange("(k p) c -> p k c", p=128), "gfg0", 0)
    gw_free[0], g_ev0 = proj4(fg0, [mk.aqk[m] for m in range(4)], "gg0", 0, extra_dep=last_users)
    fg1 = fill(mk.dram["gWg"][0:1024, 512:1024].rearrange("(k p) c -> p k c", p=128), "gfg1", 0)
    gw_free[0], g_ev1 = proj4(fg1, [mk.aqk[4 + m] for m in range(4)], "gg1", 0, extra_dep=last_users)
    g_ev = g_ev0 + g_ev1

    o_ld = {}
    for n in range(NT4):
        for j in range(8):
            o_ld[(n, j)] = b.dma("Y", (lambda sy, n=n, j=j:
                                       sy.dma_start(mk.gv[j][:, 512 * n:512 * (n + 1)],
                                                    mk.dram["gla_o"][128 * j:128 * (j + 1),
                                                                     512 * n:512 * (n + 1)])),
                                 deps=(o_dma + [t_tr]) if (n == 0 and j == 0) else [],
                                 ch="gi0", tag=f"gold{n}_{j}")
    on_tok = {}
    bar = mk.scr_bar()
    st["scr_p1d"] = {k: list(bar) for k in ("osq", "gsil", "gnt", "outA", "outB")}
    rq_prev = list(bar)
    sil_toks = []
    for n in range(NT4):
        for h in range(GH):
            bst = (n * GH + h) % 2
            brq = 2 + (n * GH + h) % 2
            sq_toks = []
            t_pr = None
            for half in range(2):
                j = 2 * h + half
                tq = b.op("S", (lambda s, n=n, j=j: s.activation(
                    mk.osq[:, 0:512], mk.gv[j][:, 512 * n:512 * (n + 1)], AF.Square)),
                    deps=[o_ld[(n, j)]] + st["scr_p1d"]["osq"] + ([t_pr] if t_pr else []),
                    tag=f"gnsq{n}_{h}_{half}")

                def mmst(t, half=half, bst=bst):
                    if half == 0:
                        t.ldweights(mk.onesb[:, 0:1])
                    return t.matmul(mk.psum[bst][0:1, 0:512], mk.onesb[:, 0:1],
                                    mk.osq[:, 0:512], start=(half == 0), stop=(half == 1))
                t_pr = b.op("T", mmst, deps=[tq] + (psf(mk, bst) if half == 0 else []) + mk.tok["consts"],
                            tag=f"gnst{n}_{h}_{half}")
                st["scr_p1d"]["osq"] = [t_pr]
                sq_toks.append(t_pr)
            t_lg = b.op("S", (lambda s, bst=bst: s.activation(
                mk.psum[bst][0:1, 0:512], mk.psum[bst][0:1, 0:512], AF.Ln,
                bias=mk.cpool[0:1, EPS_COL:EPS_COL + 1], scale=1.0 / GDV)),
                deps=[t_pr] + mk.tok["cpool"], tag=f"gnlg{n}_{h}")
            t_rq = b.op("S", (lambda s, bst=bst: s.activation(
                mk.rqg[0:1, 0:512], mk.psum[bst][0:1, 0:512], AF.Exp, scale=-0.5)),
                deps=[t_lg] + rq_prev, tag=f"gnrq{n}_{h}")
            pset(mk, bst, [t_rq])

            def mmbc(t, brq=brq):
                t.ldweights(mk.onesr[0:1, 0:128])
                return t.matmul(mk.psum[brq][0:128, 0:512], mk.onesr[0:1, 0:128],
                                mk.rqg[0:1, 0:512], start=True, stop=True)
            t_bc = b.op("T", mmbc, deps=[t_rq] + psf(mk, brq), tag=f"gnbc{n}_{h}")
            rq_prev = [t_bc]
            for half in range(2):
                j = 2 * h + half
                t_si = b.op("S", (lambda s, n=n, j=j: s.activation(
                    mk.gsil[:, 0:512], mk.aqk[j][:, 512 * n:512 * (n + 1)], AF.Silu)),
                    deps=[g_ev[4 * j + n]] + st["scr_p1d"]["gsil"], tag=f"gnsi{n}_{j}")
                sil_toks.append(t_si)
                t_v1 = b.op("V", (lambda v, n=n, j=j, half=half, brq=brq: v.scalar_tensor_tensor(
                    mk.gnt[:, 0:512], mk.gv[j][:, 512 * n:512 * (n + 1)],
                    mk.gnwv[:, half:half + 1], mk.psum[brq][0:128, 0:512],
                    ALU.mult, ALU.mult)),
                    deps=[t_bc, o_ld[(n, j)]] + par_dep + st["scr_p1d"]["gnt"], tag=f"gnv1{n}_{j}")
                t_v2 = b.op("V", (lambda v, n=n, j=j: v.tensor_mul(
                    mk.gv[j][:, 512 * n:512 * (n + 1)], mk.gnt[:, 0:512], mk.gsil[:, 0:512])),
                    deps=[t_v1, t_si], tag=f"gnv2{n}_{j}")
                st["scr_p1d"]["gsil"] = [t_v2]
                st["scr_p1d"]["gnt"] = [t_v2]
                on_tok[(n, j)] = t_v2
            pset(mk, brq, [t_v1])

    out_dma = []
    fwo0 = fill(mk.dram["gWo"][0:1024, 0:512].rearrange("(k p) c -> p k c", p=128), "gfo0", 0)
    fwo1 = None
    outst_free = {0: [], 1: []}
    for half_m in range(2):
        ftok = fwo0 if half_m == 0 else fwo1
        for m in range(4):
            gm = 4 * half_m + m
            banks = [(m % 2) * 4 + n for n in range(4)]

            def mmwo(t, m=m, banks=banks):
                r = None
                for k in range(8):
                    t.ldweights(mk.wst[0][k][:, 128 * m:128 * (m + 1)])
                    for n in range(4):
                        r = t.matmul(mk.psum[banks[n]][0:128, 0:512],
                                     mk.wst[0][k][:, 128 * m:128 * (m + 1)],
                                     mk.gv[k][:, 512 * n:512 * (n + 1)],
                                     start=(k == 0), stop=(k == 7))
                return r

            deps = [ftok] + [on_tok[(n, j)] for n in range(NT4) for j in range(8)]
            for bk in banks:
                deps += psf(mk, bk)
            tk = b.op("T", mmwo, deps=deps, tag=f"gwo{gm}")
            gw_free[0] = [tk]
            for n in range(4):
                slot = n % 2
                stg = mk.outA if slot == 0 else mk.outB
                e = b.op("S", (lambda s, stg=stg, bk=banks[n]:
                               s.copy(stg[:, 0:512], mk.psum[bk][0:128, 0:512])),
                         deps=[tk] + outst_free[slot], tag=f"gwoe{gm}_{n}")
                pset(mk, banks[n], [e])
                dm = b.dma("Y", (lambda sy, stg=stg, gm=gm, n=n:
                                 sy.dma_start(mk.dram["gla_out"][128 * gm:128 * (gm + 1),
                                                                 512 * n:512 * (n + 1)],
                                              stg[:, 0:512])),
                           deps=[e], ch=f"gu{slot}", tag=f"gud{gm}_{n}")
                outst_free[slot] = [dm]
                out_dma.append(dm)
        if half_m == 0:
            fwo1 = fill(mk.dram["gWo"][0:1024, 512:1024].rearrange("(k p) c -> p k c", p=128), "gfo1", 0)

    st["aqk_free"] = sil_toks
    st["vaug_free"] = [tk]
    st["tab_readers"] += [t_pr, t_lg, t_rq] + sq_toks
    st["wst_free"] = {0: list(gw_free[0]), 1: list(wst1_users)}
    st["scr_dma_extra"] = list(out_dma) + list(o_dma)
    return out_dma


_CACHE = {}


def _build():
    if "nc" not in _CACHE:
        _CACHE["nc"] = build_program(n_layers=6, do_gla=True)
    return _CACHE["nc"]


def kernel(L0_attn_x, L0_attn_Wq, L0_attn_Wk, L0_attn_Wv, L0_attn_Wg,
           L0_attn_Wgk1, L0_attn_Wgk2, L0_attn_bgk2, L0_attn_Wo, L0_attn_gnw,
           L0_ffn_Wqkv, L0_ffn_q_scale, L0_ffn_k_scale, L0_ffn_Wo,
           L0_ffn_ln_g, L0_ffn_ln_b, L0_ffn_W1, L0_ffn_b1, L0_ffn_W2,
           L0_ffn_b2):
    from concourse.bass_utils import run_bass_kernel_spmd
    nc, m = _build()
    inp = {
        "L0_ffn_ln_g": L0_ffn_ln_g, "L0_ffn_ln_b": L0_ffn_ln_b,
        "L0_ffn_b2": L0_ffn_b2, "L0_ffn_b1": L0_ffn_b1,
        "L0_ffn_q_scale": L0_ffn_q_scale, "L0_ffn_k_scale": L0_ffn_k_scale,
        "L0_attn_gnw": L0_attn_gnw, "L0_attn_bgk2": L0_attn_bgk2,
    }
    params = pack_params(6, True, inp)
    base = {
        "params": params,
        "Wqkv": np.ascontiguousarray(L0_ffn_Wqkv, dtype=np.float32),
        "Wo": np.ascontiguousarray(L0_ffn_Wo, dtype=np.float32),
        "W1": np.ascontiguousarray(L0_ffn_W1, dtype=np.float32),
        "W2": np.ascontiguousarray(L0_ffn_W2, dtype=np.float32),
        "gWq": np.ascontiguousarray(L0_attn_Wq, dtype=np.float32),
        "gWk": np.ascontiguousarray(L0_attn_Wk, dtype=np.float32),
        "gWv": np.ascontiguousarray(L0_attn_Wv, dtype=np.float32),
        "gWg": np.ascontiguousarray(L0_attn_Wg, dtype=np.float32),
        "gWgk1": np.ascontiguousarray(L0_attn_Wgk1, dtype=np.float32),
        "gWgk2": np.ascontiguousarray(L0_attn_Wgk2, dtype=np.float32),
        "gWo": np.ascontiguousarray(L0_attn_Wo, dtype=np.float32),
    }
    xs = np.ascontiguousarray(L0_attn_x, dtype=np.float32)
    in_maps = []
    for c in range(8):
        im = dict(base)
        im["x"] = xs[c]
        in_maps.append(im)
    res = run_bass_kernel_spmd(nc, in_maps, core_ids=list(range(8)))
    out = np.stack([np.asarray(r["out"], np.float32) for r in res.results], 0)
    return out


# version 06
# speedup vs baseline: 1.7609x; 1.0717x over previous
import math
import numpy as np
import ml_dtypes
import concourse.bass as bass
import concourse.mybir as mybir

"""Mini dependency scheduler for raw-Bass straight-line programs.

Engines: 'T' tensor, 'V' vector, 'S' scalar, 'G' gpsimd, 'Y' sync.
Each op declares its dependency tokens; finalize() assigns semaphore
increments only to ops somebody waits on, dedupes waits with per-engine
high-water marks, and emits the five engine streams inside one nc.Block().

Rules encoded:
  - cross-engine dep  -> wait_ge(sem[producer.eng], producer.inc_id)
  - same-engine dep   -> producer has inc? wait own sem : drain()
  - DMA ops go on named CHANNELS, one semaphore each (+16 per transfer).
    A wait for channel-DMA d uses value 16 * (number of channel DMAs created
    before the consumer). SOUNDNESS CONTRACT (caller's responsibility): any
    channel DMA created AFTER a consumer must be dep-gated on that consumer
    or a later op — use parity channels for ping-pong streams so waits are
    exact. The race detector in the simulator verifies this at runtime.
  - SWDGE (G) and HWDGE (Y/S) DMAs never share a channel (enforced by the
    one-engine-per-channel assert).
"""
import concourse.bass as bass

ENGINES = ("T", "V", "S", "G", "Y")
DMA_QUEUES = ("G", "Y", "S", "V")


class _Op:
    __slots__ = ("eng", "emit", "deps", "is_dma", "seq", "idx", "inc_id",
                 "ch", "ch_idx", "tag")

    def __init__(self, eng, emit, deps, is_dma, seq, idx, tag):
        self.eng = eng
        self.emit = emit
        self.deps = deps
        self.is_dma = is_dma
        self.seq = seq
        self.idx = idx
        self.inc_id = None
        self.ch = None
        self.ch_idx = None
        self.tag = tag


class Builder:
    def __init__(self, nc: bass.Bass):
        self.nc = nc
        self.ops = {e: [] for e in ENGINES}
        self._all = []
        self.ch_count = {}
        self.ch_eng = {}
        self.ch_seqs = {}

    def op(self, eng, emit, deps=(), tag=""):
        o = _Op(eng, emit, [d for d in deps if d is not None], False,
                len(self._all), len(self.ops[eng]), tag)
        self.ops[eng].append(o)
        self._all.append(o)
        return o

    def dma(self, eng, emit, deps=(), ch=None, look=0, seg=None, tag=""):
        assert eng in DMA_QUEUES
        ch = ch or ("q" + eng)
        if ch in self.ch_eng:
            assert self.ch_eng[ch] == eng, f"channel {ch} on two engines"
        else:
            self.ch_eng[ch] = eng
            self.ch_count[ch] = 0
            self.ch_seqs[ch] = []
            self.ch_look = getattr(self, "ch_look", {})
            self.ch_look[ch] = 0
            self.ch_segs = getattr(self, "ch_segs", {})
            self.ch_segs[ch] = []
        self.ch_look[ch] = max(self.ch_look[ch], look)
        self.ch_segs[ch].append(seg)
        o = _Op(eng, emit, [d for d in deps if d is not None], True,
                len(self._all), len(self.ops[eng]), tag)
        self.ch_count[ch] += 1
        o.ch = ch
        o.ch_idx = self.ch_count[ch]
        self.ch_seqs[ch].append(o.seq)
        self.ops[eng].append(o)
        self._all.append(o)
        return o

    def _wait_val(self, d, consumer):
        import bisect
        n = bisect.bisect_left(self.ch_seqs[d.ch], consumer.seq)
        assert n >= d.ch_idx, (d.tag, consumer.tag)
        look = self.ch_look.get(d.ch, 0)
        if look and n > 0:
            segs = self.ch_segs[d.ch]
            s = segs[n - 1]
            cap = n
            while cap < len(segs) and segs[cap] == s:
                cap += 1
            n = min(n + look, cap)
        else:
            n = min(n + look, self.ch_count[d.ch])
        return 16 * n

    def check_deadlock(self):
        done = set()
        ptr = {e: 0 for e in ENGINES}
        ch_done = {c: 0 for c in self.ch_count}
        progress = True
        while progress:
            progress = False
            for e in ENGINES:
                while ptr[e] < len(self.ops[e]):
                    o = self.ops[e][ptr[e]]
                    deps_ok = True
                    for d in o.deps:
                        if d.is_dma:
                            need = self._wait_val(d, o) // 16
                            if ch_done[d.ch] < need:
                                deps_ok = False
                        elif id(d) not in done:
                            deps_ok = False
                        if not deps_ok:
                            break
                    if not deps_ok:
                        break
                    done.add(id(o))
                    if o.is_dma:
                        ch_done[o.ch] = max(ch_done[o.ch], o.ch_idx)
                    ptr[e] += 1
                    progress = True
        stuck = {e: ptr[e] for e in ENGINES if ptr[e] < len(self.ops[e])}
        if stuck:
            msgs = []
            for e, i in stuck.items():
                o = self.ops[e][i]
                blocked = [f"{d.eng}:{d.tag}" for d in o.deps if id(d) not in done]
                msgs.append(f"engine {e} stuck at #{i} {o.tag!r} waiting {blocked}")
            raise RuntimeError("DEADLOCK in builder schedule:\n" + "\n".join(msgs))

    def finalize(self):
        self.check_deadlock()
        nc = self.nc

        need_inc = set()
        for o in self._all:
            for d in o.deps:
                if not d.is_dma:
                    need_inc.add(id(d))
        inc_counter = {e: 0 for e in ENGINES}
        for e in ENGINES:
            for o in self.ops[e]:
                if id(o) in need_inc:
                    inc_counter[e] += 1
                    o.inc_id = inc_counter[e]

        sems, ctxs = {}, []
        for e in ENGINES:
            if inc_counter[e] > 0:
                cm = nc.semaphore(f"sem_{e}")
                sems[e] = cm.__enter__()
                ctxs.append(cm)
        csems = {}
        for c in self.ch_count:
            cm = nc.semaphore(f"dma_{c}")
            csems[c] = cm.__enter__()
            ctxs.append(cm)

        blk_cm = nc.Block()
        block = blk_cm.__enter__()

        def run_engine(e, engine):
            hw_eng = {x: 0 for x in ENGINES}
            hw_ch = {c: 0 for c in self.ch_count}
            flushed = 0
            for o in self.ops[e]:
                for d in o.deps:
                    if d.is_dma:
                        tgt = self._wait_val(d, o)
                        if hw_ch[d.ch] < tgt:
                            engine.wait_ge(csems[d.ch], tgt)
                            hw_ch[d.ch] = tgt
                    elif d.eng == e:
                        if d.idx >= flushed:
                            if d.inc_id is not None:
                                if hw_eng[e] < d.inc_id:
                                    engine.wait_ge(sems[e], d.inc_id)
                                    hw_eng[e] = d.inc_id
                                flushed = max(flushed, d.idx + 1)
                            else:
                                engine.drain()
                                flushed = o.idx
                    else:
                        if d.inc_id is None:
                            raise RuntimeError(f"dep of {o.tag!r} on {d.tag!r} has no inc (bug)")
                        if hw_eng[d.eng] < d.inc_id:
                            engine.wait_ge(sems[d.eng], d.inc_id)
                            hw_eng[d.eng] = d.inc_id
                with self.nc.allow_low_precision("deliberate bf16 operand casts"):
                    ins = o.emit(engine)
                if o.is_dma:
                    ins.then_inc(csems[o.ch], 16)
                elif o.inc_id is not None:
                    ins.then_inc(sems[e], 1)

        if self.ops["G"]:
            @block.gpsimd
            def _(g):
                run_engine("G", g)
        if self.ops["T"]:
            @block.tensor
            def _(t):
                run_engine("T", t)
        if self.ops["S"]:
            @block.scalar
            def _(s):
                run_engine("S", s)
        if self.ops["V"]:
            @block.vector
            def _(v):
                run_engine("V", v)

        @block.sync
        def _(sy):
            run_engine("Y", sy)
            for c, n in self.ch_count.items():
                if n > 0:
                    sy.wait_ge(csems[c], 16 * n)

        blk_cm.__exit__(None, None, None)
        for cm in ctxs:
            cm.__exit__(None, None, None)
        return inc_counter, dict(self.ch_count)


"""Megakernel builder for mk_prb_d1 (GLA + 6-layer parallel-block transformer).

One raw-Bass program per core; core c owns batch element c (T=2048 tokens).
Feature-major residual stream y [1024, T] f32; bf16 matmul operands; weights
streamed from HBM with f32->bf16 cast DMAs (gpsimd).

Build: build_program(n_layers=0..6, do_gla=bool) -> (nc, meta)
"""
import math
import numpy as np
import ml_dtypes
import concourse.bass as bass
import concourse.mybir as mybir

F32, BF16 = mybir.dt.float32, mybir.dt.bfloat16
AF = mybir.ActivationFunctionType
ALU = mybir.AluOpType
AX = mybir.AxisListType
bf = ml_dtypes.bfloat16

T = 2048
D = 1024
NCH = 8
NT4 = 4
NTT = 16
NW = 8
HEADS = 16
DHEAD = 64
INNER = 1024
DI2 = 5460
DI = 2730
NA = 22
GH = 4
GDK = 128
GDV = 256
GC = 64
GNC = 32


def _round32(x):
    return (x + 31) & ~31


def host_tables():
    n = T
    half = np.arange(0, DHEAD, 2, dtype=np.float64)
    inv_freq = 1.0 / 10000.0 ** (half / DHEAD)
    t = np.arange(n, dtype=np.float64)
    freqs = t[:, None] * inv_freq[None, :]
    freqs = np.concatenate([freqs, freqs], -1)
    base_scale = (half + 0.4 * DHEAD) / (1.4 * DHEAD)
    power = (t - n // 2) / (256 // 2)
    scale = base_scale[None, :] ** power[:, None]
    scale = np.concatenate([scale, scale], -1)
    cos, sin = np.cos(freqs), np.sin(freqs)
    Cq = (cos * scale).T
    Sq = (sin * scale).T
    Ck = (cos / scale).T
    Sk = (sin / scale).T
    out = np.stack([np.concatenate([a, a], 0) for a in (Cq, Sq, Ck, Sk)])
    return out.astype(np.float32)


def host_consts():
    c = {}
    c["ident_f"] = np.eye(128, dtype=np.float32)
    c["ident_b"] = np.eye(128, dtype=bf)
    pr = np.zeros((128, 128), dtype=np.float32)
    for blk in (0, 64):
        for r in range(32):
            pr[blk + r + 32, blk + r] = -1.0
            pr[blk + r, blk + r + 32] = 1.0
    c["permrot"] = pr.astype(bf)
    j = np.arange(128)[:, None]
    i = np.arange(128)[None, :]
    c["mask_lo"] = (j >= i).astype(bf)
    c["mask_hi"] = (i >= j).astype(bf)
    for nm, valid in (("lo", j >= i), ("hi", i >= j)):
        addmT = np.where(valid, 0.0, -30.0).T.astype(np.float32)
        c[f"maskA_{nm}"] = np.tile(addmT[0:64], (2, 1)).astype(bf)
        c[f"maskB_{nm}"] = np.tile(addmT[64:128], (2, 1)).astype(bf)
    idd = np.zeros((128, 256), dtype=bf)
    for p in range(128):
        idd[p, p + 64] = 1
    c["identD"] = idd
    c["ones_col"] = np.ones((128, 1), dtype=bf)
    c["ones_rows"] = np.ones((128, 128), dtype=bf)
    blkA = np.zeros((128, 2), dtype=bf)
    blkA[0:64, 0] = 1
    blkA[64:128, 1] = 1
    c["blkA"] = blkA
    blk2 = np.zeros((2, 128), dtype=bf)
    blk2[0, 0:64] = 1
    blk2[1, 64:128] = 1
    c["blk2"] = blk2
    m = np.ones((128, T), dtype=np.float32)
    m[:, ::GC] = 0.0
    c["scanmask"] = m
    c["tables"] = host_tables()
    return c


def pack_params(n_layers, do_gla, inp):
    cols = []
    for l in range(n_layers):
        cols.append(np.asarray(inp["L0_ffn_ln_g"][l]).reshape(8, 128).T)
        cols.append(np.asarray(inp["L0_ffn_ln_b"][l]).reshape(8, 128).T)
        cols.append(np.asarray(inp["L0_ffn_b2"][l]).reshape(8, 128).T)
        b1 = np.asarray(inp["L0_ffn_b1"][l])
        a = np.zeros((NA * 128,), np.float32); a[:DI] = b1[:DI]
        g = np.zeros((NA * 128,), np.float32); g[:DI] = b1[DI:]
        cols.append(a.reshape(NA, 128).T)
        cols.append(g.reshape(NA, 128).T)
        cols.append(np.tile(np.asarray(inp["L0_ffn_q_scale"][l]), 2)[:, None])
        cols.append(np.tile(np.asarray(inp["L0_ffn_k_scale"][l]), 2)[:, None])
    if do_gla:
        cols.append(np.asarray(inp["L0_attn_gnw"]).reshape(2, 128).T)
        cols.append(np.asarray(inp["L0_attn_bgk2"]).reshape(4, 128).T)
    if not cols:
        return None
    return np.ascontiguousarray(np.concatenate(cols, 1).astype(np.float32))


class MK:
    def __init__(self, nc, b, n_layers, do_gla):
        self.nc = nc
        self.b = b
        self.n_layers = n_layers
        self.do_gla = do_gla
        self.sb = {}
        self.ps = []
        self.dram = {}
        self.const = {}
        self.tok = {}

    def alloc(self):
        nc = self.nc
        arena = nc.alloc_sbuf_tensor("ARENA", [128, 212800], mybir.dt.uint8)
        base = nc.lookup_mloc(arena).addr
        self.base = base
        off = 0

        def at(name, shape, dt, o):
            h = nc.alloc_sbuf_tensor_at(name, shape, dt, offset=base + o)
            self.sb[name] = h
            return h

        self.y = [at(f"y{d}", [128, T], F32, off + d * T * 4) for d in range(NCH)]
        off += NCH * T * 4
        self.ybf_off = off
        stride = _round32((T + 1) * 2)
        self.ybf = [at(f"ybf{d}", [128, T + 1], BF16, off + d * stride) for d in range(NCH)]
        off += NCH * stride
        self.aqk_off = off
        self.aqk = [at(f"aqk{d}", [128, T], BF16, off + d * T * 2) for d in range(NCH)]
        off += NCH * T * 2
        self.av_off = off
        self.vaug = [at(f"vaug{t}", [128, 520], BF16, off + t * 1056) for t in range(NTT)]
        self.gv = [at(f"gv{d}", [128, T], BF16, off + d * T * 2) for d in range(NCH)]
        self.u = [at(f"u{i}", [128, 512], BF16, off + i * 1024) for i in range(NA)]
        self.xstg = [at(f"xstg{i}", [128, D], F32, off + i * 4096) for i in range(2)]
        self.sqstg = [at(f"sqstg{i}", [128, D], F32, off + 8192 + i * 4096) for i in range(2)]
        self.xnstg = [at(f"xnstg{i}", [128, D], F32, off + 16384 + i * 4096) for i in range(2)]
        self.ostg = [at(f"ostg{i}", [128, D], F32, off + 24576 + i * 4096) for i in range(2)]
        self.glstg = [at(f"glstg{i}", [128, T], F32, off + i * 8192) for i in range(2)]
        off += NTT * 2080
        self.u2 = [at(f"u2_{i}", [128, 512], BF16, self.ybf_off + i * 1024) for i in range(NA)]
        self.tab_off = off
        self.tabC = at("tabC", [128, T], BF16, off)
        self.tabS = at("tabS", [128, T], BF16, off + T * 2)
        self.scanmask = at("scanmask_sb", [128, T], F32, off)
        self.wo_big = at("wo_big", [128, 4096], BF16, off)
        self.wo_row = [self.wo_big[:, k * 1024:(k + 1) * 1024] for k in range(4)]
        self.wo_t = [[self.wo_big[:, (k * 8 + m) * 128:(k * 8 + m) * 128 + 128]
                      for m in range(8)] for k in range(4)]
        off += 2 * T * 2
        self.wst_off = off
        self.wstbig = at("wstbig", [128, 8192], BF16, off)
        self.wst = [[self.wstbig[:, (hb * 8 + k) * 512:(hb * 8 + k) * 512 + 512]
                     for k in range(8)] for hb in range(2)]
        self.t16 = at("t16", [16, T], BF16, off + 8192)
        self.w16 = at("w16", [128, 128], BF16, off + 8192 + 4096)
        self.wgk2 = at("wgk2", [16, 512], BF16, off + 8192 + 4096 + 256)
        off += 16 * 1024
        self.ao_off = off
        self.ow = [[at(f"ow{pp}_{j}", [128, 256], BF16, off + (pp * 4 + j) * 512)
                    for j in range(4)] for pp in range(2)]
        self.gG = at("gG", [128, T], F32, off)
        self.gost = [[at(f"gost{pp}_{j}", [128, 256], BF16, off + (pp * 8 + j) * 512)
                      for j in range(NCH)] for pp in range(2)]
        self.w2ring = at("w2ring", [128, 4096], BF16, off)
        off += 8192
        self.scr_off = off
        s = off
        self.qsq = [at(f"qsq{j}", [128, 512], BF16, s + j * 1024) for j in range(2)]
        self.sdq = at("sdq", [2, 512], F32, s + 2048)
        self.rqb = at("rqb", [2, 512], BF16, s + 4096)
        self.qn = [at(f"qn{j}", [128, 512], BF16, s + 5120 + j * 1024) for j in range(2)]
        self.t1 = at("t1", [128, 512], BF16, s + 7168)
        self.expS = [at(f"expS{i}", [128, 384], BF16, s + i * 768) for i in range(4)]
        self.lrow = [at("lrow0", [128, 256], BF16, s + 5120),
                     at("lrow1", [128, 256], BF16, s + 6656)]
        self.rbs = [at("rbs0", [64, 256], F32, s + 5632),
                    at("rbs1", [64, 256], F32, s + 7168)]
        self.sqb = at("sqb", [128, 512], BF16, self.ao_off + 4096)
        self.sqb2 = at("sqb2", [128, 512], BF16, self.ao_off + 5120)
        self.stt = at("sttbuf", [128, 512], F32, self.ao_off + 6144)
        self.mr = at("mr", [1, 1024], BF16, s + 3072)
        self.gg = at("gg", [128, 512], F32, s + 0)
        self.gg2 = at("gg2", [128, 512], F32, s + 2048)
        self.gsp = at("gsp", [128, T], F32, s + 0)
        self.e512a = at("e512a", [128, 512], F32, s + 0)
        self.e512b = at("e512b", [128, 512], F32, s + 2048)
        self.kgs = at("kgs", [128, 64], BF16, s + 0)
        self.kgt = at("kgt", [64, 128], BF16, s + 256)
        self.vt = at("vt", [64, 256], BF16, s + 512)
        self.at_sb = at("at_sb", [64, 64], BF16, s + 1024)
        self.Sst = [at(f"Sst{h}", [128, GDV], F32, s + 2048 + h * GDV * 4) for h in range(GH)]
        self.Sbf = [at(f"Sbf{h}", [128, GDV], BF16, s + 6144 + h * GDV * 2) for h in range(GH)]
        self.gsil = at("gsil", [128, 512], F32, s + 0)
        self.gnt = at("gnt", [128, 512], F32, s + 2048)
        self.outA = at("outA", [128, 512], F32, s + 4096)
        self.outB = at("outB", [128, 512], F32, s + 6144)
        self.osq = at("osq", [128, 512], BF16, self.tab_off + 4096)
        self.rqg = at("rqg", [1, 512], BF16, self.tab_off + 5120)
        off += 8192
        self.par_off = off
        p = off
        self.cpool = at("cpool", [128, 8], F32, p); p += 32
        self.stat = at("stat", [128, 8], F32, p); p += 32
        self.onesb = at("onesb", [128, 1], BF16, p); p += 32
        self.onesr = at("onesr", [128, 128], BF16, p); p += 256
        self.blkA = at("blkA_sb", [128, 2], BF16, p); p += 32
        self.blk2 = at("blk2_sb", [2, 128], BF16, p); p += 256
        self.permrot = at("permrot_sb", [128, 128], BF16, p); p += 256
        self.mask_lo = at("mask_lo_sb", [128, 128], BF16, p); p += 256
        self.mask_hi = at("mask_hi_sb", [128, 128], BF16, p); p += 256
        self.maskA = {}
        self.maskB = {}
        for nm in ("lo", "hi"):
            self.maskA[nm] = at(f"maskA_{nm}_sb", [128, 128], BF16, p); p += 256
            self.maskB[nm] = at(f"maskB_{nm}_sb", [128, 128], BF16, p); p += 256
        self.identD = at("identD_sb", [128, 256], BF16, p); p += 512
        self.ident_f = at("ident_f_sb", [128, 128], F32, p); p += 512
        self.ident_b = at("ident_b_sb", [128, 128], BF16, p); p += 256
        self.glast = at("glast", [128, 128], F32, p); p += 512
        L = max(self.n_layers, 0)
        self.npar = 70 * L + (6 if self.do_gla else 0)
        if self.npar:
            self.par = at("par", [128, self.npar], F32, p)
            p += _round32(self.npar * 4)
            if self.do_gla:
                gb = 70 * L
                self.gnwv = self.par[:, gb:gb + 2]
                self.bgk2s = self.par[:, gb + 2:gb + 6]
                self.bgk2n = at("bgk2n", [128, 4], F32, p)
                p += 32
        self.par_end = p
        total = p
        assert total <= 212800, total
        self.total = total

        self.psum = [self.nc.alloc_psum_tensor(f"PS{i}", [128, 512], F32) for i in range(8)]

    def pc_lng(self, l):
        return self.par[:, 70 * l + 0:70 * l + 8]

    def pc_lnb(self, l):
        return self.par[:, 70 * l + 8:70 * l + 16]

    def pc_b2(self, l):
        return self.par[:, 70 * l + 16:70 * l + 24]

    def pc_b1a(self, l):
        return self.par[:, 70 * l + 24:70 * l + 46]

    def pc_b1g(self, l):
        return self.par[:, 70 * l + 46:70 * l + 68]

    def pc_qs(self, l):
        return self.par[:, 70 * l + 68:70 * l + 69]

    def pc_ks(self, l):
        return self.par[:, 70 * l + 69:70 * l + 70]

    def scr_bar(self):
        mark = self.state.get("scr_mark", 0)
        bar = [o for o in self.b._all[mark:] if not o.is_dma]
        bar += self.state.pop("scr_dma_extra", [])
        self.state["scr_mark"] = len(self.b._all)
        return bar

    def load_consts(self, c):
        nc, b = self.nc, self.b
        self.inl = {}
        names = ["ident_f", "ident_b", "permrot", "mask_lo", "mask_hi", "ones_col",
                 "ones_rows", "blkA", "blk2", "maskA_lo", "maskB_lo", "maskA_hi",
                 "maskB_hi", "identD"] + (["scanmask"] if self.do_gla else [])
        for name in names:
            self.inl[name] = nc.inline_tensor(np.asarray(c[name]), name="c_" + name)
        self.inl["tables"] = nc.inline_tensor(c["tables"], name="c_tables")
        t = []
        t.append(b.dma("G", lambda g: g.dma_start(self.ident_f[:], self.inl["ident_f"][:, :]), ch="cst", tag="c_idf"))
        t.append(b.dma("G", lambda g: g.dma_start(self.ident_b[:], self.inl["ident_b"][:, :]), ch="cst", tag="c_idb"))
        t.append(b.dma("G", lambda g: g.dma_start(self.permrot[:], self.inl["permrot"][:, :]), ch="cst", tag="c_pr"))
        t.append(b.dma("G", lambda g: g.dma_start(self.mask_lo[:], self.inl["mask_lo"][:, :]), ch="cst", tag="c_ml"))
        t.append(b.dma("G", lambda g: g.dma_start(self.mask_hi[:], self.inl["mask_hi"][:, :]), ch="cst", tag="c_mh"))
        for nm in ("lo", "hi"):
            t.append(b.dma("G", (lambda g, nm=nm: g.dma_start(self.maskA[nm][:], self.inl[f"maskA_{nm}"][:, :])), ch="cst", tag=f"c_mA{nm}"))
            t.append(b.dma("G", (lambda g, nm=nm: g.dma_start(self.maskB[nm][:], self.inl[f"maskB_{nm}"][:, :])), ch="cst", tag=f"c_mB{nm}"))
        t.append(b.dma("G", lambda g: g.dma_start(self.identD[:], self.inl["identD"][:, :]), ch="cst", tag="c_idD"))
        t.append(b.dma("G", lambda g: g.dma_start(self.onesb[:], self.inl["ones_col"][:, :]), ch="cst", tag="c_oc"))
        t.append(b.dma("G", lambda g: g.dma_start(self.onesr[:], self.inl["ones_rows"][:, :]), ch="cst", tag="c_or"))
        t.append(b.dma("G", lambda g: g.dma_start(self.blkA[:], self.inl["blkA"][:, :]), ch="cst", tag="c_ba"))
        t.append(b.dma("G", lambda g: g.dma_start(self.blk2[:], self.inl["blk2"][:, :]), ch="cst", tag="c_b2"))
        if self.do_gla:
            t.append(b.dma("G", lambda g: g.dma_start(self.scanmask[:], self.inl["scanmask"][:, :]), ch="cst", tag="c_sm"))
        self.tok["consts"] = t
        cp = self.cpool
        vals = [1.0, 1e-5, 1e-24, math.log(GDK ** -0.5), 1e-5, 0.0, 0.0, 0.0]
        cps = []
        for i, v in enumerate(vals):
            cps.append(b.op("V", (lambda vv, i=i, v=v: vv.memset(cp[:, i:i + 1], v)), tag=f"cp{i}"))
        self.tok["cpool"] = cps
        if self.npar:
            tp = b.dma("G", lambda g: g.dma_start(self.par[:], self.dram["params"][:, :]),
                       ch="cst", tag="c_par")
            self.tok["par"] = [tp]
            if self.do_gla:
                self.tok["par"].append(
                    b.op("V", (lambda v: v.tensor_scalar(self.bgk2n[:], self.bgk2s, -1.0, None, ALU.mult)),
                         deps=[tp], tag="negb"))

    def swap_tables(self, which, deps):
        b = self.b
        t0 = b.dma("G", (lambda g, w=which: g.dma_start(self.tabC[:], self.inl["tables"][2 * w, :, :])),
                   deps=deps, ch="tab", tag=f"tabC{which}")
        t1 = b.dma("G", (lambda g, w=which: g.dma_start(self.tabS[:], self.inl["tables"][2 * w + 1, :, :])),
                   deps=deps, ch="tab", tag=f"tabS{which}")
        return [t0, t1]

    def emit_p0(self):
        import os
        b = self.b
        x = self.dram["x"]
        lvl = int(os.environ.get("MK_P0_LVL", "9"))
        ev_last = {}
        xn_done, sq_done, s2_done = {}, {}, {}
        self.tok["p0_ybf"] = []
        for tt in range(NTT):
            st = tt % 2
            so = 8 * st
            so = 4 * st
            dep = []
            if tt - 2 in xn_done:
                dep = [xn_done[tt - 2], sq_done[tt - 2]]
            ld = b.dma("Y", (lambda sy, tt=tt, st=st:
                             sy.dma_start(self.xstg[st][:], x[128 * tt:128 * (tt + 1), :])),
                       deps=dep, ch=f"xi{tt % 2}", tag=f"p0x{tt}")
            if lvl < 2:
                xn_done[tt] = b.op("V", (lambda v, st=st, tt=tt: v.tensor_copy(
                    self.y[tt % NCH][:, 0:D] if False else self.xnstg[st][:], self.xstg[st][:])),
                    deps=[ld] + ([xn_done[tt - 2]] if tt - 2 in xn_done else []), tag=f"p0cp{tt}")
                sq_done[tt] = xn_done[tt]
                s2_done[tt] = xn_done[tt]
                if lvl < 1:
                    self.tok["p0_ybf"].append(xn_done[tt])
                    continue
                t_xn = xn_done[tt]
            if lvl >= 2:
                t_sq = b.op("S", (lambda s, st=st: s.activation(self.sqstg[st][:], self.xstg[st][:], AF.Square)),
                            deps=[ld] + ([s2_done[tt - 2]] if tt - 2 in s2_done else []), tag=f"p0sq{tt}")
                sq_done[tt] = t_sq
                prev = [xn_done[tt - 2]] if tt - 2 in xn_done else []
                t_mu = b.op("V", (lambda v, st=st, so=so: v.reduce_sum(self.stat[:, so:so + 1],
                                                                       self.xstg[st][:], AX.X)),
                            deps=[ld] + prev, tag=f"p0mu{tt}")
                t_s2 = b.op("V", (lambda v, st=st, so=so: v.reduce_sum(self.stat[:, so + 1:so + 2],
                                                                       self.sqstg[st][:], AX.X)),
                            deps=[t_sq] + prev, tag=f"p0s2{tt}")
                s2_done[tt] = t_s2
                t_st = b.op("V", (lambda v, so=so: v.tensor_scalar(self.stat[:, so:so + 1],
                                                                   self.stat[:, so:so + 1], 1.0 / D, None, ALU.mult)),
                            deps=[t_mu], tag=f"p0mun{tt}")
                t_v2 = b.op("V", (lambda v, so=so: v.tensor_scalar(self.stat[:, so + 1:so + 2],
                                                                   self.stat[:, so + 1:so + 2], 1.0 / D, None, ALU.mult)),
                            deps=[t_s2], tag=f"p0e2{tt}")
                t_v1 = b.op("V", (lambda v, so=so: v.tensor_mul(self.stat[:, so + 2:so + 3],
                                                                self.stat[:, so:so + 1], self.stat[:, so:so + 1])),
                            deps=[t_st], tag=f"p0msq{tt}")
                t_var = b.op("V", (lambda v, so=so: v.tensor_sub(self.stat[:, so + 1:so + 2],
                                                                 self.stat[:, so + 1:so + 2],
                                                                 self.stat[:, so + 2:so + 3])),
                             deps=[t_v1, t_v2], tag=f"p0var{tt}")
                t_sd = b.op("S", (lambda s, so=so: s.activation(self.stat[:, so + 2:so + 3],
                                                                self.stat[:, so + 1:so + 2], AF.Sqrt,
                                                                bias=self.cpool[:, 1:2])),
                            deps=[t_var] + self.tok["cpool"], tag=f"p0sd{tt}")
                t_rs = b.op("V", (lambda v, so=so: v.reciprocal(self.stat[:, so + 3:so + 4],
                                                                self.stat[:, so + 2:so + 3])),
                            deps=[t_sd], tag=f"p0rs{tt}")
                t_xn = b.op("V", (lambda v, st=st, so=so: v.tensor_scalar(self.xnstg[st][:], self.xstg[st][:],
                                                                          self.stat[:, so:so + 1],
                                                                          self.stat[:, so + 3:so + 4],
                                                                          ALU.subtract, ALU.mult)),
                            deps=[t_rs] + ([xn_done[tt - 2]] if tt - 2 in xn_done else []), tag=f"p0xn{tt}")
                xn_done[tt] = t_xn
            if lvl < 3:
                self.tok["p0_ybf"].append(xn_done[tt])
                continue
            bk = (tt % 2) * 2
            for half in range(2):
                bank = bk + half
                def tr(t, st=st, half=half, bank=bank):
                    r = None
                    for q in range(4):
                        d = half * 4 + q
                        r = t.transpose(self.psum[bank][0:128, 128 * q:128 * (q + 1)],
                                        self.xnstg[st][:, 128 * d:128 * (d + 1)],
                                        self.ident_f[:, :])
                    return r
                deps = [t_xn, self.tok["consts"][0]]
                key = (tt - 2, half)
                if key in ev_last:
                    deps += ev_last[key]
                t_tr = b.op("T", tr, deps=deps, tag=f"p0tr{tt}_{half}")
                evs = []
                eng = "V" if half == 0 else "S"
                for q in range(4):
                    d = half * 4 + q
                    if eng == "V":
                        e1 = b.op("V", (lambda v, d=d, bank=bank, q=q, tt=tt:
                                        v.tensor_copy(self.y[d][:, 128 * tt:128 * (tt + 1)],
                                                      self.psum[bank][0:128, 128 * q:128 * (q + 1)])),
                                  deps=[t_tr], tag=f"p0ey{tt}_{d}")
                        e2 = b.op("V", (lambda v, d=d, bank=bank, q=q, tt=tt:
                                        v.tensor_copy(self.ybf[d][:, 1 + 128 * tt:1 + 128 * (tt + 1)],
                                                      self.psum[bank][0:128, 128 * q:128 * (q + 1)])),
                                  deps=[t_tr], tag=f"p0eb{tt}_{d}")
                    else:
                        e1 = b.op("S", (lambda s, d=d, bank=bank, q=q, tt=tt:
                                        s.copy(self.y[d][:, 128 * tt:128 * (tt + 1)],
                                               self.psum[bank][0:128, 128 * q:128 * (q + 1)])),
                                  deps=[t_tr], tag=f"p0ey{tt}_{d}")
                        e2 = b.op("S", (lambda s, d=d, bank=bank, q=q, tt=tt:
                                        s.copy(self.ybf[d][:, 1 + 128 * tt:1 + 128 * (tt + 1)],
                                               self.psum[bank][0:128, 128 * q:128 * (q + 1)])),
                                  deps=[t_tr], tag=f"p0eb{tt}_{d}")
                    evs += [e1, e2]
                ev_last[(tt, half)] = evs
                self.tok["p0_ybf"] += evs
        for d in range(NCH):
            self.tok["p0_ybf"].append(
                b.op("V", (lambda v, d=d: v.memset(self.ybf[d][:, 0:1], 0.0)), tag=f"padz{d}"))

    def emit_p3(self, y_ready, gla_ready):
        b = self.b
        x = self.dram["x"]
        out = self.dram["out"]
        add_tok = []
        if self.do_gla:
            gld = {}
            for d in range(NCH):
                dep = [add_tok[d - 2]] if d - 2 >= 0 else []
                ld = b.dma("Y", (lambda sy, d=d: sy.dma_start(self.glstg[d % 2][:],
                                                              self.dram["gla_out"][128 * d:128 * (d + 1), :])),
                           deps=gla_ready + dep + (y_ready if d < 2 else []),
                           ch=f"gi{d % 2}", tag=f"p3g{d}")
                a = b.op("V", (lambda v, d=d: v.tensor_add(self.y[d][:], self.y[d][:], self.glstg[d % 2][:])),
                         deps=[ld] + y_ready, tag=f"p3a{d}")
                add_tok.append(a)
        else:
            add_tok = y_ready
        ev_prev = {}
        st_tok = {}
        ld2 = {}
        for tt in range(NTT):
            st = tt % 2
            dep = list(ev_prev.get((tt - 2, 0), [])) + list(ev_prev.get((tt - 2, 1), []))
            ld2[tt] = b.dma("Y", (lambda sy, tt=tt, st=st:
                                  sy.dma_start(self.xstg[st][:], x[128 * tt:128 * (tt + 1), :])),
                            deps=dep + (list(add_tok) if tt < 2 else []), ch=f"xi{tt % 2}", tag=f"p3x{tt}")
            bk = (tt % 2) * 2
            evs_all = []
            for half in range(2):
                bank = bk + half
                def tr(t, tt=tt, half=half, bank=bank):
                    r = None
                    for q in range(4):
                        d = half * 4 + q
                        r = t.transpose(self.psum[bank][0:128, 128 * q:128 * (q + 1)],
                                        self.y[d][:, 128 * tt:128 * (tt + 1)],
                                        self.ident_f[:, :])
                    return r
                deps = list(add_tok) + [self.tok["consts"][0]]
                if (tt - 2, half) in ev_prev:
                    deps += ev_prev[(tt - 2, half)]
                t_tr = b.op("T", tr, deps=deps, tag=f"p3tr{tt}_{half}")
                evs = []
                for q in range(4):
                    d = half * 4 + q
                    edep = [t_tr, ld2[tt]]
                    if tt - 2 in st_tok:
                        edep.append(st_tok[tt - 2])
                    e = b.op("V", (lambda v, d=d, bank=bank, q=q, st=st:
                                   v.tensor_add(self.ostg[st][:, 128 * d:128 * (d + 1)],
                                                self.psum[bank][0:128, 128 * q:128 * (q + 1)],
                                                self.xstg[st][:, 128 * d:128 * (d + 1)])),
                             deps=edep, tag=f"p3e{tt}_{d}")
                    evs.append(e)
                ev_prev[(tt, half)] = evs
                evs_all += evs
            st_tok[tt] = b.dma("Y", (lambda sy, tt=tt, st=st:
                                     sy.dma_start(out[128 * tt:128 * (tt + 1), :], self.ostg[st][:])),
                               deps=evs_all, ch=f"ou{tt % 2}", tag=f"p3o{tt}")


def build_program(n_layers=0, do_gla=False):
    nc = bass.Bass("TRN2", target_bir_lowering=False)
    b = Builder(nc)
    mk = MK(nc, b, n_layers, do_gla)
    mk.dram["x"] = nc.dram_tensor("x", [T, D], F32, kind="ExternalInput")
    mk.dram["out"] = nc.dram_tensor("out", [T, D], F32, kind="ExternalOutput")
    if do_gla:
        mk.dram["gla_o"] = nc.dram_tensor("gla_o", [D, T], BF16, kind="Internal")
        mk.dram["gla_out"] = nc.dram_tensor("gla_out", [D, T], F32, kind="Internal")
        for nm, shp in [("gWq", [D, 512]), ("gWk", [D, 512]), ("gWv", [D, D]), ("gWg", [D, D]),
                        ("gWgk1", [D, 16]), ("gWgk2", [16, 512]), ("gWo", [D, D])]:
            mk.dram[nm] = nc.dram_tensor(nm, shp, F32, kind="ExternalInput")
    if n_layers > 0:
        for nm, shp in [("Wqkv", [n_layers, D, 3072]), ("Wo", [n_layers, D, D]),
                        ("W1", [n_layers, D, DI2]), ("W2", [n_layers, DI, D])]:
            mk.dram[nm] = nc.dram_tensor(nm, shp, F32, kind="ExternalInput")
    mk.alloc()
    if mk.npar:
        mk.dram["params"] = nc.dram_tensor("params", [128, mk.npar], F32, kind="ExternalInput")
    mk.load_consts(host_consts())
    mk.state = {
        "ps_free": {}, "tab_readers": [], "ybf_readers": [], "u2_writers": [],
        "aqk_free": [], "vaug_free": [], "wst_free": {0: [], 1: []},
        "wq_bi": 0, "attn_step": 0, "w1_slot": 0, "w2_slot": 0,
        "w1_free": {}, "w2_free": {}, "u_free": {},
        "expS_free": {0: [], 1: [], 2: [], 3: []},
        "ao_free": {0: [], 1: []},
        "scr_qsq": {0: [], 1: []}, "scr_sdq": [], "scr_rqb": [],
        "scr_qn": {0: [], 1: []}, "scr_t1": [],
        "scr_lrow": {0: [], 1: []}, "scr_rbs": {0: [], 1: []},
        "scr_sqb": {0: [], 1: []}, "scr_stt": [], "scr_mr": [], "scr_gg": {0: [], 1: []},
        "scr_gsp": [], "scr_gG": [], "scr_glast": [], "scr_e": {0: [], 1: []},
        "scr_sst": [], "scr_kgs": [], "scr_kgt": [], "scr_vt": [], "scr_at": [],
        "scr_mark": 0,
    }
    mk.emit_p0()
    y_ready = list(mk.tok["p0_ybf"])
    mk.state["ybf_ready"] = y_ready
    import os
    if os.environ.get("MK_P0_ONLY"):
        out2 = nc.dram_tensor("out_fm", [D, T], F32, kind="ExternalOutput")
        for d in range(NCH):
            b.dma("Y", (lambda sy, d=d: sy.dma_start(out2[128 * d:128 * (d + 1), :], mk.y[d][:])),
                  deps=y_ready, ch="ou0", tag=f"dmpy{d}")
        b.finalize()
        return nc, mk
    gla_ready = []
    if do_gla:
        gla_ready = emit_gla(mk)
        mk.state["ao_free"] = {0: list(gla_ready), 1: list(gla_ready)}
        mk.state["ao_hi_init"] = list(gla_ready)
    for l in range(n_layers):
        y_ready = emit_layer(mk, l, y_ready)
    mk.emit_p3(y_ready, gla_ready)
    b.finalize()
    return nc, mk


"""Inner-transformer layer emitter (attn sublayer + geglu ffn sublayer).

PSUM discipline: mk.state["ps_free"][bank] holds the tokens of the LAST
readers of that bank; any op group writing the bank deps on them and the
group's evac tokens replace them.  One evac ENGINE per bank at a time.
"""
import numpy as np

QK_SCALE = 8.0


def psf(mk, bank):
    return list(mk.state["ps_free"].get(bank, []))


def pset(mk, bank, toks):
    mk.state["ps_free"][bank] = list(toks)


def emit_layer(mk, l, y_ready):
    b = mk.b
    st = mk.state

    if l > 0:
        _ybf_refresh(mk, l, "a", list(y_ready))
    y_writers = []
    ywin = {}
    emit_attn_hg(mk, l, 0, y_writers, ywin)
    fh = FfnHook(mk, l, ywin)
    emit_attn_hg(mk, l, 1, y_writers, ywin, hook=fh.hook)
    emit_ffn_w(mk, l, fh, y_writers)
    return y_writers


def _ybf_refresh(mk, l, ph, ydep):
    b = mk.b
    st = mk.state
    toks = []
    for d in range(NCH):
        dep = ydep + st["ybf_readers"] + st["u2_writers"]
        if d < 4:
            t = b.op("V", (lambda v, d=d: v.tensor_copy(mk.ybf[d][:, 1:T + 1], mk.y[d][:])),
                     deps=dep, tag=f"L{l}{ph}cast{d}")
            tz = b.op("V", (lambda v, d=d: v.memset(mk.ybf[d][:, 0:1], 0.0)),
                      deps=dep, tag=f"L{l}{ph}cz{d}")
        else:
            t = b.op("S", (lambda s, d=d: s.copy(mk.ybf[d][:, 1:T + 1], mk.y[d][:])),
                     deps=dep, tag=f"L{l}{ph}cast{d}")
            tz = b.op("V", (lambda v, d=d: v.memset(mk.ybf[d][:, 0:1], 0.0)),
                      deps=dep, tag=f"L{l}{ph}cz{d}")
        toks += [t, tz]
    st["ybf_readers"] = []
    st["u2_writers"] = []
    st["ybf_ready"] = toks


def _shift_ap(mk, k, c0, n):
    off = 1 + c0 if k < 4 else c0
    return mk.ybf[k][:, off:off + n]


def emit_attn_hg(mk, l, hg, y_writers, ywin=None, hook=None):
    b = mk.b
    st = mk.state
    if ywin is None:
        ywin = {}
    Wqkv = mk.dram["Wqkv"]
    Wo = mk.dram["Wo"]

    tabq = mk.swap_tables(0, st["tab_readers"])
    st["tab_readers"] = []

    secs = [hg * 512, 1024 + hg * 512, 2048 + hg * 512]
    ybf_dep = st["ybf_ready"]

    def fill_block(si, col0):
        bi = st["wq_bi"]
        pp = bi % 2
        dep = st["wst_free"][pp]
        dst = mk.wstbig[:, pp * 4096:pp * 4096 + 4096].rearrange("p (k c) -> p k c", c=512)
        srcm = Wqkv[l, 0:1024, col0:col0 + 512].rearrange("(k p) c -> p k c", p=128)
        last = b.dma("G", (lambda g, dst=dst, srcm=srcm: g.dma_start(dst, srcm)),
                     deps=dep, ch=f"wq{pp}", tag=f"L{l}h{hg}wf{si}")
        st["wq_bi"] += 1
        return pp, last

    pp0, f0 = fill_block(0, secs[0])
    pp1, f1 = fill_block(1, secs[1])
    ppv = fnv = None

    qk_evac = {}
    for si in range(2):
        pp = (pp0, pp1)[si]
        ftok = (f0, f1)[si]
        for m in range(4):
            banks = [(m % 2) * 4 + n for n in range(4)]

            def mm(t, pp=pp, m=m, banks=banks):
                r = None
                for k in range(8):
                    t.ldweights(mk.wst[pp][k][:, 128 * m:128 * (m + 1)])
                    for n in range(4):
                        r = t.matmul(mk.psum[banks[n]][0:128, 0:512],
                                     mk.wst[pp][k][:, 128 * m:128 * (m + 1)],
                                     _shift_ap(mk, k, 512 * n, 512),
                                     start=(k == 0), stop=(k == 7))
                return r

            deps = [ftok] + ybf_dep + st["aqk_free"]
            for bk in banks:
                deps += psf(mk, bk)
            tk = b.op("T", mm, deps=deps, tag=f"L{l}h{hg}qk{si}_{m}")
            st["ybf_readers"].append(tk)
            st["wst_free"][pp] = [tk]
            dst = mk.aqk[4 * si + m]
            for n in range(4):
                e = b.op("S", (lambda s, dst=dst, bk=banks[n], n=n:
                               s.copy(dst[:, 512 * n:512 * (n + 1)], mk.psum[bk][0:128, 0:512])),
                         deps=[tk], tag=f"L{l}h{hg}qke{si}_{m}_{n}")
                pset(mk, banks[n], [e])
                qk_evac[(si, m, n)] = e
        if si == 0:
            ppv, fnv = fill_block(2, secs[2])
    st["aqk_free"] = []

    ones_tok = []
    for tt2 in range(NTT):
        ones_tok.append(b.op("V", (lambda v, tt2=tt2: v.memset(
            mk.vaug[tt2][:, 0:520].rearrange("p (h d) -> p h d", d=65)[:, :, 64:65], 1.0)),
            deps=st["vaug_free"], tag=f"L{l}h{hg}vo{tt2}"))
    st["vaug_free"] = []
    v_evac = {}
    for tt2 in range(NTT):
        bank = tt2 % 4

        def mmv(t, ppv=ppv, tt2=tt2, bank=bank):
            r = None
            for k in range(8):
                t.ldweights(_shift_ap(mk, k, 128 * tt2, 128))
                r = t.matmul(mk.psum[bank][0:128, 0:512],
                             _shift_ap(mk, k, 128 * tt2, 128),
                             mk.wst[ppv][k][:, 0:512],
                             start=(k == 0), stop=(k == 7))
            return r

        tk = b.op("T", mmv, deps=[fnv] + ybf_dep + psf(mk, bank), tag=f"L{l}h{hg}v{tt2}")
        st["ybf_readers"].append(tk)
        st["wst_free"][ppv] = [tk]
        e = b.op("S", (lambda s, tt2=tt2, bank=bank:
                       s.copy(mk.vaug[tt2][:, 0:520].rearrange("p (h d) -> p h d", d=65)[:, :, 0:64],
                              mk.psum[bank][0:128, 0:512].rearrange("p (h d) -> p h d", d=64))),
                 deps=[tk, ones_tok[tt2]], tag=f"L{l}h{hg}ve{tt2}")
        pset(mk, bank, [e])
        v_evac[tt2] = e

    rot_done = {}

    def rotary(si, tab_toks):
        tab_readers = []
        scv = mk.pc_qs(l) if si == 0 else mk.pc_ks(l)
        for i in range(16):
            m, n = i % 4, i // 4
            src = mk.aqk[4 * si + m]
            sl = slice(512 * n, 512 * (n + 1))
            bY, bZ = i % 2, 2 + i % 2
            pr = i % 2
            qsq = mk.qsq[pr]
            qn = mk.qn[pr]
            t_sq = b.op("S", (lambda s, src=src, sl=sl, qsq=qsq:
                              s.activation(qsq[:, 0:512], src[:, sl], AF.Square)),
                        deps=[qk_evac[(si, m, n)]] + st["scr_qsq"][pr], tag=f"L{l}h{hg}r{si}sq{i}")

            def mm1(t, qsq=qsq):
                t.ldweights(mk.blkA[:, 0:2])
                return t.matmul(mk.psum[6][0:2, 0:512], mk.blkA[:, 0:2], qsq[:, 0:512],
                                start=True, stop=True)
            t_m1 = b.op("T", mm1, deps=[t_sq] + psf(mk, 6) + mk.tok["consts"], tag=f"L{l}h{hg}r{si}m1_{i}")
            t_sd = b.op("S", (lambda s: s.activation(mk.sdq[0:2, 0:512], mk.psum[6][0:2, 0:512], AF.Ln,
                                                     bias=mk.cpool[0:2, 2:3])),
                        deps=[t_m1] + st["scr_sdq"] + mk.tok["cpool"], tag=f"L{l}h{hg}r{si}sd{i}")
            pset(mk, 6, [t_sd])
            t_rq = b.op("S", (lambda s: s.activation(mk.rqb[0:2, 0:512], mk.sdq[0:2, 0:512],
                                                     AF.Exp, scale=-0.5)),
                        deps=[t_sd] + st["scr_rqb"], tag=f"L{l}h{hg}r{si}rq{i}")
            st["scr_sdq"] = [t_rq]

            def mm2(t, bY=bY):
                t.ldweights(mk.blk2[0:2, 0:128])
                return t.matmul(mk.psum[bY][0:128, 0:512], mk.blk2[0:2, 0:128], mk.rqb[0:2, 0:512],
                                start=True, stop=True)
            t_m2 = b.op("T", mm2, deps=[t_rq] + psf(mk, bY), tag=f"L{l}h{hg}r{si}m2_{i}")
            st["scr_rqb"] = [t_m2]
            t_qn = b.op("V", (lambda v, src=src, sl=sl, scv=scv, qn=qn, bY=bY:
                              v.scalar_tensor_tensor(qn[:, 0:512], src[:, sl], scv,
                                                     mk.psum[bY][0:128, 0:512],
                                                     ALU.mult, ALU.mult)),
                        deps=[t_m2] + mk.tok["par"][:1] + st["scr_qn"][pr], tag=f"L{l}h{hg}r{si}qn{i}")
            pset(mk, bY, [t_qn])

            def mm3(t, bZ=bZ, qn=qn):
                t.ldweights(mk.permrot[:, :])
                return t.matmul(mk.psum[bZ][0:128, 0:512], mk.permrot[:, :], qn[:, 0:512],
                                start=True, stop=True)
            t_m3 = b.op("T", mm3, deps=[t_qn] + psf(mk, bZ), tag=f"L{l}h{hg}r{si}m3_{i}")
            t_t1 = b.op("V", (lambda v, sl=sl, qn=qn: v.tensor_mul(mk.t1[:, 0:512], qn[:, 0:512], mk.tabC[:, sl])),
                        deps=[t_qn] + tab_toks + st["scr_t1"], tag=f"L{l}h{hg}r{si}t1_{i}")
            st["scr_qn"][pr] = [t_t1, t_m3]
            t_t2 = b.op("V", (lambda v, sl=sl, bZ=bZ, qsq=qsq:
                              v.tensor_mul(qsq[:, 0:512], mk.psum[bZ][0:128, 0:512],
                                           mk.tabS[:, sl])),
                        deps=[t_m3] + st["scr_qsq"][pr], tag=f"L{l}h{hg}r{si}t2_{i}")
            pset(mk, bZ, [t_t2])
            t_fin = b.op("V", (lambda v, src=src, sl=sl, qsq=qsq: v.tensor_add(src[:, sl], mk.t1[:, 0:512],
                                                                               qsq[:, 0:512])),
                         deps=[t_t2, t_t1], tag=f"L{l}h{hg}r{si}f{i}")
            st["scr_t1"] = [t_fin]
            st["scr_qsq"][pr] = [t_fin]
            tab_readers += [t_t1, t_t2]
            rot_done[(si, m, n)] = t_fin
            if i == 15:
                st["rot_guard"] = [t_fin, t_rq, t_m3]
        return tab_readers

    bar = mk.scr_bar()
    for key in ("scr_sdq", "scr_rqb", "scr_t1"):
        st[key] = list(bar)
    for key in ("scr_qsq", "scr_qn"):
        st[key] = {0: list(bar), 1: list(bar)}
    tabq_readers = rotary(0, tabq)
    tabk = mk.swap_tables(1, tabq_readers)
    tabk_readers = rotary(1, tabk)

    wo_tok = []
    for kk in range(4):
        dst = mk.tabC.tensor if False else None
        wo_tok.append(b.dma("G", (lambda g, kk=kk:
                                  g.dma_start(_wo_row(mk, kk),
                                              Wo[l, hg * 512 + 128 * kk:hg * 512 + 128 * (kk + 1), 0:1024])),
                            deps=tabk_readers if kk == 0 else [],
                            ch="wo", tag=f"L{l}h{hg}wo{kk}"))

    bar = mk.scr_bar()
    for i in range(4):
        st["expS_free"][i] = st["expS_free"][i] + bar
    for pkey in ("scr_lrow", "scr_rbs"):
        for pp_ in range(2):
            st[pkey][pp_] = st[pkey][pp_] + bar
    step = st["attn_step"]
    ow_free = {0: st["ao_free"].get(0, []), 1: st["ao_free"].get(1, [])}
    pend_c = None
    for w in range(NW):
        pp = w % 2
        ow_writers = []
        for hl in range(8):
            qt = mk.aqk[hl // 2]
            kt = mk.aqk[4 + hl // 2]
            base = (hl % 2) * 64
            p = step % 2
            bA, bB = p * 2, p * 2 + 1
            bPV = 4 + p
            eA = mk.expS[p * 2]
            eB = mk.expS[p * 2 + 1]
            q0 = 256 * w

            def sim(t, w=w, base=base, qt=qt, kt=kt, bA=bA, bB=bB, q0=q0):
                r = None
                for c in range(4):
                    if w == 0 and c < 2:
                        continue
                    kt0 = 256 * (w - 1) + 128 * c
                    bank = mk.psum[bA] if c < 2 else mk.psum[bB]
                    first = (c == 0) or (c == 2)
                    col = 0 if c in (0, 2) else (128 if c == 1 else 256)
                    ncols = 256 if c in (1, 2) else 128
                    rhs0 = q0 if c != 3 else q0 + 128
                    t.ldweights(kt[base:base + 64, kt0:kt0 + 128], tile_position=(base, 0))
                    r = t.matmul(bank[0:128, col:col + ncols],
                                 kt[base:base + 64, kt0:kt0 + 128],
                                 qt[base:base + 64, rhs0:rhs0 + ncols],
                                 start=first, stop=False,
                                 tile_position=(base, 0))
                rhs_id = mk.identD[base:base + 64, base + 64:base + 128]
                for bank_i, nm in ((bA, "lo"), (bB, "hi")):
                    if w == 0 and nm == "lo":
                        continue
                    bank = mk.psum[bank_i]
                    for col in (0, 256):
                        for half, mc in ((0, mk.maskA[nm]), (1, mk.maskB[nm])):
                            lastm = (col == 256 and half == 1)
                            t.ldweights(mc[base:base + 64, 0:128], tile_position=(base, 0))
                            r = t.matmul(
                                bank[0:128, col + 64 * half:col + 64 * half + 64],
                                mc[base:base + 64, 0:128], rhs_id,
                                start=False, stop=lastm, tile_position=(base, 0))
                return r

            deps = [rot_done[(0, hl // 2, w // 2)], rot_done[(1, hl // 2, w // 2)]]
            if w > 0:
                deps.append(rot_done[(1, hl // 2, (w - 1) // 2)])
            deps += psf(mk, bA) + psf(mk, bB) + mk.tok["consts"]
            t_sim = b.op("T", sim, deps=deps, tag=f"L{l}h{hg}w{w}s{hl}")
            t_eA = None
            if w > 0:
                t_eA = b.op("S", (lambda s, bA=bA, eA=eA: s.activation(eA[:, 0:384], mk.psum[bA][0:128, 0:384],
                                                                       AF.Exp, scale=QK_SCALE)),
                            deps=[t_sim] + st["expS_free"][p * 2], tag=f"L{l}h{hg}w{w}eA{hl}")
                pset(mk, bA, [t_eA])
            t_eB = b.op("S", (lambda s, bB=bB, eB=eB: s.activation(eB[:, 0:384], mk.psum[bB][0:128, 0:384],
                                                                   AF.Exp, scale=QK_SCALE)),
                        deps=[t_sim] + st["expS_free"][p * 2 + 1], tag=f"L{l}h{hg}w{w}eB{hl}")
            pset(mk, bB, [t_eB])

            def pvmm(t, w=w, hl=hl, eA=eA, eB=eB, bPV=bPV):
                srcs_all = []
                for th in range(2):
                    if th == 0:
                        s_ = ([(2 * (w - 1), eA, 0), (2 * (w - 1) + 1, eA, 128)] if w > 0 else [])
                        s_ += [(2 * w, eB, 0)]
                    else:
                        s_ = ([(2 * (w - 1) + 1, eA, 256)] if w > 0 else [])
                        s_ += [(2 * w, eB, 128), (2 * w + 1, eB, 256)]
                    srcs_all.append(s_)
                r = None
                first = True
                for th in range(2):
                    for (vt, ebuf, col) in srcs_all[th]:
                        t.ldweights(mk.vaug[vt][:, 65 * hl:65 * hl + 65])
                        r = t.matmul(mk.psum[bPV][0:65, 128 * th:128 * (th + 1)],
                                     mk.vaug[vt][:, 65 * hl:65 * hl + 65],
                                     ebuf[:, col:col + 128],
                                     start=first, stop=(th == 1 and (vt, ebuf, col) == srcs_all[1][-1]))
                        first = False
                return r

            pv_deps = ([t_eB] + ([t_eA] if t_eA else []) + psf(mk, bPV) +
                       [v_evac[j] for j in range(max(0, 2 * w - 2), 2 * w + 2)])
            t_pv = b.op("T", pvmm, deps=pv_deps, tag=f"L{l}h{hg}w{w}pv{hl}")
            st["vaug_free"].append(t_pv)
            st["expS_free"][p * 2] = [t_pv]
            st["expS_free"][p * 2 + 1] = [t_pv]

            def stageC(t_pv=t_pv, p=p, bPV=bPV, pp=pp, hl=hl, w=w,
                       ow_writers=ow_writers):
                lrow = mk.lrow[p]
                rbs = mk.rbs[p]
                bbc = 6 + p
                t_rc1 = b.op("S", (lambda s, bPV=bPV: s.activation(mk.psum[bPV][64:65, 0:256],
                                                                   mk.psum[bPV][64:65, 0:256], AF.Ln)),
                             deps=[t_pv], tag=f"L{l}h{hg}w{w}rcl{hl}")
                t_rc = b.op("S", (lambda s, bPV=bPV, lrow=lrow: s.activation(lrow[64:65, 0:256],
                                                                             mk.psum[bPV][64:65, 0:256],
                                                                             AF.Exp, scale=-1.0)),
                            deps=[t_rc1] + st["scr_lrow"][p], tag=f"L{l}h{hg}w{w}rc{hl}")

                def bcmm(t, bbc=bbc, lrow=lrow):
                    t.ldweights(mk.onesr[64:65, 0:64], tile_position=(64, 0))
                    return t.matmul(mk.psum[bbc][0:64, 0:256], mk.onesr[64:65, 0:64],
                                    lrow[64:65, 0:256], start=True, stop=True, tile_position=(64, 0))
                t_bc = b.op("T", bcmm, deps=[t_rc] + psf(mk, bbc), tag=f"L{l}h{hg}w{w}bc{hl}")
                st["scr_lrow"][p] = [t_bc]
                t_rb = b.op("S", (lambda s, bbc=bbc, rbs=rbs: s.copy(rbs[0:64, 0:256], mk.psum[bbc][0:64, 0:256])),
                            deps=[t_bc] + st["scr_rbs"][p], tag=f"L{l}h{hg}w{w}rb{hl}")
                pset(mk, bbc, [t_rb])
                j = hl // 2
                r0 = 0 if hl % 2 == 0 else 64
                t_om = b.op("V", (lambda v, bPV=bPV, pp=pp, j=j, rbs=rbs, r0=r0:
                                  v.tensor_mul(mk.ow[pp][j][r0:r0 + 64, 0:256],
                                               mk.psum[bPV][0:64, 0:256],
                                               rbs[0:64, 0:256])),
                            deps=[t_rb, t_pv] + ow_free[pp], tag=f"L{l}h{hg}w{w}om{hl}")
                ow_writers.append(t_om)
                pset(mk, bPV, [t_om])
                st["scr_rbs"][p] = [t_om]

            if pend_c is not None:
                pend_c()
            pend_c = stageC
            step += 1
        if pend_c is not None:
            pend_c()
            pend_c = None
        wo_evacs = []
        for m in range(8):
            bWo = (pp * 2) + (m % 2)
            r0 = 0 if m < 4 else 256

            def womm(t, m=m, bWo=bWo, r0=r0, pp=pp):
                r = None
                for kk in range(4):
                    t.ldweights(mk.wo_t[kk][m][:, :])
                    r = t.matmul(mk.psum[bWo][0:128, r0:r0 + 256],
                                 mk.wo_t[kk][m][:, :], mk.ow[pp][kk][:, 0:256],
                                 start=(kk == 0), stop=(kk == 3))
                return r

            deps = list(ow_writers) + wo_tok + psf(mk, bWo)
            t_wo = b.op("T", womm, deps=deps, tag=f"L{l}h{hg}w{w}WO{m}")
            st["tab_readers"].append(t_wo)
            e = b.op("V", (lambda v, m=m, w=w, bWo=bWo, r0=r0:
                           v.tensor_add(mk.y[m][:, 256 * w:256 * (w + 1)],
                                        mk.y[m][:, 256 * w:256 * (w + 1)],
                                        mk.psum[bWo][0:128, r0:r0 + 256])),
                     deps=[t_wo], tag=f"L{l}h{hg}w{w}WOe{m}")
            pset(mk, bWo, [e])
            wo_evacs.append(e)
            y_writers.append(e)
        ow_free[pp] = wo_evacs
        ywin[(hg, w)] = wo_evacs
        if hook is not None:
            hook(w)
    st["ao_free"] = ow_free
    st["attn_step"] = step


def _wo_row(mk, kk):
    return mk.wo_row[kk][:, :]


def _w2slot(mk, j):
    return mk.w2ring[:, j * 256:j * 256 + 256]


class FfnHook:

    def __init__(self, mk, l, ywin):
        self.mk, self.l, self.ywin = mk, l, ywin
        self.rtok = {}
        self.xn2_tok = {}
        self.stats_guard = []

    def hook(self, w):
        if w == 1:
            self.start()
            self.refresh(0)
        elif w == 3:
            self.refresh(1)
            self.stats(0)
        elif w == 5:
            self.refresh(2)
            self.stats(1)
        elif w == 7:
            self.refresh(3)
            self.stats(2)
            self.stats(3)

    def start(self):
        mk, st = self.mk, self.mk.state
        self.ybf_rd = st["ybf_readers"] + st["u2_writers"]
        st["ybf_readers"] = []
        st["u2_writers"] = []
        w2seed = sum((st["w2_free"].get(j, []) for j in range(8, 16)), [])
        w2seed += st.pop("ao_hi_init", [])
        st["scr_sqb"] = {0: list(w2seed), 1: list(w2seed)}
        st["scr_stt"] = list(w2seed)
        st["scr_mr"] = list(st.get("rot_guard", []))

    def refresh(self, n):
        mk, l, st, b = self.mk, self.l, self.mk.state, self.mk.b
        dep = list(self.ybf_rd)
        for hgx in (0, 1):
            for wx in range(max(0, 2 * n - 1), 2 * n + 2):
                dep += self.ywin.get((hgx, wx), [])
        c0 = 512 * n
        toks = []
        for k in range(NCH):
            if k < 4:
                t = b.op("V", (lambda v, k=k, c0=c0: v.tensor_copy(
                    mk.ybf[k][:, 1 + c0:513 + c0], mk.y[k][:, c0:c0 + 512])),
                    deps=dep, tag=f"L{l}fr{n}_{k}")
            else:
                t = b.op("S", (lambda s, k=k, c0=c0: s.copy(
                    mk.ybf[k][:, 1 + c0:513 + c0], mk.y[k][:, c0:c0 + 512])),
                    deps=dep, tag=f"L{l}fr{n}_{k}")
            toks.append(t)
            if n == 0:
                toks.append(b.op("V", (lambda v, k=k: v.memset(mk.ybf[k][:, 0:1], 0.0)),
                                 deps=dep, tag=f"L{l}frz{k}"))
        self.rtok[n] = toks

    def stats(self, n):
        mk, l, st, b = self.mk, self.l, self.mk.state, self.mk.b
        xn2_tok = self.xn2_tok
        ybf_dep = self.rtok[n] + self.rtok.get(n - 1, [])
        par = n % 2
        bMu = 4 if par == 0 else 6
        bSd = 5 if par == 0 else 7

        def mumm(t, n=n, bMu=bMu):
            t.ldweights(mk.onesb[:, 0:1])
            r = None
            for k in range(NCH):
                r = t.matmul(mk.psum[bMu][0:1, 0:512], mk.onesb[:, 0:1],
                             _shift_ap(mk, k, 512 * n, 512),
                             start=(k == 0), stop=(k == NCH - 1))
            return r
        t_mumm = b.op("T", mumm, deps=ybf_dep + mk.tok["consts"] + psf(mk, bMu),
                      tag=f"L{l}F{n}mum")
        st["ybf_readers"].append(t_mumm)
        t_st = None
        for k in range(NCH):
            sb = mk.sqb if k % 2 == 0 else mk.sqb2
            tq = b.op("G", (lambda g, k=k, n=n, sb=sb: g.tensor_mul(
                sb[:, 0:512], _shift_ap(mk, k, 512 * n, 512),
                _shift_ap(mk, k, 512 * n, 512))),
                deps=ybf_dep + st["scr_sqb"][k % 2], tag=f"L{l}F{n}sq{k}")

            def statmm(t, k=k, sb=sb, bSd=bSd):
                if k == 0:
                    t.ldweights(mk.onesb[:, 0:1])
                return t.matmul(mk.psum[bSd][0:1, 0:512], mk.onesb[:, 0:1],
                                sb[:, 0:512], start=(k == 0), stop=(k == NCH - 1))

            deps = [tq] + mk.tok["consts"]
            if k == 0:
                deps += psf(mk, bSd)
            if t_st is not None:
                deps.append(t_st)
            t_st = b.op("T", statmm, deps=deps, tag=f"L{l}F{n}st{k}")
            st["scr_sqb"][k % 2] = [t_st]
        st["ybf_readers"].append(t_st)
        t_mu = b.op("V", (lambda v, bMu=bMu: v.tensor_scalar(mk.mr[0:1, 0:512],
                                                             mk.psum[bMu][0:1, 0:512],
                                                             1.0 / D, None, ALU.mult)),
                    deps=[t_mumm] + st["scr_mr"], tag=f"L{l}F{n}mu")
        t_e2 = b.op("V", (lambda v, bSd=bSd: v.tensor_scalar(mk.psum[bSd][0:1, 0:512],
                                                             mk.psum[bSd][0:1, 0:512],
                                                             1.0 / D, None, ALU.mult)),
                    deps=[t_st], tag=f"L{l}F{n}e2")
        t_ms = b.op("V", (lambda v: v.tensor_mul(mk.stt[0:1, 0:512],
                                                 mk.mr[0:1, 0:512], mk.mr[0:1, 0:512])),
                    deps=[t_mu] + st["scr_stt"], tag=f"L{l}F{n}ms")
        t_var = b.op("V", (lambda v, bSd=bSd: v.scalar_tensor_tensor(
            mk.psum[bSd][0:1, 0:512], mk.stt[0:1, 0:512], -1.0,
            mk.psum[bSd][0:1, 0:512], ALU.mult, ALU.add)),
            deps=[t_ms, t_e2], tag=f"L{l}F{n}var")
        st["scr_stt"] = [t_var]
        t_sdv = b.op("S", (lambda s, bSd=bSd: s.activation(mk.psum[bSd][0:1, 0:512],
                                                           mk.psum[bSd][0:1, 0:512], AF.Ln,
                                                           bias=mk.cpool[0:1, 1:2])),
                     deps=[t_var] + mk.tok["cpool"], tag=f"L{l}F{n}sd")
        t_rs = b.op("S", (lambda s, bSd=bSd: s.activation(mk.mr[0:1, 512:1024],
                                                          mk.psum[bSd][0:1, 0:512],
                                                          AF.Exp, scale=-0.5)),
                    deps=[t_sdv], tag=f"L{l}F{n}rs")
        def bcm(t, bMu=bMu, bSd=bSd):
            t.ldweights(mk.onesr[0:1, 0:128])
            t.matmul(mk.psum[bMu][0:128, 0:512], mk.onesr[0:1, 0:128],
                     mk.mr[0:1, 0:512], start=True, stop=True)
            return t.matmul(mk.psum[bSd][0:128, 0:512], mk.onesr[0:1, 0:128],
                            mk.mr[0:1, 512:1024], start=True, stop=True)
        t_bc = b.op("T", bcm, deps=[t_mu, t_rs, t_ms, t_var, t_sdv], tag=f"L{l}F{n}bc")
        st["scr_mr"] = [t_bc]
        for k in range(NCH):
            ts1 = b.op("V", (lambda v, k=k, n=n, bMu=bMu: v.tensor_sub(
                mk.stt[:, 0:512], _shift_ap(mk, k, 512 * n, 512), mk.psum[bMu][0:128, 0:512])),
                deps=[t_bc] + st["scr_stt"] + ybf_dep, tag=f"L{l}F{n}x1_{k}")
            ts2 = b.op("V", (lambda v, k=k, bSd=bSd: v.scalar_tensor_tensor(
                mk.stt[:, 0:512], mk.stt[:, 0:512], mk.pc_lng(l)[:, k:k + 1],
                mk.psum[bSd][0:128, 0:512], ALU.mult, ALU.mult)),
                deps=[ts1] + mk.tok["par"][:1], tag=f"L{l}F{n}x2_{k}")
            ts3 = b.op("S", (lambda s, k=k, n=n: s.activation(
                mk.aqk[k][:, 512 * n:512 * (n + 1)], mk.stt[:, 0:512],
                AF.Identity, bias=mk.pc_lnb(l)[:, k:k + 1])),
                deps=[ts2] + st["aqk_free"] + mk.tok["par"][:1] +
                     self.ywin.get((1, min(2 * n + 2, 7)), []),
                tag=f"L{l}F{n}x3_{k}")
            st["scr_stt"] = [ts3]
            st["ybf_readers"].append(ts1)
            xn2_tok[(k, n)] = ts3
            last_ts1, last_ts2, last_ts3 = ts1, ts2, ts3
        pset(mk, bMu, [last_ts1])
        pset(mk, bSd, [last_ts2])
        self.stats_guard = [t_bc, last_ts3, last_ts2]


def emit_ffn_w(mk, l, fh, y_writers):
    b = mk.b
    st = mk.state
    W1 = mk.dram["W1"]
    W2 = mk.dram["W2"]
    xn2_tok = fh.xn2_tok
    st["aqk_free"] = []
    for slj in range(8, 16):
        st["w2_free"][slj] = st["w2_free"].get(slj, []) + fh.stats_guard

    bar2 = mk.scr_bar()
    st["scr_gg"] = {0: list(bar2), 1: list(bar2)}
    for sl in range(4):
        st["w1_free"][sl] = list(st["wst_free"][sl % 2])
    uvdep = list(st["vaug_free"])
    st["vaug_free"] = []
    CA = [128] * (NA - 1) + [DI - 128 * (NA - 1)]

    def w1slot(sl, k):
        base = (sl % 2) * 4096 + k * 512 + 256 * (sl // 2)
        return mk.wstbig[:, base:base + 256]

    w1_mm_tok = {}
    u_written = {}
    w1_fill_tok = {}
    sl_base = st["w1_slot"]
    st["w1_slot"] += 2 * NA

    def w1_fill(np_, i, seg=None):
        sl = (sl_base + np_ * NA + i) % 4
        ca = CA[i]
        gate0 = DI + 128 * i
        dep = st["w1_free"].get(sl, [])
        kv = mk.wstbig[:, :].rearrange("p (k c) -> p k c", c=512)
        co = 256 * (sl // 2)
        dsta = kv[:, 8 * (sl % 2):8 * (sl % 2) + 8, co:co + ca]
        dstg = kv[:, 8 * (sl % 2):8 * (sl % 2) + 8, co + 128:co + 128 + ca]
        srca = W1[l, 0:1024, 128 * i:128 * i + ca].rearrange("(k p) c -> p k c", p=128)
        srcg = W1[l, 0:1024, gate0:gate0 + ca].rearrange("(k p) c -> p k c", p=128)
        seg = seg or (l, np_)
        ch = f"w1{sl % 2}"
        b.dma("G", (lambda g, dsta=dsta, srca=srca: g.dma_start(dsta, srca)),
              deps=dep, ch=ch, look=2, seg=seg, tag=f"L{l}W1f{np_}_{i}a")
        w1_fill_tok[(np_, i)] = b.dma(
            "G", (lambda g, dstg=dstg, srcg=srcg: g.dma_start(dstg, srcg)),
            deps=[], ch=ch, look=2, seg=seg, tag=f"L{l}W1f{np_}_{i}g")

    def w1_compute(np_):
        for i in range(NA):
            if (np_, i) not in w1_fill_tok:
                w1_fill(np_, i)
            sl = (sl_base + np_ * NA + i) % 4
            ca = CA[i]
            last = w1_fill_tok[(np_, i)]
            banks = [(i % 2) * 4 + j for j in range(4)]

            def w1mm(t, sl=sl, i=i, ca=ca, np_=np_, banks=banks):
                r = None
                for k in range(8):
                    t.ldweights(w1slot(sl, k)[:, 0:ca])
                    for jn in range(2):
                        n = 2 * np_ + jn
                        r = t.matmul(mk.psum[banks[jn]][0:ca, 0:512],
                                     w1slot(sl, k)[:, 0:ca],
                                     mk.aqk[k][:, 512 * n:512 * (n + 1)],
                                     start=(k == 0), stop=(k == 7))
                    t.ldweights(w1slot(sl, k)[:, 128:128 + ca])
                    for jn in range(2):
                        n = 2 * np_ + jn
                        r = t.matmul(mk.psum[banks[2 + jn]][0:ca, 0:512],
                                     w1slot(sl, k)[:, 128:128 + ca],
                                     mk.aqk[k][:, 512 * n:512 * (n + 1)],
                                     start=(k == 0), stop=(k == 7))
                return r

            deps = [last] + [xn2_tok[(k, 2 * np_ + jn)] for k in range(8) for jn in range(2)]
            for bk in banks:
                deps += psf(mk, bk)
            t_mm = b.op("T", w1mm, deps=deps, tag=f"L{l}W1m{np_}_{i}")
            st["w1_free"][sl] = [t_mm]
            st["aqk_free"].append(t_mm)
            w1_mm_tok[(np_, i)] = t_mm
            for jn in range(2):
                n = 2 * np_ + jn
                gbuf = mk.gg if jn == 0 else mk.gg2
                tg = b.op("S", (lambda s, i=i, ca=ca, bk=banks[2 + jn], gbuf=gbuf:
                               s.activation(gbuf[0:ca, 0:512],
                                            mk.psum[bk][0:ca, 0:512], AF.Gelu,
                                            bias=mk.pc_b1g(l)[0:ca, i:i + 1])),
                          deps=[t_mm] + st["scr_gg"][jn] + mk.tok["par"][:1], tag=f"L{l}W1g{np_}_{i}_{jn}")
                udst = (mk.u if jn == 0 else mk.u2)[i]
                uw = b.op("V", (lambda v, i=i, ca=ca, bk=banks[jn], gbuf=gbuf, udst=udst:
                               v.scalar_tensor_tensor(udst[0:ca, 0:512], mk.psum[bk][0:ca, 0:512],
                                                      mk.pc_b1a(l)[0:ca, i:i + 1], gbuf[0:ca, 0:512],
                                                      ALU.add, ALU.mult)),
                          deps=[tg] + st["u_free"].get((jn, i), []) + (uvdep if np_ == 0 else []),
                          tag=f"L{l}W1u{np_}_{i}_{jn}")
                st["scr_gg"][jn] = [uw]
                pset(mk, banks[jn], [uw])
                pset(mk, banks[2 + jn], [uw])
                st["u_free"][(jn, i)] = []
                u_written[(np_, jn, i)] = uw
                if jn == 1:
                    st["u2_writers"].append(uw)

    def w2_emit(np_):
        if np_ == 0:
            ao_seed = st["ao_free"].get(0, []) + st["ao_free"].get(1, [])
            for slj in range(16):
                st["w2_free"][slj] = st["w2_free"].get(slj, []) + ao_seed
        for mg in range(4):
            banks = [(mg % 2) * 4 + j for j in range(4)]
            prev_mm = None
            first_deps = []
            for bk in banks:
                first_deps += psf(mk, bk)
            for kk in range(NA):
                slj = st["w2_slot"] % 16
                st["w2_slot"] += 1
                ck = 128 if kk < NA - 1 else DI - 128 * (NA - 1)
                ft = b.dma("G", (lambda g, kk=kk, slj=slj, ck=ck, mg=mg:
                                 g.dma_start(_w2slot(mk, slj)[0:ck, 0:256],
                                             W2[l, 128 * kk:128 * kk + ck, 256 * mg:256 * (mg + 1)])),
                           deps=st["w2_free"].get(slj, []), ch=f"w2{slj % 4}",
                           look=3, seg=(l, "w2", np_), tag=f"L{l}W2f{np_}_{mg}_{kk}")

                def w2mm(t, np_=np_, mg=mg, kk=kk, slj=slj, ck=ck, banks=banks):
                    r = None
                    for jm in range(2):
                        t.ldweights(_w2slot(mk, slj)[0:ck, 128 * jm:128 * (jm + 1)])
                        for jn in range(2):
                            n = 2 * np_ + jn
                            usrc = (mk.u if jn == 0 else mk.u2)[kk]
                            r = t.matmul(mk.psum[banks[2 * jm + jn]][0:128, 0:512],
                                         _w2slot(mk, slj)[0:ck, 128 * jm:128 * (jm + 1)],
                                         usrc[0:ck, 0:512],
                                         start=(kk == 0), stop=(kk == NA - 1))
                    return r

                deps = [ft] + [u_written[(np_, jn, kk)] for jn in range(2)]
                if prev_mm is not None:
                    deps.append(prev_mm)
                else:
                    deps += first_deps
                t_w2 = b.op("T", w2mm, deps=deps, tag=f"L{l}W2m{np_}_{mg}_{kk}")
                st["w2_free"][slj] = [t_w2]
                for jn in range(2):
                    st["u_free"][(jn, kk)] = st["u_free"].get((jn, kk), []) + [t_w2]
                prev_mm = t_w2
            for jm in range(2):
                m = 2 * mg + jm
                for jn in range(2):
                    n = 2 * np_ + jn
                    e = b.op("V", (lambda v, m=m, n=n, bk=banks[2 * jm + jn]:
                                   v.scalar_tensor_tensor(mk.y[m][:, 512 * n:512 * (n + 1)],
                                                          mk.psum[bk][0:128, 0:512],
                                                          mk.pc_b2(l)[:, m:m + 1],
                                                          mk.y[m][:, 512 * n:512 * (n + 1)],
                                                          ALU.add, ALU.add)),
                             deps=[t_w2] + mk.tok["par"][:1], tag=f"L{l}W2e{np_}_{m}_{n}")
                    pset(mk, banks[2 * jm + jn], [e])
                    y_writers.append(e)
        return t_w2

    w1_compute(0)
    for i in range(4):
        w1_fill(1, i, seg=(l, 1, "h"))
    w2_emit(0)
    w1_compute(1)
    last_w2 = w2_emit(1)
    last_w1mm = w1_mm_tok[(1, NA - 1)]
    st["wst_free"] = {0: [last_w1mm], 1: [last_w1mm]}
    st["vaug_free"] = [last_w2]
    st["ybf_readers"].append(last_w2)
    st["ao_free"] = {0: [last_w2], 1: [last_w2]}


"""GLA token-mixer branch: gla_out = (GLA(xn) normalized, gated) @ Wo.

Phases:
  P1a: q,k,v,gk projections from xn (ybf cols 1:T+1); weights single-buffered
       through wstbig block 0; t16/w16/wgk2 parked in block-1 region.
  P1b: per head: gk pre -> sp=-log_sigmoid path (Exp,Ln), masked cumsum scan
       -> G' (gG); factors qg (in-place over q), ki (in-place over k), glast.
  P1c: chunk scan: per (c, h): kg scale+transpose, v transposes, A^T+mask,
       o = o_intra + o_inter, S update; o -> gla_o scratch (HBM, bf16).
  P1d: g projection (-> aqk); o groupnorm * silu(g); @ gWo -> gla_out (HBM f32).
"""

LNSCALE_COL = 3
EPS_COL = 4


def emit_gla(mk):
    b = mk.b
    st = mk.state
    xn = lambda k, n: mk.ybf[k][:, 1 + 512 * n:1 + 512 * (n + 1)]
    ybf_dep = st["ybf_ready"]

    gw_free = {0: [], 1: []}

    def wslot(bi, k):
        if bi == 0:
            return mk.wst[0][k]
        return mk.w2ring[:, 512 * k:512 * (k + 1)]

    def fill(src_ap, tag, bi=0):
        dst = (mk.wstbig[:, 0:4096] if bi == 0
               else mk.w2ring[:, 0:4096]).rearrange("p (k c) -> p k c", c=512)
        tok = b.dma("G", (lambda g, dst=dst, src_ap=src_ap: g.dma_start(dst, src_ap)),
                    deps=gw_free[bi], ch=f"gw{bi}", tag=tag)
        return tok

    def proj4(ftok, dst_tiles, tag, bi=0, extra_dep=()):
        evs = []
        for m in range(4):
            banks = [(m % 2) * 4 + n for n in range(4)]

            def mm(t, m=m, banks=banks, bi=bi):
                r = None
                for k in range(8):
                    t.ldweights(wslot(bi, k)[:, 128 * m:128 * (m + 1)])
                    for n in range(4):
                        r = t.matmul(mk.psum[banks[n]][0:128, 0:512],
                                     wslot(bi, k)[:, 128 * m:128 * (m + 1)],
                                     xn(k, n), start=(k == 0), stop=(k == 7))
                return r

            deps = [ftok] + list(extra_dep) + ybf_dep
            for bk in banks:
                deps += psf(mk, bk)
            tk = b.op("T", mm, deps=deps, tag=f"{tag}m{m}")
            st["ybf_readers"].append(tk)
            for n in range(4):
                e = b.op("S", (lambda s, m=m, n=n, bk=banks[n]:
                               s.copy(dst_tiles[m][:, 512 * n:512 * (n + 1)],
                                      mk.psum[bk][0:128, 0:512])),
                         deps=[tk], tag=f"{tag}e{m}_{n}")
                pset(mk, banks[n], [e])
                evs.append(e)
        return [tk], evs

    fq = fill(mk.dram["gWq"][0:1024, 0:512].rearrange("(k p) c -> p k c", p=128), "gfq", 0)
    fk = fill(mk.dram["gWk"][0:1024, 0:512].rearrange("(k p) c -> p k c", p=128), "gfk", 1)
    gw_free[0], q_ev = proj4(fq, [mk.aqk[m] for m in range(4)], "gq", 0)
    gw_free[1], k_ev = proj4(fk, [mk.aqk[4 + m] for m in range(4)], "gk", 1)
    fv0 = fill(mk.dram["gWv"][0:1024, 0:512].rearrange("(k p) c -> p k c", p=128), "gfv0", 0)
    gw_free[0], v_ev0 = proj4(fv0, [mk.gv[m] for m in range(4)], "gv0", 0, extra_dep=st["vaug_free"])
    st["vaug_free"] = []
    fv1 = fill(mk.dram["gWv"][0:1024, 512:1024].rearrange("(k p) c -> p k c", p=128), "gfv1", 1)
    gw_free[1], v_ev1 = proj4(fv1, [mk.gv[4 + m] for m in range(4)], "gv1", 1)
    v_ev = v_ev0 + v_ev1
    st["scr_gG"] = list(st["scr_gG"]) + list(gw_free[1])

    wst1_users = []
    f16 = b.dma("G", (lambda g: g.dma_start(
        mk.w16[:, :].rearrange("p (k c) -> p k c", c=16),
        mk.dram["gWgk1"][0:1024, 0:16].rearrange("(k p) c -> p k c", p=128))),
        ch="gw2", tag="gf16")
    fg2 = b.dma("G", (lambda g: g.dma_start(mk.wgk2[:, :], mk.dram["gWgk2"][0:16, 0:512])),
                ch="gw2", tag="gfg2")
    t16_ev = []
    for n in range(NT4):
        bank = 4 + n % 2

        def mm16(t, n=n, bank=bank):
            r = None
            for k in range(8):
                t.ldweights(mk.w16[:, 16 * k:16 * (k + 1)])
                r = t.matmul(mk.psum[bank][0:16, 0:512], mk.w16[:, 16 * k:16 * (k + 1)],
                             xn(k, n), start=(k == 0), stop=(k == 7))
            return r

        tk = b.op("T", mm16, deps=[f16] + ybf_dep + psf(mk, bank), tag=f"gt16_{n}")
        wst1_users.append(tk)
        st["ybf_readers"].append(tk)
        e = b.op("S", (lambda s, n=n, bank=bank:
                       s.copy(mk.t16[0:16, 512 * n:512 * (n + 1)], mk.psum[bank][0:16, 0:512])),
                 deps=[tk], tag=f"gt16e{n}")
        pset(mk, bank, [e])
        t16_ev.append(e)

    par_dep = mk.tok["par"]
    glast_tok = {}
    fac_tok = {}
    scan_tok = []
    prev_head = []
    for h in range(GH):
        sp_ev = []
        for n in range(NT4):
            bank = (n % 2)

            def mmp(t, h=h, n=n, bank=bank):
                t.ldweights(mk.wgk2[0:16, 128 * h:128 * (h + 1)])
                return t.matmul(mk.psum[bank][0:128, 0:512], mk.wgk2[0:16, 128 * h:128 * (h + 1)],
                                mk.t16[0:16, 512 * n:512 * (n + 1)], start=True, stop=True)
            tk = b.op("T", mmp, deps=[fg2, t16_ev[n]] + psf(mk, bank), tag=f"gp{h}_{n}")
            wst1_users.append(tk)
            e1 = b.op("S", (lambda s, n=n, bank=bank, h=h:
                            s.activation(mk.gsp[:, 512 * n:512 * (n + 1)], mk.psum[bank][0:128, 0:512],
                                         AF.Exp, bias=mk.bgk2n[:, h:h + 1], scale=-1.0)),
                     deps=[tk] + par_dep + prev_head + st["scr_gsp"], tag=f"gpe{h}_{n}")
            pset(mk, bank, [e1])
            e2 = b.op("S", (lambda s, n=n: s.activation(mk.gsp[:, 512 * n:512 * (n + 1)],
                                                        mk.gsp[:, 512 * n:512 * (n + 1)],
                                                        AF.Ln, bias=mk.cpool[:, 0:1])),
                     deps=[e1] + mk.tok["cpool"], tag=f"gpl{h}_{n}")
            sp_ev.append(e2)
        t_scan = b.op("V", (lambda v: v.tensor_tensor_scan(mk.gG[:, :], mk.scanmask[:, :],
                                                           mk.gsp[:, :], 0.0, ALU.mult, ALU.add)),
                      deps=sp_ev + mk.tok["consts"] + st["scr_gG"], tag=f"gsc{h}")
        scan_tok.append(t_scan)
        st["scr_gsp"] = [t_scan]
        t_gl = b.op("S", (lambda s, h=h: s.activation(
            mk.glast[:, 32 * h:32 * (h + 1)],
            mk.gG[:, :].rearrange("p (c g) -> p c g", g=GC)[:, :, 63:64],
            AF.Exp, scale=-1.0 / 16.0)),
            deps=[t_scan] + st["scr_glast"], tag=f"ggl{h}")
        glast_tok[h] = t_gl
        ftoks = []
        for n in range(NT4):
            ea = mk.e512a if n % 2 == 0 else mk.e512b
            t_e1 = b.op("S", (lambda s, n=n, ea=ea: s.activation(
                ea[:, 0:512], mk.gG[:, 512 * n:512 * (n + 1)], AF.Exp,
                bias=mk.cpool[:, LNSCALE_COL:LNSCALE_COL + 1], scale=-1.0 / 16.0)),
                deps=[t_scan] + mk.tok["cpool"] + st["scr_e"][n % 2], tag=f"gf1{h}_{n}")
            t_q = b.op("V", (lambda v, h=h, n=n, ea=ea: v.tensor_mul(
                mk.aqk[h][:, 512 * n:512 * (n + 1)], mk.aqk[h][:, 512 * n:512 * (n + 1)],
                ea[:, 0:512])),
                deps=[t_e1, q_ev[4 * h + n]], tag=f"gfq{h}_{n}")
            t_e2 = b.op("S", (lambda s, n=n, ea=ea: s.activation(
                ea[:, 0:512], mk.gG[:, 512 * n:512 * (n + 1)], AF.Exp, scale=1.0 / 16.0)),
                deps=[t_q, t_e1], tag=f"gf2{h}_{n}")
            t_k = b.op("V", (lambda v, h=h, n=n, ea=ea: v.tensor_mul(
                mk.aqk[4 + h][:, 512 * n:512 * (n + 1)], mk.aqk[4 + h][:, 512 * n:512 * (n + 1)],
                ea[:, 0:512])),
                deps=[t_e2, k_ev[4 * h + n]], tag=f"gfk{h}_{n}")
            st["scr_e"][n % 2] = [t_k]
            ftoks += [t_q, t_k]
        fac_tok[h] = ftoks
        prev_head = ftoks
        st["scr_gG"] = ftoks + [t_gl]

    go_cnt = 0
    stage_done = {}
    ost_wr = {}
    sst_rd = {h: [] for h in range(GH)}
    sst_wr = {h: [] for h in range(GH)}
    zS = []
    for h in range(GH):
        zS.append(b.op("V", (lambda v, h=h: v.memset(mk.Sst[h][:, :], 0.0)),
                       deps=list(prev_head) + scan_tok, tag=f"gz{h}"))
        zS[-1] = b.op("V", (lambda v, h=h: v.memset(mk.Sbf[h][:, :], 0.0)),
                      deps=[zS[-1]] + list(prev_head), tag=f"gzb{h}")
    step = 0
    o_dma = []
    for c in range(GNC):
        for h in range(GH):
            p = step % 2
            bTR, bA, bO, bSD = p, 2 + p, 4 + p, 6 + p
            qg = mk.aqk[h][:, GC * c:GC * (c + 1)]
            ki = mk.aqk[4 + h][:, GC * c:GC * (c + 1)]
            t_kg = b.op("V", (lambda v, ki=ki, h=h, c=c: v.tensor_scalar_mul(
                mk.kgs[:, 0:GC], ki, mk.glast[:, 32 * h + c:32 * h + c + 1])),
                deps=[glast_tok[h]] + fac_tok[h] + st["scr_kgs"] +
                     (prev_head if c == 0 else []), tag=f"gkg{c}_{h}")
            def mmtr(t, h=h, c=c, bTR=bTR):
                t.ldweights(mk.kgs[:, 0:GC])
                t.matmul(mk.psum[bTR][0:64, 0:128], mk.kgs[:, 0:GC], mk.ident_b[:, :],
                         start=True, stop=False)
                t.ldweights(mk.gv[2 * h][:, GC * c:GC * (c + 1)])
                t.matmul(mk.psum[bTR][0:64, 128:256], mk.gv[2 * h][:, GC * c:GC * (c + 1)],
                         mk.ident_b[:, :], start=False, stop=False)
                t.ldweights(mk.gv[2 * h + 1][:, GC * c:GC * (c + 1)])
                return t.matmul(mk.psum[bTR][0:64, 256:384], mk.gv[2 * h + 1][:, GC * c:GC * (c + 1)],
                                mk.ident_b[:, :], start=False, stop=True)
            deps = [t_kg] + mk.tok["consts"] + psf(mk, bTR)
            if c == 0:
                deps += v_ev
            t_tr = b.op("T", mmtr, deps=deps, tag=f"gtr{c}_{h}")
            t_ek = b.op("S", (lambda s, bTR=bTR: s.copy(mk.kgt[0:64, 0:128], mk.psum[bTR][0:64, 0:128])),
                        deps=[t_tr] + st["scr_kgt"], tag=f"gek{c}_{h}")
            t_ev = b.op("S", (lambda s, bTR=bTR: s.copy(mk.vt[0:64, 0:256], mk.psum[bTR][0:64, 128:384])),
                        deps=[t_tr] + st["scr_vt"], tag=f"gev{c}_{h}")
            pset(mk, bTR, [t_ek, t_ev])
            def mmat(t, qg=qg, ki=ki, bA=bA):
                t.ldweights(ki)
                return t.matmul(mk.psum[bA][0:64, 0:64], ki, qg, start=True, stop=True)
            t_at = b.op("T", mmat, deps=fac_tok[h] + psf(mk, bA), tag=f"gat{c}_{h}")
            t_am = b.op("V", (lambda v, bA=bA: v.tensor_mul(mk.at_sb[0:64, 0:64],
                                                            mk.psum[bA][0:64, 0:64],
                                                            mk.mask_hi[0:64, 0:64])),
                        deps=[t_at] + st["scr_at"] + mk.tok["consts"], tag=f"gam{c}_{h}")
            pset(mk, bA, [t_am])
            def mmo(t, h=h, qg=qg, bO=bO):
                import os
                nointer = os.environ.get("MK_GLA_NOINTER") == "1"
                nointra = os.environ.get("MK_GLA_NOINTRA") == "1"
                if nointra:
                    t.ldweights(mk.Sbf[h][:, 0:128])
                    t.matmul(mk.psum[bO][0:128, 0:GC], mk.Sbf[h][:, 0:128], qg,
                             start=True, stop=False)
                    t.ldweights(mk.Sbf[h][:, 128:256])
                    return t.matmul(mk.psum[bO][0:128, GC:2 * GC], mk.Sbf[h][:, 128:256], qg,
                                    start=False, stop=True)
                t.ldweights(mk.vt[0:64, 0:128])
                t.matmul(mk.psum[bO][0:128, 0:GC], mk.vt[0:64, 0:128], mk.at_sb[0:64, 0:64],
                         start=True, stop=False)
                t.ldweights(mk.vt[0:64, 128:256])
                r2 = t.matmul(mk.psum[bO][0:128, GC:2 * GC], mk.vt[0:64, 128:256], mk.at_sb[0:64, 0:64],
                              start=False, stop=nointer)
                if nointer:
                    return r2
                if os.environ.get("MK_GLA_SWAP") == "1":
                    pass
                t.ldweights(mk.Sbf[h][:, 0:128])
                t.matmul(mk.psum[bO][0:128, 0:GC], mk.Sbf[h][:, 0:128], qg,
                         start=False, stop=False)
                t.ldweights(mk.Sbf[h][:, 128:256])
                return t.matmul(mk.psum[bO][0:128, GC:2 * GC], mk.Sbf[h][:, 128:256], qg,
                                start=False, stop=True)
            t_o = b.op("T", mmo, deps=[t_am, t_ev, t_ek, zS[h]] + sst_wr[h] + psf(mk, bO),
                       tag=f"go{c}_{h}")
            sst_rd[h] = [t_o]
            def mmsd(t, bSD=bSD):
                t.ldweights(mk.kgt[0:64, 0:128])
                return t.matmul(mk.psum[bSD][0:128, 0:256], mk.kgt[0:64, 0:128],
                                mk.vt[0:64, 0:256], start=True, stop=True)
            t_sd = b.op("T", mmsd, deps=[t_ek, t_ev] + psf(mk, bSD), tag=f"gsd{c}_{h}")
            st["scr_kgt"] = [t_sd]
            st["scr_vt"] = [t_sd, t_o]
            st["scr_at"] = [t_o]
            t_s1 = b.op("V", (lambda v, h=h, c=c: v.tensor_scalar_mul(
                mk.Sst[h][:, :], mk.Sst[h][:, :], mk.glast[:, 32 * h + c:32 * h + c + 1])),
                deps=[t_o, zS[h], glast_tok[h]] + sst_wr[h], tag=f"gs1{c}_{h}")
            t_s2 = b.op("V", (lambda v, h=h, bSD=bSD: v.tensor_add(
                mk.Sst[h][:, :], mk.Sst[h][:, :], mk.psum[bSD][0:128, 0:256])),
                deps=[t_s1, t_sd], tag=f"gs2{c}_{h}")
            t_s3 = b.op("V", (lambda v, h=h: v.tensor_copy(mk.Sbf[h][:, :], mk.Sst[h][:, :])),
                        deps=[t_s2] + sst_rd[h], tag=f"gs3{c}_{h}")
            sst_wr[h] = [t_s2, t_s3]
            pset(mk, bSD, [t_s2])
            pp = (c // 4) % 2
            grp = c % 4
            oev = []
            for half in range(2):
                j = 2 * h + half
                e = b.op("S", (lambda s, pp=pp, j=j, grp=grp, half=half, bO=bO:
                               s.copy(mk.gost[pp][j][:, GC * grp:GC * (grp + 1)],
                                      mk.psum[bO][0:128, GC * half:GC * (half + 1)])),
                         deps=[t_o] + stage_done.get((pp, j), []), tag=f"goe{c}_{h}_{half}")
                ost_wr.setdefault((pp, j), []).append(e)
                oev.append(e)
            pset(mk, bO, oev)
            step += 1
        if c % 4 == 3:
            pp = (c // 4) % 2
            c0 = c - 3
            for j in range(8):
                dm = b.dma("Y", (lambda sy, pp=pp, j=j, c0=c0:
                                 sy.dma_start(mk.dram["gla_o"][128 * j:128 * (j + 1),
                                                              GC * c0:GC * (c0 + 4)],
                                              mk.gost[pp][j][:, 0:256])),
                           deps=ost_wr.get((pp, j), []), ch=f"go{pp}", tag=f"godma{c0}_{j}")
                stage_done[(pp, j)] = [dm]
                ost_wr[(pp, j)] = []
                o_dma.append(dm)

    last_users = []
    for h in range(GH):
        last_users += sst_rd[h] + sst_wr[h]
    last_users += [t_at, t_o]
    fg0 = fill(mk.dram["gWg"][0:1024, 0:512].rearrange("(k p) c -> p k c", p=128), "gfg0", 0)
    gw_free[0], g_ev0 = proj4(fg0, [mk.aqk[m] for m in range(4)], "gg0", 0, extra_dep=last_users)
    fg1 = fill(mk.dram["gWg"][0:1024, 512:1024].rearrange("(k p) c -> p k c", p=128), "gfg1", 0)
    gw_free[0], g_ev1 = proj4(fg1, [mk.aqk[4 + m] for m in range(4)], "gg1", 0, extra_dep=last_users)
    g_ev = g_ev0 + g_ev1

    o_ld = {}
    for n in range(NT4):
        for j in range(8):
            o_ld[(n, j)] = b.dma("Y", (lambda sy, n=n, j=j:
                                       sy.dma_start(mk.gv[j][:, 512 * n:512 * (n + 1)],
                                                    mk.dram["gla_o"][128 * j:128 * (j + 1),
                                                                     512 * n:512 * (n + 1)])),
                                 deps=(o_dma + [t_tr]) if (n == 0 and j == 0) else [],
                                 ch="gi0", tag=f"gold{n}_{j}")
    on_tok = {}
    bar = mk.scr_bar()
    st["scr_p1d"] = {k: list(bar) for k in ("osq", "gsil", "gnt", "outA", "outB")}
    rq_prev = list(bar)
    sil_toks = []
    for n in range(NT4):
        for h in range(GH):
            bst = (n * GH + h) % 2
            brq = 2 + (n * GH + h) % 2
            sq_toks = []
            t_pr = None
            for half in range(2):
                j = 2 * h + half
                tq = b.op("S", (lambda s, n=n, j=j: s.activation(
                    mk.osq[:, 0:512], mk.gv[j][:, 512 * n:512 * (n + 1)], AF.Square)),
                    deps=[o_ld[(n, j)]] + st["scr_p1d"]["osq"] + ([t_pr] if t_pr else []),
                    tag=f"gnsq{n}_{h}_{half}")

                def mmst(t, half=half, bst=bst):
                    if half == 0:
                        t.ldweights(mk.onesb[:, 0:1])
                    return t.matmul(mk.psum[bst][0:1, 0:512], mk.onesb[:, 0:1],
                                    mk.osq[:, 0:512], start=(half == 0), stop=(half == 1))
                t_pr = b.op("T", mmst, deps=[tq] + (psf(mk, bst) if half == 0 else []) + mk.tok["consts"],
                            tag=f"gnst{n}_{h}_{half}")
                st["scr_p1d"]["osq"] = [t_pr]
                sq_toks.append(t_pr)
            t_lg = b.op("S", (lambda s, bst=bst: s.activation(
                mk.psum[bst][0:1, 0:512], mk.psum[bst][0:1, 0:512], AF.Ln,
                bias=mk.cpool[0:1, EPS_COL:EPS_COL + 1], scale=1.0 / GDV)),
                deps=[t_pr] + mk.tok["cpool"], tag=f"gnlg{n}_{h}")
            t_rq = b.op("S", (lambda s, bst=bst: s.activation(
                mk.rqg[0:1, 0:512], mk.psum[bst][0:1, 0:512], AF.Exp, scale=-0.5)),
                deps=[t_lg] + rq_prev, tag=f"gnrq{n}_{h}")
            pset(mk, bst, [t_rq])

            def mmbc(t, brq=brq):
                t.ldweights(mk.onesr[0:1, 0:128])
                return t.matmul(mk.psum[brq][0:128, 0:512], mk.onesr[0:1, 0:128],
                                mk.rqg[0:1, 0:512], start=True, stop=True)
            t_bc = b.op("T", mmbc, deps=[t_rq] + psf(mk, brq), tag=f"gnbc{n}_{h}")
            rq_prev = [t_bc]
            for half in range(2):
                j = 2 * h + half
                t_si = b.op("S", (lambda s, n=n, j=j: s.activation(
                    mk.gsil[:, 0:512], mk.aqk[j][:, 512 * n:512 * (n + 1)], AF.Silu)),
                    deps=[g_ev[4 * j + n]] + st["scr_p1d"]["gsil"], tag=f"gnsi{n}_{j}")
                sil_toks.append(t_si)
                t_v1 = b.op("V", (lambda v, n=n, j=j, half=half, brq=brq: v.scalar_tensor_tensor(
                    mk.gnt[:, 0:512], mk.gv[j][:, 512 * n:512 * (n + 1)],
                    mk.gnwv[:, half:half + 1], mk.psum[brq][0:128, 0:512],
                    ALU.mult, ALU.mult)),
                    deps=[t_bc, o_ld[(n, j)]] + par_dep + st["scr_p1d"]["gnt"], tag=f"gnv1{n}_{j}")
                t_v2 = b.op("V", (lambda v, n=n, j=j: v.tensor_mul(
                    mk.gv[j][:, 512 * n:512 * (n + 1)], mk.gnt[:, 0:512], mk.gsil[:, 0:512])),
                    deps=[t_v1, t_si], tag=f"gnv2{n}_{j}")
                st["scr_p1d"]["gsil"] = [t_v2]
                st["scr_p1d"]["gnt"] = [t_v2]
                on_tok[(n, j)] = t_v2
            pset(mk, brq, [t_v1])

    out_dma = []
    fwo0 = fill(mk.dram["gWo"][0:1024, 0:512].rearrange("(k p) c -> p k c", p=128), "gfo0", 0)
    fwo1 = None
    outst_free = {0: [], 1: []}
    for half_m in range(2):
        ftok = fwo0 if half_m == 0 else fwo1
        for m in range(4):
            gm = 4 * half_m + m
            banks = [(m % 2) * 4 + n for n in range(4)]

            def mmwo(t, m=m, banks=banks):
                r = None
                for k in range(8):
                    t.ldweights(mk.wst[0][k][:, 128 * m:128 * (m + 1)])
                    for n in range(4):
                        r = t.matmul(mk.psum[banks[n]][0:128, 0:512],
                                     mk.wst[0][k][:, 128 * m:128 * (m + 1)],
                                     mk.gv[k][:, 512 * n:512 * (n + 1)],
                                     start=(k == 0), stop=(k == 7))
                return r

            deps = [ftok] + [on_tok[(n, j)] for n in range(NT4) for j in range(8)]
            for bk in banks:
                deps += psf(mk, bk)
            tk = b.op("T", mmwo, deps=deps, tag=f"gwo{gm}")
            gw_free[0] = [tk]
            for n in range(4):
                slot = n % 2
                stg = mk.outA if slot == 0 else mk.outB
                e = b.op("S", (lambda s, stg=stg, bk=banks[n]:
                               s.copy(stg[:, 0:512], mk.psum[bk][0:128, 0:512])),
                         deps=[tk] + outst_free[slot], tag=f"gwoe{gm}_{n}")
                pset(mk, banks[n], [e])
                dm = b.dma("Y", (lambda sy, stg=stg, gm=gm, n=n:
                                 sy.dma_start(mk.dram["gla_out"][128 * gm:128 * (gm + 1),
                                                                 512 * n:512 * (n + 1)],
                                              stg[:, 0:512])),
                           deps=[e], ch=f"gu{slot}", tag=f"gud{gm}_{n}")
                outst_free[slot] = [dm]
                out_dma.append(dm)
        if half_m == 0:
            fwo1 = fill(mk.dram["gWo"][0:1024, 512:1024].rearrange("(k p) c -> p k c", p=128), "gfo1", 0)

    st["aqk_free"] = sil_toks
    st["vaug_free"] = [tk]
    st["tab_readers"] += [t_pr, t_lg, t_rq] + sq_toks
    st["wst_free"] = {0: list(gw_free[0]), 1: list(wst1_users)}
    st["scr_dma_extra"] = list(out_dma) + list(o_dma)
    return out_dma


_CACHE = {}


def _build():
    if "nc" not in _CACHE:
        _CACHE["nc"] = build_program(n_layers=6, do_gla=True)
    return _CACHE["nc"]


def kernel(L0_attn_x, L0_attn_Wq, L0_attn_Wk, L0_attn_Wv, L0_attn_Wg,
           L0_attn_Wgk1, L0_attn_Wgk2, L0_attn_bgk2, L0_attn_Wo, L0_attn_gnw,
           L0_ffn_Wqkv, L0_ffn_q_scale, L0_ffn_k_scale, L0_ffn_Wo,
           L0_ffn_ln_g, L0_ffn_ln_b, L0_ffn_W1, L0_ffn_b1, L0_ffn_W2,
           L0_ffn_b2):
    from concourse.bass_utils import run_bass_kernel_spmd
    nc, m = _build()
    inp = {
        "L0_ffn_ln_g": L0_ffn_ln_g, "L0_ffn_ln_b": L0_ffn_ln_b,
        "L0_ffn_b2": L0_ffn_b2, "L0_ffn_b1": L0_ffn_b1,
        "L0_ffn_q_scale": L0_ffn_q_scale, "L0_ffn_k_scale": L0_ffn_k_scale,
        "L0_attn_gnw": L0_attn_gnw, "L0_attn_bgk2": L0_attn_bgk2,
    }
    params = pack_params(6, True, inp)
    base = {
        "params": params,
        "Wqkv": np.ascontiguousarray(L0_ffn_Wqkv, dtype=np.float32),
        "Wo": np.ascontiguousarray(L0_ffn_Wo, dtype=np.float32),
        "W1": np.ascontiguousarray(L0_ffn_W1, dtype=np.float32),
        "W2": np.ascontiguousarray(L0_ffn_W2, dtype=np.float32),
        "gWq": np.ascontiguousarray(L0_attn_Wq, dtype=np.float32),
        "gWk": np.ascontiguousarray(L0_attn_Wk, dtype=np.float32),
        "gWv": np.ascontiguousarray(L0_attn_Wv, dtype=np.float32),
        "gWg": np.ascontiguousarray(L0_attn_Wg, dtype=np.float32),
        "gWgk1": np.ascontiguousarray(L0_attn_Wgk1, dtype=np.float32),
        "gWgk2": np.ascontiguousarray(L0_attn_Wgk2, dtype=np.float32),
        "gWo": np.ascontiguousarray(L0_attn_Wo, dtype=np.float32),
    }
    xs = np.ascontiguousarray(L0_attn_x, dtype=np.float32)
    in_maps = []
    for c in range(8):
        im = dict(base)
        im["x"] = xs[c]
        in_maps.append(im)
    res = run_bass_kernel_spmd(nc, in_maps, core_ids=list(range(8)))
    out = np.stack([np.asarray(r["out"], np.float32) for r in res.results], 0)
    return out
